# version 11
# baseline (speedup 1.0000x reference)
import numpy as np
import ml_dtypes
import concourse.bass as bass
import concourse.mybir as mybir
from concourse.bass_utils import run_bass_kernel_spmd
from concourse.alu_op_type import AluOpType as OP

F32 = mybir.dt.float32
BF16 = mybir.dt.bfloat16
AF = mybir.ActivationFunctionType
AX = mybir.AxisListType
BFD = ml_dtypes.bfloat16

HID = 2048
H = 16
T_OWN = 2048
HALO = 512
T_IN = T_OWN + HALO
NCI = T_IN // 128
NCC = 19
NCO = 16
FFH = 5632
SCALE = 128.0 ** -0.5
NELEM_W = float(HID * FFH)


class Sch:
    ENGS = ("sync", "vector", "scalar", "tensor", "gpsimd")

    def __init__(self):
        self.ops = {e: [] for e in self.ENGS}
        self.cnt = {}
        self.seen = {e: {} for e in self.ENGS}

    def sem(self, name):
        self.cnt.setdefault(name, 0)
        return name

    DMA_SEMS = tuple("LD%d" % i for i in range(8))

    def op(self, eng, fn, waits=(), inc=None, amt=1):
        w = []
        for t in waits:
            if t is None:
                continue
            s, vv = t
            if s is None or vv is None or vv <= 0:
                continue
            if self.seen[eng].get(s, -1) >= vv:
                continue
            self.seen[eng][s] = vv
            w.append((s, vv))
        tgt = None
        if inc is not None:
            self.cnt[inc] += amt
            tgt = self.cnt[inc]
        self.ops[eng].append((w, fn, inc, amt))
        return (inc, tgt) if inc is not None else None

    def now(self, sem):
        return (sem, self.cnt.get(sem, 0))

    def emit(self, eng_name, eng, sems):
        for w, fn, inc, amt in self.ops[eng_name]:
            for s, vv in w:
                eng.wait_ge(sems[s], vv)
            ins = fn(eng)
            if inc is not None:
                if ins is None:
                    ins = eng.nop()
                ins.then_inc(sems[inc], amt)


def build_program(debug=False):
    nc = bass.Bass("TRN2", target_bir_lowering=False)

    def din(name, shape, dt=F32):
        return nc.dram_tensor(name, list(shape), dt, kind="ExternalInput").ap()

    x_in = din("x_in", (T_IN, HID))
    WQ = din("Wq", (HID, HID)); WK = din("Wk", (HID, HID))
    WV = din("Wv", (HID, HID)); WW = din("Ww", (HID, HID))
    WOp = din("Wo", (HID, HID))
    WF1 = din("Wf1", (HID, 128)); WF2 = din("Wf2", (128, HID))
    WB = din("Wb", (HID, H))
    WG1 = din("Wg1", (HID, 128)); WG2 = din("Wg2", (128, HID))
    CQ = din("cq", (HID, 4)); CK = din("ck", (HID, 4)); CV = din("cv", (HID, 4))
    W1T = din("w1T", (HID, FFH)); W2T = din("w2T", (HID, FFH))
    W3T = din("w3T", (FFH, HID))
    BG2R = din("bg2r", (128, HID))
    NWC = din("nw_col", (128, 1))
    CB = din("cb", (128, 3 * 128), BF16)
    CF = din("cf", (128, 8))
    CROW = din("crow", (1, 128))

    y = nc.dram_tensor("y", [T_OWN, HID], F32, kind="ExternalOutput").ap()

    def dint(name, shape, dt):
        return nc.dram_tensor(name, list(shape), dt, kind="Internal").ap()

    def dmaybe(name, shape, dt):
        kind = "ExternalOutput" if debug else "Internal"
        return nc.dram_tensor(name, list(shape), dt, kind=kind).ap()

    q2w = [dmaybe("q2w1", (HID, FFH), BF16), dmaybe("q2w2", (HID, FFH), BF16),
           dmaybe("q2w3", (FFH, HID), BF16)]
    hidB = dmaybe("hidB", (FFH, T_OWN), BF16)
    oSp = dmaybe("oSp", (H * 128, NCO * 128), BF16)

    DBG = {}

    def dbg_tap(name, ap, waits):
        if not debug:
            return
        if not isinstance(ap, bass.AP):
            ap = ap[:]
        t = nc.dram_tensor("dbg_" + name, [ap.shape[0], ap.shape[1]],
                           ap.dtype, kind="ExternalOutput").ap()
        DBG[name] = dma(t, ap, waits=waits)

    sbt = nc.alloc_sbuf_tensor

    cb_s = sbt("cb_s", [128, 3 * 128], BF16)
    IDENT = cb_s[:, 0:128]
    UTC = cb_s[:, 128:256]
    HALFM = cb_s[:, 256:384]
    cf_s = sbt("cf_s", [128, 8], F32)
    EPS_LN = cf_s[:, 0:1]; EPS_L2 = cf_s[:, 1:2]; ONE_F = cf_s[:, 2:3]
    crow_s = sbt("crow_s", [1, 128], F32)
    ones_bf = sbt("ones_bf", [128, 16], BF16)
    ONES_COL = ones_bf[:, 0:1]
    nw_s = sbt("nw_s", [128, 1], F32)
    bg2_s = sbt("bg2_s", [128, HID], BF16)
    wf2b = sbt("wf2b", [128, HID], BF16)
    wg2b = sbt("wg2b", [128, HID], BF16)
    cqall = sbt("cqall", [128, 3 * 64], F32)
    beta_s = sbt("beta_s", [128, NCI * H], F32)
    out1T = sbt("out1T", [128, T_IN], BF16)
    out2T = sbt("out2T", [128, T_OWN], BF16)
    lnst = sbt("lnst", [128, 8], F32)
    gcols = sbt("gcols", [128, 32], F32)
    dcols = sbt("dcols", [128, 32], F32)
    ercols = sbt("ercols", [128, 32], F32)
    sscol = sbt("sscol", [128, 48], F32)
    brcol = sbt("brcol", [128, 32], F32)
    brS = sbt("brS", [128, 32], F32)
    ncol = sbt("ncol", [128, 8], F32)
    rstd16 = sbt("rstd16", [128, 16], F32)
    gsum = sbt("gsum", [128, 8], F32)
    gcolq = sbt("gcolq", [128, 4], F32)
    gtmp = sbt("gtmp", [1, 8], F32)
    SkT = sbt("SkT", [128, 128], BF16)
    Sv = sbt("Sv", [128, 128], BF16)
    SkE = sbt("SkE", [128, 128], BF16)
    SkEd = sbt("SkEd", [128, 128], BF16)
    QKm = sbt("QKm", [128, 128], BF16)
    stil = sbt("stil", [128, 128], BF16)
    a_e = sbt("a_e", [128, 128], BF16)
    aTt = sbt("aTt", [128, 128], BF16)
    aHt = sbt("aHt", [128, 128], BF16)
    PTm = sbt("PTm", [128, 128], BF16)

    AR1 = sbt("AR1", [128, 43008], BF16)
    AR2 = sbt("AR2", [128, 47232], BF16)

    def carve(ar, off_bytes, cols, dt):
        assert off_bytes % 4 == 0
        c2 = off_bytes // 2
        if dt == BF16:
            assert (off_bytes + 2 * cols) <= 2 * ar.shape[1]
            return ar[:, c2:c2 + cols]
        assert (off_bytes + 4 * cols) <= 2 * ar.shape[1]
        return ar[:, c2:c2 + 2 * cols].bitcast(F32)

    XNT = carve(AR1, 0, 16 * T_IN, BF16)
    onT = carve(AR1, 0, 16 * T_OWN, BF16)
    wocb = carve(AR1, 65536, 16 * 512, BF16)
    wstage = carve(AR1, 81920, 512, F32)
    outrow = carve(AR1, 83968, 512, F32)

    xrow = carve(AR2, 0, HID, F32)
    sqrow = carve(AR2, 8192, HID, F32)
    xnrow = carve(AR2, 16384, HID, BF16)
    _off = [0]

    def hc(cols, dt):
        ap = carve(AR2, _off[0], cols, dt)
        _off[0] += cols * (2 if dt == BF16 else 4)
        _off[0] = (_off[0] + 63) & ~63
        return ap

    wcol = hc(HID, F32)
    wcolb = hc(HID, BF16)
    preconv = hc(T_IN + 4, F32)
    E1 = carve(AR2, 0, NCC * 128, BF16)
    E2 = carve(AR2, 4864, NCC * 128, BF16)
    ERb = carve(AR2, 9728, NCC * 128, BF16)
    expRr = carve(AR2, 14592, NCC * 128, BF16)
    projbuf = hc(T_IN, BF16)
    kT_s = hc(T_IN, BF16)
    qT_s = hc(T_IN, BF16)
    K_tm = hc(NCC * 128, BF16)
    V_tm = hc(NCC * 128, BF16)
    Vh_tm = hc(NCC * 128, BF16)
    W_tm = hc(NCC * 128, BF16)
    g_sm = hc(NCC * 128, BF16)
    wt_sm = hc(NCC * 128, BF16)
    wh_sm = hc(NCC * 128, BF16)
    grow = hc(512, F32)
    zrow = hc(512, F32)
    z2row = hc(512, F32)
    obuf = hc(NCO * 128, BF16)
    assert _off[0] <= 94464, _off[0]
    fslab = carve(AR2, 0, FFH, F32)
    fsl_a = carve(AR2, 22528, FFH, BF16)
    fsl_b = carve(AR2, 33792, FFH, BF16)
    q2a = carve(AR2, 0, HID, BF16)
    q2b = carve(AR2, 4096, HID, BF16)
    sigx = carve(AR2, 8192, T_OWN, BF16)
    hslab = carve(AR2, 12288, T_OWN, BF16)
    onrow = carve(AR2, 0, HID, BF16)
    gate_f = carve(AR2, 4096, HID, F32)
    gatesig = carve(AR2, 12288, HID, BF16)
    sqn = carve(AR2, 16384, HID, BF16)
    hidq = carve(AR2, 0, 44 * 512, BF16)
    w3cb = carve(AR2, 45056, 44 * 512, BF16)
    xstage = carve(AR2, 90112, 512, F32)

    PSF = [nc.alloc_psum_tensor("PS%d" % i, [128, 512], F32) for i in range(6)]
    PB6 = nc.alloc_psum_tensor("PB6", [128, 1024], BF16)
    PB7 = nc.alloc_psum_tensor("PB7", [128, 1024], BF16)
    TRA = PB6[:, 0:128]
    TRB = PB7[:, 0:128]
    TRC = PB7[:, 512:640]

    S = Sch()
    LDS = [S.sem("LD%d" % i) for i in range(8)]
    SV = S.sem("SV"); SS = S.sem("SS"); ST = S.sem("ST")
    _rr = [0]

    def dma(out_, in_, waits=(), transpose=False):
        if transpose:
            waits = list(waits) + [S.now(n) for n in LDS]
        sem = LDS[_rr[0] % 8]
        _rr[0] += 1
        if S.cnt[sem] > 0:
            S.op("sync", lambda e: None, waits=[S.now(sem)])

        def f(e, _o=out_, _i=in_, _t=transpose):
            return e.dma_start(out=_o, in_=_i, transpose=_t)
        tok = S.op("sync", f, waits=waits, inc=sem, amt=16)
        if transpose:
            for n in LDS:
                S.seen["sync"].pop(n, None)
            S.op("sync", lambda e: None, waits=[S.now(sem)])
        return tok

    def v(fn, waits=()):
        return S.op("vector", fn, waits=waits, inc=SV)

    def sc(fn, waits=()):
        return S.op("scalar", fn, waits=waits, inc=SS)

    def tn(fn, waits=()):
        return S.op("tensor", fn, waits=waits, inc=ST)

    def barrier_toks():
        return [S.now(SV), S.now(SS), S.now(ST)] + [S.now(n) for n in LDS]

    def mm_acc(ps_ap, pairs, waits_first):
        tok = None
        n = len(pairs)
        for k, (lt, rh) in enumerate(pairs):
            tok = tn(lambda e, _l=lt, _r=rh, _s=(k == 0), _e=(k == n - 1),
                     _p=ps_ap: e.matmul(_p, _l, _r, start=_s, stop=_e),
                     waits=waits_first if k == 0 else ())
        return tok

    dma(cb_s[:], CB)
    dma(cf_s[:], CF)
    dma(crow_s[:], CROW)
    dma(nw_s[:], NWC)
    tl = dma(gate_f[:], BG2R)
    v(lambda e: e.memset(ones_bf[:], 1.0))
    v(lambda e: e.memset(PSF[5][64:128, 0:64], 0.0))
    bgtok = v(lambda e: e.tensor_copy(bg2_s[:], gate_f[:]), waits=[tl])
    dma(cqall.rearrange("p (s k w) -> p s k w", s=3, w=4)[:, 0],
        CQ.rearrange("(k p) w -> p k w", p=128))
    dma(cqall.rearrange("p (s k w) -> p s k w", s=3, w=4)[:, 1],
        CK.rearrange("(k p) w -> p k w", p=128))
    dma(cqall.rearrange("p (s k w) -> p s k w", s=3, w=4)[:, 2],
        CV.rearrange("(k p) w -> p k w", p=128))
    tl = dma(wcol[:], WF2, waits=[bgtok])
    wf2tok = v(lambda e: e.tensor_copy(wf2b[:], wcol[:]), waits=[tl])
    tl = dma(xrow[:], WG2, waits=[wf2tok])
    wg2tok = v(lambda e: e.tensor_copy(wg2b[:], xrow[:]), waits=[tl])

    prev_used = wg2tok
    ttok = None
    for i in range(NCI):
        tl = dma(xrow[:], x_in[i * 128:(i + 1) * 128, :], waits=[prev_used])
        t1 = v(lambda e: e.reduce_sum(lnst[:, 0:1], xrow[:], AX.X), waits=[tl])
        t2 = sc(lambda e: e.activation(sqrow[:], xrow[:], AF.Square,
                                       accum_out=lnst[:, 1:2]), waits=[tl])
        t3 = v(lambda e: e.tensor_scalar(lnst[:, 2:3], lnst[:, 0:1], 1.0 / HID,
                                         None, OP.mult), waits=[t1])
        t4 = v(lambda e: e.tensor_scalar(lnst[:, 3:4], lnst[:, 1:2], 1.0 / HID,
                                         None, OP.mult), waits=[t2])
        t5 = v(lambda e: e.tensor_mul(lnst[:, 4:5], lnst[:, 2:3], lnst[:, 2:3]),
               waits=[t3])
        t6 = v(lambda e: e.tensor_sub(lnst[:, 5:6], lnst[:, 3:4], lnst[:, 4:5]),
               waits=[t4, t5])
        t7 = sc(lambda e: e.activation(lnst[:, 6:7], lnst[:, 5:6], AF.Ln,
                                       bias=EPS_LN), waits=[t6])
        t8 = sc(lambda e: e.activation(lnst[:, 7:8], lnst[:, 6:7], AF.Exp,
                                       scale=-0.5), waits=[t7])
        t9 = v(lambda e: e.tensor_scalar(xnrow[:], xrow[:], lnst[:, 2:3],
                                         lnst[:, 7:8], OP.subtract, OP.mult),
               waits=[t8, t3, ttok])
        ttok = dma(XNT.rearrange("p (k t) -> p k t", k=16)
                   [:, :, i * 128:(i + 1) * 128],
                   xnrow[:], waits=[t9], transpose=True)
        prev_used = t9
    XN_RDY = ttok
    dbg_tap("XNT", XNT, [XN_RDY])

    tl = dma(wcol.rearrange("p (k c) -> p k c", k=16),
             WF1.rearrange("(k p) c -> p k c", p=128), waits=[XN_RDY])
    cw = v(lambda e: e.tensor_copy(wcolb[:], wcol[:]), waits=[tl, wf2tok])
    evtoks = {}
    for tb in range(5):
        ps = PSF[tb % 2][:, 0:512]
        pairs = [(wcolb[:, k * 128:(k + 1) * 128],
                  XNT[:, k * T_IN + tb * 512:k * T_IN + tb * 512 + 512])
                 for k in range(16)]
        mt = mm_acc(ps, pairs, [cw, XN_RDY, evtoks.get(tb - 2)])
        evtoks[tb] = v(lambda e, _tb=tb, _ps=ps: e.tensor_copy(
            out1T[:, _tb * 512:(_tb + 1) * 512], _ps), waits=[mt])
    o1tok = S.now(SV)
    tl = dma(wcol.rearrange("p (k c) -> p k c", k=16),
             WG1.rearrange("(k p) c -> p k c", p=128), waits=[o1tok])
    cw = v(lambda e: e.tensor_copy(wcolb[:], wcol[:]), waits=[tl, o1tok])
    evtoks = {}
    for tb in range(4):
        ps = PSF[2 + tb % 2][:, 0:512]
        off = 512 + tb * 512
        pairs = [(wcolb[:, k * 128:(k + 1) * 128],
                  XNT[:, k * T_IN + off:k * T_IN + off + 512]) for k in range(16)]
        mt = mm_acc(ps, pairs, [cw, evtoks.get(tb - 2)])
        evtoks[tb] = v(lambda e, _tb=tb, _ps=ps: e.tensor_copy(
            out2T[:, _tb * 512:(_tb + 1) * 512], _ps), waits=[mt])
    o2tok = S.now(SV)
    tl = dma(wcol.rearrange("p (k c) -> p k c", c=16)[:, 0:16, :],
             WB.rearrange("(k p) c -> p k c", p=128), waits=[o2tok])
    cw = v(lambda e: e.tensor_copy(wcolb[:, 0:256], wcol[:, 0:256]),
           waits=[tl, o2tok])
    bprev = None
    for i in range(NCI):
        ps = PSF[4][:, 0:16]
        pairs = [(XNT[:, k * T_IN + i * 128:k * T_IN + i * 128 + 128],
                  wcolb[:, k * 16:(k + 1) * 16]) for k in range(16)]
        mt = mm_acc(ps, pairs, [cw, bprev])
        bprev = sc(lambda e, _i=i, _ps=ps: e.activation(
            beta_s[:, _i * 16:(_i + 1) * 16], _ps, AF.Sigmoid), waits=[mt])
    beta_tok = bprev
    dbg_tap("out1T", out1T, [o1tok])
    dbg_tap("out2T", out2T, [o2tok])
    dbg_tap("beta", beta_s, [beta_tok])

    beta3 = beta_s.rearrange("p (c h) -> p c h", h=H)
    zpad = v(lambda e: e.memset(preconv[:, 0:4], 0.0), waits=[beta_tok])

    def proj_conv(h, Wmat, cvoff, dest, extra_w):
        tl = dma(wcol.rearrange("p (k c) -> p k c", k=16),
                 Wmat.rearrange("(k p) c -> p k c", p=128)
                 [:, :, h * 128:(h + 1) * 128],
                 waits=[S.now(SV)] + list(extra_w))
        cw = v(lambda e: e.tensor_copy(wcolb[:], wcol[:]),
               waits=[tl, S.now(ST)])
        evs = {}
        for tb in range(5):
            ps = PSF[tb % 2][:, 0:512]
            pairs = [(wcolb[:, k * 128:(k + 1) * 128],
                      XNT[:, k * T_IN + tb * 512:k * T_IN + tb * 512 + 512])
                     for k in range(16)]
            mt = mm_acc(ps, pairs, [cw, evs.get(tb - 2), S.now(SV)])
            evs[tb] = v(lambda e, _tb=tb, _ps=ps: e.tensor_copy(
                preconv[:, 4 + _tb * 512:4 + (_tb + 1) * 512], _ps), waits=[mt])
        base = cvoff * 64 + h * 4
        cc = [cqall[:, base + i:base + i + 1] for i in range(4)]
        t = v(lambda e: e.tensor_scalar(dest[:], preconv[:, 1:1 + T_IN], cc[0],
                                        None, OP.mult),
              waits=[evs[4], zpad, S.now(ST)] + list(extra_w))
        for i in (1, 2, 3):
            t = v(lambda e, _i=i: e.scalar_tensor_tensor(
                dest[:], preconv[:, 1 + _i:1 + _i + T_IN], cc[_i], dest[:],
                OP.mult, OP.add), waits=[t])
        t = sc(lambda e: e.activation(dest[:], dest[:], AF.Silu), waits=[t])
        return t

    GRPS = [(0, 4), (4, 4), (8, 4), (12, 4), (16, 3)]
    spill_prev = None
    for h in range(H):
        hb = barrier_toks()
        t = proj_conv(h, WW, 2, projbuf, hb)
        trW = dma(W_tm.rearrange("p (c t) -> p c t", c=NCC),
                  projbuf[:, 128:T_IN], waits=[t, S.now(ST)], transpose=True)
        t = proj_conv(h, WV, 2, projbuf, [trW])
        trV = dma(V_tm.rearrange("p (c t) -> p c t", c=NCC),
                  projbuf[:, 128:T_IN], waits=[t, S.now(ST)], transpose=True)
        t = proj_conv(h, WK, 1, kT_s, [])
        trK = dma(K_tm.rearrange("p (c t) -> p c t", c=NCC),
                  kT_s[:, 128:T_IN], waits=[t, S.now(ST)], transpose=True)
        qtok = proj_conv(h, WQ, 0, qT_s, [])
        gevs = []
        ge_prev = None
        for tb in range(5):
            ps = PSF[2 + tb % 2][:, 0:512]
            mt = mm_acc(ps, [(wf2b[:, h * 128:(h + 1) * 128],
                              out1T[:, tb * 512:(tb + 1) * 512])],
                        [wf2tok, S.now(SV), S.now(SS)])
            e1 = sc(lambda e, _ps=ps: e.activation(zrow[:], _ps, AF.Abs),
                    waits=[mt, ge_prev])
            e2 = sc(lambda e: e.activation(grow[:], zrow[:], AF.Exp, scale=-1.0),
                    waits=[e1])
            e3 = sc(lambda e: e.activation(grow[:], grow[:], AF.Ln, bias=ONE_F),
                    waits=[e2])
            e4 = v(lambda e, _tb=tb, _ps=ps: e.scalar_tensor_tensor(
                projbuf[:, _tb * 512:(_tb + 1) * 512], _ps, 0.0, grow[:],
                OP.min, OP.subtract), waits=[e3, trV, trK])
            ge_prev = e4
            gevs.append(e4)
        trG = dma(g_sm.rearrange("p (c t) -> p c t", c=NCC),
                  projbuf[:, 128:T_IN], waits=gevs + [S.now(ST)], transpose=True)

        dtoks = []
        cp_prev = zz_prev = se1_prev = se3_prev = se4_prev = None
        for g0, gn in GRPS:
            gw = gn * 128
            mg = None
            for j in range(gn):
                mg = tn(lambda e, _j=j, _c=g0 + j: e.matmul(
                    PSF[0][:, _j * 128:(_j + 1) * 128], UTC,
                    g_sm[:, _c * 128:(_c + 1) * 128], start=True, stop=True),
                    waits=[trG, cp_prev, se1_prev, S.now(SV)] if j == 0 else ())
            mr = None
            for j in range(gn):
                mr = tn(lambda e, _j=j, _c=g0 + j: e.matmul(
                    PSF[1][:, _j * 128:(_j + 1) * 128], HALFM,
                    g_sm[:, _c * 128:(_c + 1) * 128], start=True, stop=True),
                    waits=[zz_prev, se4_prev] if j == 0 else ())
            cp = v(lambda e, _w=gw: e.tensor_copy(grow[:, 0:_w], PSF[0][:, 0:_w]),
                   waits=[mg, S.now(SS)])
            se1 = sc(lambda e, _w=gw, _o=g0 * 128: e.activation(
                E1[:, _o:_o + _w], PSF[0][:, 0:_w], AF.Exp),
                waits=[mg, S.now(SV)])
            zz = v(lambda e, _w=gw: e.tensor_sub(zrow[:, 0:_w], PSF[1][:, 0:_w],
                                                 grow[:, 0:_w]),
                   waits=[cp, mr, se3_prev])
            zh = v(lambda e, _w=gw: e.tensor_add(z2row[:, 0:_w], zrow[:, 0:_w],
                                                 PSF[1][:, 0:_w]),
                   waits=[zz, se4_prev])
            zc = v(lambda e, _w=gw: e.tensor_scalar(zrow[:, 0:_w], zrow[:, 0:_w],
                                                    80.0, -80.0, OP.min, OP.max),
                   waits=[zh])
            se2 = sc(lambda e, _w=gw, _o=g0 * 128: e.activation(
                E2[:, _o:_o + _w], zrow[:, 0:_w], AF.Exp, scale=-1.0), waits=[zc])
            se3 = sc(lambda e, _w=gw, _o=g0 * 128: e.activation(
                ERb[:, _o:_o + _w], zrow[:, 0:_w], AF.Exp))
            se4 = sc(lambda e, _w=gw, _o=g0 * 128: e.activation(
                expRr[:, _o:_o + _w], z2row[:, 0:_w], AF.Exp), waits=[zh])
            cp_prev, zz_prev, se1_prev, se3_prev, se4_prev = cp, zc, se1, se3, se4
            dtoks += [se1, se2, se3, se4]
        mgc = None
        for c in range(NCC):
            mgc = tn(lambda e, _c=c: e.matmul(PSF[2][:, _c:_c + 1],
                                              g_sm[:, _c * 128:(_c + 1) * 128],
                                              ONES_COL, start=True, stop=True),
                     waits=[trG, S.now(SV), S.now(SS)] if c == 0 else ())
        cgc = v(lambda e: e.tensor_copy(gcols[:, 0:NCC], PSF[2][:, 0:NCC]),
                waits=[mgc, S.now(SS)])
        sdc = sc(lambda e: e.activation(dcols[:, 0:NCC], gcols[:, 0:NCC], AF.Exp),
                 waits=[cgc, S.now(SV)])
        sq1 = v(lambda e: e.tensor_mul(wt_sm[:], W_tm[:], W_tm[:]),
                waits=[trW, S.now(ST)])
        l2b = v(lambda e: e.reduce_sum(
            sscol[:, 0:NCC], wt_sm.rearrange("p (c m) -> p c m", c=NCC), AX.X),
            waits=[sq1, S.now(SS)])
        l2c = sc(lambda e: e.activation(sscol[:, 20:20 + NCC], sscol[:, 0:NCC],
                                        AF.Ln, bias=EPS_L2), waits=[l2b])
        l2d = sc(lambda e: e.activation(sscol[:, 20:20 + NCC],
                                        sscol[:, 20:20 + NCC], AF.Exp,
                                        scale=-0.5), waits=[l2c])
        brt = v(lambda e, _h=h: e.tensor_mul(brcol[:, 0:NCC],
                                             sscol[:, 20:20 + NCC],
                                             beta3[:, 1:1 + NCC, _h]),
                waits=[l2d, beta_tok, S.now(ST)])
        brst = v(lambda e: e.tensor_scalar(brS[:, 0:NCC], brcol[:, 0:NCC], SCALE,
                                           None, OP.mult), waits=[brt])
        wtk = v(lambda e: e.tensor_mul(wt_sm[:], W_tm[:], ERb[:]),
                waits=[l2b] + dtoks)
        whk = v(lambda e: e.tensor_mul(wh_sm[:], W_tm[:], expRr[:]),
                waits=[wtk])
        br3 = brcol[:, 0:NCC].unsqueeze(2).broadcast_to([128, NCC, 128])
        khk = v(lambda e: e.tensor_tensor(
            K_tm.rearrange("p (c t) -> p c t", c=NCC),
            K_tm.rearrange("p (c t) -> p c t", c=NCC), br3, OP.mult),
            waits=[trK, brt])
        vhk = v(lambda e: e.tensor_tensor(
            Vh_tm.rearrange("p (c t) -> p c t", c=NCC),
            V_tm.rearrange("p (c t) -> p c t", c=NCC), br3, OP.mult),
            waits=[trV, brt])
        if h == 0:
            dbg_tap("kT", kT_s, [S.now(SS)])
            dbg_tap("qT", qT_s, [qtok])
            dbg_tap("W_tm", W_tm, [trW])
            dbg_tap("V_tm", V_tm, [trV])
            dbg_tap("g_sm", g_sm, [trG])
            dbg_tap("E1", E1, dtoks)
            dbg_tap("E2", E2, dtoks)
            dbg_tap("ERb", ERb, dtoks)
            dbg_tap("expRr", expRr, dtoks)
            dbg_tap("gcols", gcols, [cgc])
            dbg_tap("dcols", dcols, [sdc])
            dbg_tap("sscol", sscol, [l2d])
            dbg_tap("brcol", brcol, [brt])
            dbg_tap("wt_sm", wt_sm, [wtk])
            dbg_tap("wh_sm", wh_sm, [whk])
            dbg_tap("Khat", K_tm, [khk])
            dbg_tap("Vhat", Vh_tm, [vhk])
        trWT = dma(W_tm.rearrange("p (c t) -> p c t", c=NCC)[:, 0:NCO, :],
                   wt_sm[:, 3 * 128:NCC * 128],
                   waits=[wtk, khk, vhk, S.now(ST)], transpose=True)
        wtT = W_tm

        z1 = v(lambda e: e.memset(SkT[:], 0.0), waits=[S.now(ST)])
        z2 = v(lambda e: e.memset(Sv[:], 0.0))

        evQK = evSC = evPT = evO = evA = evB = evT1 = None
        prev_ae = None
        upK = None; upV = None
        for c in range(NCC):
            gc = c + 1
            oc = c - 3
            own = oc >= 0
            tcol = slice(gc * 128, (gc + 1) * 128)
            if own:
                ttr = tn(lambda e: e.transpose(TRA, SkT[:], IDENT),
                         waits=[upK, z1, evA])
                eve = v(lambda e: e.tensor_copy(SkEd[:], TRA), waits=[ttr])
                mqk = tn(lambda e, _t=tcol: e.matmul(
                    PSF[3][:, 0:128], kT_s[:, _t], qT_s[:, _t],
                    start=True, stop=True), waits=[evT1, qtok])
                evQK = v(lambda e, _c=c: e.scalar_tensor_tensor(
                    QKm[:], PSF[3][:, 0:128], brS[:, _c:_c + 1], UTC,
                    OP.mult, OP.mult), waits=[mqk, brst])
                ms1 = tn(lambda e, _c=c: e.matmul(
                    PSF[4][:, 0:128], QKm[:], wt_sm[:, _c * 128:(_c + 1) * 128],
                    start=True, stop=True), waits=[evQK, wtk, evSC])
                mt1 = tn(lambda e, _t=tcol: e.matmul(
                    PSF[3][:, 0:128], qT_s[:, _t], SkEd[:],
                    start=True, stop=True), waits=[eve, evQK])
                evT1 = v(lambda e, _c=c: e.scalar_tensor_tensor(
                    SkE[:], PSF[3][:, 0:128], SCALE,
                    E1[:, _c * 128:(_c + 1) * 128], OP.mult, OP.mult),
                    waits=[mt1, prev_ae])
                evS1 = v(lambda e, _c=c: e.tensor_mul(
                    stil[:], PSF[4][:, 0:128], E2[:, _c * 128:(_c + 1) * 128]),
                    waits=[ms1, evT1])
                evSC = v(lambda e: e.tensor_add(stil[:], stil[:], SkE[:]),
                         waits=[evS1])
                nm = v(lambda e: e.reduce_max(ncol[:, 0:1], stil[:], AX.X,
                                              negate=True), waits=[evSC, prev_ae])
                ae = sc(lambda e: e.activation(a_e[:], stil[:], AF.Exp,
                                               bias=ncol[:, 0:1]), waits=[nm])
                prev_ae = ae
                smt = v(lambda e: e.reduce_sum(ncol[:, 1:2], a_e[:], AX.X),
                        waits=[ae])
                ri = v(lambda e: e.reciprocal(ncol[:, 2:3], ncol[:, 1:2]),
                       waits=[smt])
                at = v(lambda e, _c=c: e.tensor_mul(
                    aTt[:], a_e[:], E2[:, _c * 128:(_c + 1) * 128]),
                    waits=[ae, evPT])
                ah = v(lambda e, _c=c: e.tensor_mul(
                    aHt[:], a_e[:], E1[:, _c * 128:(_c + 1) * 128]),
                    waits=[evO])
                ttb = tn(lambda e: e.transpose(TRB, aTt[:], IDENT),
                         waits=[at, evB])
                evA = v(lambda e: e.tensor_copy(aTt[:], TRB), waits=[ttb])
                ttc = tn(lambda e: e.transpose(TRC, aHt[:], IDENT),
                         waits=[ah, evA])
                evB = v(lambda e: e.tensor_copy(aHt[:], TRC), waits=[ttc])
                tn(lambda e, _o=oc: e.matmul(
                    PSF[5][0:64, 0:64], wtT[:, _o * 128:_o * 128 + 64],
                    aTt[:, 0:64], start=True, stop=True),
                    waits=[evA, evPT, trWT])
                tn(lambda e, _o=oc: e.matmul(
                    PSF[5][64:128, 64:128],
                    wtT[:, _o * 128 + 64:_o * 128 + 128],
                    aTt[:, 64:128], start=True, stop=True))
                mp3 = tn(lambda e, _o=oc: e.matmul(
                    PSF[5][0:64, 64:128], wtT[:, _o * 128:_o * 128 + 64],
                    aTt[:, 64:128], start=True, stop=True))
                evPT = v(lambda e, _c=c: e.scalar_tensor_tensor(
                    PTm[:], PSF[5][:, 0:128], brcol[:, _c:_c + 1], UTC,
                    OP.mult, OP.mult), waits=[mp3])
                mo1 = tn(lambda e, _c=c: e.matmul(
                    PSF[2][:, 0:128], PTm[:], V_tm[:, _c * 128:(_c + 1) * 128],
                    start=True, stop=False), waits=[evPT, evO])
                mo2 = tn(lambda e: e.matmul(PSF[2][:, 0:128], aHt[:], Sv[:],
                                            start=False, stop=True),
                         waits=[evB, upV, z2])
                evO = v(lambda e, _o=oc: e.tensor_scalar(
                    obuf[:, _o * 128:(_o + 1) * 128], PSF[2][:, 0:128],
                    ncol[:, 2:3], None, OP.mult),
                    waits=[mo2, ri, spill_prev])
            md0 = tn(lambda e, _c=c: e.matmul(
                PSF[0][:, 0:128], wh_sm[:, _c * 128:(_c + 1) * 128],
                K_tm[:, _c * 128:(_c + 1) * 128], start=True, stop=True),
                waits=[whk, khk, upK, S.now(SV) if c == 0 else None])
            md1 = tn(lambda e, _c=c: e.matmul(
                PSF[1][:, 0:128], wh_sm[:, _c * 128:(_c + 1) * 128],
                Vh_tm[:, _c * 128:(_c + 1) * 128], start=True, stop=True),
                waits=[vhk, upV, S.now(SV) if c == 0 else None])
            stw = [S.now(ST)] if own else []
            upK = v(lambda e, _c=c: e.scalar_tensor_tensor(
                SkT[:], SkT[:], dcols[:, _c:_c + 1], PSF[0][:, 0:128],
                OP.mult, OP.add), waits=[md0, sdc, z1] + stw)
            upV = v(lambda e, _c=c: e.scalar_tensor_tensor(
                Sv[:], Sv[:], dcols[:, _c:_c + 1], PSF[1][:, 0:128],
                OP.mult, OP.add), waits=[md1, z2] + stw)
        spill_prev = dma(oSp[h * 128:(h + 1) * 128, :], obuf[:], waits=[evO])
        if h == 0:
            dbg_tap("SkT_f", SkT, [upK])
            dbg_tap("Sv_f", Sv, [upV])
    ATTN_LD = [S.now(n) for n in LDS]

    srcs = [(W1T, 16, FFH), (W2T, 16, FFH), (W3T, 44, HID)]
    gb = barrier_toks()
    zg = v(lambda e: e.memset(gsum[:, 0:4], 0.0), waits=gb)
    gprev = zg
    for mi, (WSRC, nslab, wid) in enumerate(srcs):
        for i in range(nslab):
            tl = dma(fslab[:, 0:wid], WSRC[i * 128:(i + 1) * 128, :],
                     waits=[gprev] + (gb if (mi == 0 and i == 0) else []))
            ga = sc(lambda e, _w=wid: e.activation(
                fsl_a[:, 0:_w], fslab[:, 0:_w], AF.Abs,
                accum_out=gsum[:, 4:5]), waits=[tl, gprev])
            gprev = v(lambda e, _m=mi: e.tensor_add(
                gsum[:, _m:_m + 1], gsum[:, _m:_m + 1], gsum[:, 4:5]),
                waits=[ga])
    qprev = gprev
    qst0 = [None]
    for mi, (WSRC, nslab, wid) in enumerate(srcs):
        mt = tn(lambda e, _m=mi: e.matmul(
            PSF[2][0:1, _m:_m + 1], gsum[:, _m:_m + 1], ONE_F,
            start=True, stop=True), waits=[gprev, S.now(SV), S.now(SS)])
        cv_ = v(lambda e, _m=mi: e.tensor_copy(gtmp[0:1, _m:_m + 1],
                                               PSF[2][0:1, _m:_m + 1]),
                waits=[mt])
        mr = tn(lambda e, _m=mi: e.matmul(
            PSF[3][:, _m:_m + 1], crow_s[:], gtmp[0:1, _m:_m + 1],
            start=True, stop=True), waits=[cv_])
        tp = v(lambda e, _m=mi: e.tensor_scalar(
            gcolq[:, 0:1], PSF[3][:, _m:_m + 1], 0.5 / NELEM_W, 0.5e-8,
            OP.mult, OP.add), waits=[mr, qprev, S.now(SS)])
        tng = v(lambda e: e.tensor_scalar(gcolq[:, 1:2], gcolq[:, 0:1], -1.0,
                                          None, OP.mult), waits=[tp])
        qprev = tng
        qst = qst0[0]
        if mi == 0:
            dbg_tap("gcolq", gcolq, [tng])
            dbg_tap("gsum", gsum, [tng])
        for i in range(nslab):
            tl = dma(fslab[:, 0:wid], WSRC[i * 128:(i + 1) * 128, :],
                     waits=[qprev])
            s1 = sc(lambda e, _w=wid: e.activation(
                fsl_a[:, 0:_w], fslab[:, 0:_w], AF.Sign, bias=gcolq[:, 1:2]),
                waits=[tl, tng, qst])
            s2 = sc(lambda e, _w=wid: e.activation(
                fsl_b[:, 0:_w], fslab[:, 0:_w], AF.Sign, bias=gcolq[:, 0:1]),
                waits=[qst])
            qprev = v(lambda e, _w=wid: e.tensor_add(
                fsl_b[:, 0:_w], fsl_b[:, 0:_w], fsl_a[:, 0:_w]), waits=[s1, s2])
            qst = dma(q2w[mi][i * 128:(i + 1) * 128, :], fsl_b[:, 0:wid],
                      waits=[qprev])
            qst0[0] = qst
    QDONE = [S.now(n) for n in LDS]

    q2w1r = q2w[0].rearrange("(k p) c -> p k c", p=128)
    q2w2r = q2w[1].rearrange("(k p) c -> p k c", p=128)
    fb = barrier_toks()
    hprev = None
    hst = None
    for cs in range(44):
        tla = dma(q2a.rearrange("p (k c) -> p k c", k=16),
                  q2w1r[:, :, cs * 128:(cs + 1) * 128],
                  waits=QDONE + [hprev] + (fb if cs == 0 else []))
        tlb = dma(q2b.rearrange("p (k c) -> p k c", k=16),
                  q2w2r[:, :, cs * 128:(cs + 1) * 128])
        for tb in range(4):
            off = 512 + tb * 512
            p0 = PSF[0][:, 0:512]
            p1 = PSF[1][:, 0:512]
            pairs0 = [(q2a[:, k * 128:(k + 1) * 128],
                       XNT[:, k * T_IN + off:k * T_IN + off + 512])
                      for k in range(16)]
            pairs1 = [(q2b[:, k * 128:(k + 1) * 128],
                       XNT[:, k * T_IN + off:k * T_IN + off + 512])
                      for k in range(16)]
            m0 = mm_acc(p0, pairs0, [tla, hprev] + (fb if cs == 0 else []))
            m1 = mm_acc(p1, pairs1, [tlb])
            sg = sc(lambda e, _tb=tb, _p=p0: e.activation(
                sigx[:, _tb * 512:(_tb + 1) * 512], _p, AF.Sigmoid, scale=0.5),
                waits=[m0, hprev])
            hprev = v(lambda e, _tb=tb, _p=p1: e.scalar_tensor_tensor(
                hslab[:, _tb * 512:(_tb + 1) * 512], _p, 0.25,
                sigx[:, _tb * 512:(_tb + 1) * 512], OP.mult, OP.mult),
                waits=[m1, sg, hst])
        hst = dma(hidB[cs * 128:(cs + 1) * 128, :], hslab[:], waits=[hprev])
    F1_LD = [S.now(n) for n in LDS]

    oSpr = oSp.rearrange("(h p) (c d) -> p c h d", p=128, d=128)
    nb = barrier_toks() + F1_LD + ATTN_LD
    ntok = None
    gf_prev = None
    on2 = None
    for oc in range(NCO):
        tl = dma(onrow.rearrange("p (g d) -> p g d", g=H), oSpr[:, oc],
                 waits=[ntok] + (nb if oc == 0 else []))
        gts = []
        for cb in range(4):
            ps = PSF[cb][:, 0:512]
            mg = tn(lambda e, _o=oc, _cb=cb, _p=ps: e.matmul(
                _p, out2T[:, _o * 128:(_o + 1) * 128],
                wg2b[:, _cb * 512:(_cb + 1) * 512], start=True, stop=True),
                waits=[gf_prev, o2tok] + (nb if oc == 0 else []))
            gf = v(lambda e, _cb=cb, _p=ps: e.tensor_add(
                gate_f[:, _cb * 512:(_cb + 1) * 512], _p,
                bg2_s[:, _cb * 512:(_cb + 1) * 512]),
                waits=[mg] + ([on2] if on2 else []))
            gts.append(sc(lambda e, _cb=cb: e.activation(
                gatesig[:, _cb * 512:(_cb + 1) * 512],
                gate_f[:, _cb * 512:(_cb + 1) * 512], AF.Sigmoid), waits=[gf]))
        gf_prev = gts[-1]
        sq = v(lambda e: e.tensor_mul(sqn[:], onrow[:], onrow[:]), waits=[tl])
        rs2 = v(lambda e: e.reduce_sum(
            rstd16[:, 0:16], sqn.rearrange("p (g d) -> p g d", g=H), AX.X),
            waits=[sq, S.now(SS)])
        rs3 = v(lambda e: e.tensor_scalar(rstd16[:, 0:16], rstd16[:, 0:16],
                                          1.0 / 128.0, 1e-5, OP.mult, OP.add),
                waits=[rs2])
        rs4 = sc(lambda e: e.activation(rstd16[:, 0:16], rstd16[:, 0:16],
                                        AF.Ln), waits=[rs3])
        rs5 = sc(lambda e: e.activation(rstd16[:, 0:16], rstd16[:, 0:16],
                                        AF.Exp, scale=-0.5), waits=[rs4])
        on1 = v(lambda e: e.tensor_mul(onrow[:], onrow[:], gatesig[:]),
                waits=gts + [rs2, sq])
        rb3 = rstd16[:, 0:16].unsqueeze(2).broadcast_to([128, 16, 128])
        on2 = v(lambda e: e.tensor_tensor(
            onrow.rearrange("p (g d) -> p g d", g=H),
            onrow.rearrange("p (g d) -> p g d", g=H), rb3, OP.mult),
            waits=[rs5, on1])
        ntok = dma(onT.rearrange("p (j t) -> p j t", j=16)
                   [:, :, oc * 128:(oc + 1) * 128],
                   onrow[:], waits=[on2], transpose=True)
        if oc == 0:
            dbg_tap("gatesig", gatesig, [gts[-1]])
            dbg_tap("rstd16", rstd16, [rs5])
            dbg_tap("onrow0", onrow, [on2])
    NORM_LD = [S.now(n) for n in LDS]
    dbg_tap("onT", onT, NORM_LD)

    q2w3r = q2w[2].rearrange("(k p) c -> p k c", p=128)
    f2b = barrier_toks() + NORM_LD
    fprev = None
    wcast_prev = None
    yst = None
    for tq in range(4):
        tlh = dma(hidq.rearrange("p (k c) -> p k c", k=44),
                  hidB.rearrange("(k p) c -> p k c", p=128)
                  [:, :, tq * 512:(tq + 1) * 512],
                  waits=[fprev] + (f2b if tq == 0 else []))
        for cbk in range(4):
            tl3 = dma(w3cb.rearrange("p (k c) -> p k c", k=44),
                      q2w3r[:, :, cbk * 512:(cbk + 1) * 512],
                      waits=[fprev] + (f2b if tq == 0 and cbk == 0 else []))
            wtoks = []
            for j in range(16):
                tw = dma(wstage[:], WOp[j * 128:(j + 1) * 128,
                                        cbk * 512:(cbk + 1) * 512],
                         waits=[wcast_prev] + (f2b if tq == 0 and cbk == 0
                                               else []))
                wcast_prev = v(lambda e, _j=j: e.tensor_scalar(
                    wocb[:, _j * 512:(_j + 1) * 512], wstage[:], nw_s[:, 0:1],
                    None, OP.mult),
                    waits=[tw] + ([S.now(ST)] if j == 0 else []))
                wtoks.append(wcast_prev)
            for tc in range(4):
                co = tq * 4 + tc
                ps = PSF[tc % 2][:, 0:512]
                tx = dma(xstage[:], x_in[HALO + co * 128:HALO + (co + 1) * 128,
                                         cbk * 512:(cbk + 1) * 512],
                         waits=[fprev])
                pairs = [(hidq[:, k * 512 + tc * 128:k * 512 + tc * 128 + 128],
                          w3cb[:, k * 512:(k + 1) * 512]) for k in range(44)]
                pairs += [(onT[:, j * T_OWN + co * 128:j * T_OWN + co * 128 + 128],
                           wocb[:, j * 512:(j + 1) * 512]) for j in range(16)]
                mt = mm_acc(ps, pairs, [tlh, tl3, fprev] + wtoks)
                fprev = v(lambda e, _p=ps: e.tensor_add(outrow[:], _p,
                                                        xstage[:]),
                          waits=[mt, tx, yst])
                yst = dma(y[co * 128:(co + 1) * 128, cbk * 512:(cbk + 1) * 512],
                          outrow[:, 0:512], waits=[fprev])
    S.op("sync", lambda e: None, waits=[S.now(n) for n in LDS])

    from contextlib import ExitStack
    es = ExitStack()
    sems = {n: es.enter_context(nc.semaphore(n))
            for n in (["SV", "SS", "ST"] + ["LD%d" % i for i in range(8)])}
    with nc.Block() as blk:
        @blk.sync
        def _(eng):
            S.emit("sync", eng, sems)

        @blk.vector
        def _(eng):
            S.emit("vector", eng, sems)

        @blk.scalar
        def _(eng):
            S.emit("scalar", eng, sems)

        @blk.tensor
        def _(eng):
            S.emit("tensor", eng, sems)

        @blk.gpsimd
        def _(eng):
            S.emit("gpsimd", eng, sems)
    es.close()
    return nc


_NC_CACHE = None


def kernel(L0_attn_hidden_states, L0_attn_Wq, L0_attn_Wk, L0_attn_Wv, L0_attn_Ww,
           L0_attn_cq, L0_attn_ck, L0_attn_cv, L0_attn_Wf1, L0_attn_Wf2,
           L0_attn_Wb, L0_attn_Wg1, L0_attn_Wg2, L0_attn_bg2, L0_attn_norm_w,
           L0_attn_Wo, L0_ffn_w1, L0_ffn_w2, L0_ffn_w3):
    global _NC_CACHE
    x = np.asarray(L0_attn_hidden_states, dtype=np.float32)
    B, T, _ = x.shape
    assert (B, T) == (2, 8192)

    shared = {
        "Wq": np.ascontiguousarray(np.asarray(L0_attn_Wq, np.float32)),
        "Wk": np.ascontiguousarray(np.asarray(L0_attn_Wk, np.float32)),
        "Wv": np.ascontiguousarray(np.asarray(L0_attn_Wv, np.float32)),
        "Ww": np.ascontiguousarray(np.asarray(L0_attn_Ww, np.float32)),
        "Wo": np.ascontiguousarray(np.asarray(L0_attn_Wo, np.float32)),
        "Wf1": np.ascontiguousarray(np.asarray(L0_attn_Wf1, np.float32)),
        "Wf2": np.ascontiguousarray(np.asarray(L0_attn_Wf2, np.float32)),
        "Wb": np.ascontiguousarray(np.asarray(L0_attn_Wb, np.float32)),
        "Wg1": np.ascontiguousarray(np.asarray(L0_attn_Wg1, np.float32)),
        "Wg2": np.ascontiguousarray(np.asarray(L0_attn_Wg2, np.float32)),
        "cq": np.ascontiguousarray(np.asarray(L0_attn_cq, np.float32)),
        "ck": np.ascontiguousarray(np.asarray(L0_attn_ck, np.float32)),
        "cv": np.ascontiguousarray(np.asarray(L0_attn_cv, np.float32)),
        "w1T": np.ascontiguousarray(np.asarray(L0_ffn_w1, np.float32).T),
        "w2T": np.ascontiguousarray(np.asarray(L0_ffn_w2, np.float32).T),
        "w3T": np.ascontiguousarray(np.asarray(L0_ffn_w3, np.float32).T),
        "bg2r": np.ascontiguousarray(
            np.broadcast_to(np.asarray(L0_attn_bg2, np.float32), (128, HID))),
        "nw_col": np.ascontiguousarray(
            np.asarray(L0_attn_norm_w, np.float32).reshape(128, 1)),
        "cb": np.ascontiguousarray(np.concatenate(
            [np.eye(128, dtype=BFD),
             np.triu(np.ones((128, 128), dtype=BFD)),
             np.full((128, 128), 0.5, dtype=BFD)], axis=1)),
        "cf": np.ascontiguousarray(np.tile(
            np.array([1e-5, 1e-6, 1.0, 0.0, 0.0, 0.0, 0.0, 0.0],
                     np.float32), (128, 1))),
        "crow": np.ones((1, 128), np.float32),
    }

    in_maps = []
    for c in range(8):
        b, qi = c // 4, c % 4
        lo = qi * T_OWN
        if qi == 0:
            xs = np.concatenate([np.zeros((HALO, HID), np.float32),
                                 x[b, 0:T_OWN]], axis=0)
        else:
            xs = x[b, lo - HALO: lo + T_OWN]
        m = dict(shared)
        m["x_in"] = np.ascontiguousarray(xs)
        in_maps.append(m)

    if _NC_CACHE is None:
        _NC_CACHE = build_program(debug=bool(globals().get("_DBG_MODE")))
    res = run_bass_kernel_spmd(_NC_CACHE, in_maps, core_ids=list(range(8)))
    if globals().get("_DBG_MODE"):
        globals()["_DBG_RES"] = res

    out = np.empty((2, 8192, HID), np.float32)
    for c in range(8):
        b, qi = c // 4, c % 4
        out[b, qi * T_OWN:(qi + 1) * T_OWN] = res.results[c]["y"]
    return out


if __name__ == "__main__":
    nc = build_program()
    print("build ok, instructions:", nc.n_instructions())


# version 12
# speedup vs baseline: 1.0438x; 1.0438x over previous
import numpy as np
import ml_dtypes
import concourse.bass as bass
import concourse.mybir as mybir
from concourse.bass_utils import run_bass_kernel_spmd
from concourse.alu_op_type import AluOpType as OP

F32 = mybir.dt.float32
BF16 = mybir.dt.bfloat16
AF = mybir.ActivationFunctionType
AX = mybir.AxisListType
BFD = ml_dtypes.bfloat16

HID = 2048
H = 16
T_OWN = 2048
HALO = 512
T_IN = T_OWN + HALO
NCI = T_IN // 128
NCC = 19
NCO = 16
FFH = 5632
SCALE = 128.0 ** -0.5
NELEM_W = float(HID * FFH)


class Sch:
    ENGS = ("sync", "vector", "scalar", "tensor", "gpsimd")

    def __init__(self):
        self.ops = {e: [] for e in self.ENGS}
        self.cnt = {}
        self.seen = {e: {} for e in self.ENGS}

    def sem(self, name):
        self.cnt.setdefault(name, 0)
        return name

    DMA_SEMS = tuple("LD%d" % i for i in range(8))

    def op(self, eng, fn, waits=(), inc=None, amt=1):
        w = []
        for t in waits:
            if t is None:
                continue
            s, vv = t
            if s is None or vv is None or vv <= 0:
                continue
            if self.seen[eng].get(s, -1) >= vv:
                continue
            self.seen[eng][s] = vv
            w.append((s, vv))
        tgt = None
        if inc is not None:
            self.cnt[inc] += amt
            tgt = self.cnt[inc]
        self.ops[eng].append((w, fn, inc, amt))
        return (inc, tgt) if inc is not None else None

    def now(self, sem):
        return (sem, self.cnt.get(sem, 0))

    def emit(self, eng_name, eng, sems):
        for w, fn, inc, amt in self.ops[eng_name]:
            for s, vv in w:
                eng.wait_ge(sems[s], vv)
            ins = fn(eng)
            if inc is not None:
                if ins is None:
                    ins = eng.nop()
                ins.then_inc(sems[inc], amt)


def build_program(debug=False):
    nc = bass.Bass("TRN2", target_bir_lowering=False)

    def din(name, shape, dt=F32):
        return nc.dram_tensor(name, list(shape), dt, kind="ExternalInput").ap()

    x_in = din("x_in", (T_IN, HID))
    WQ = din("Wq", (HID, HID)); WK = din("Wk", (HID, HID))
    WV = din("Wv", (HID, HID)); WW = din("Ww", (HID, HID))
    WOp = din("Wo", (HID, HID))
    WF1 = din("Wf1", (HID, 128)); WF2 = din("Wf2", (128, HID))
    WB = din("Wb", (HID, H))
    WG1 = din("Wg1", (HID, 128)); WG2 = din("Wg2", (128, HID))
    CQ = din("cq", (HID, 4)); CK = din("ck", (HID, 4)); CV = din("cv", (HID, 4))
    W1T = din("w1T", (HID, FFH)); W2T = din("w2T", (HID, FFH))
    W3T = din("w3T", (FFH, HID))
    BG2R = din("bg2r", (128, HID))
    NWC = din("nw_col", (128, 1))
    CB = din("cb", (128, 3 * 128), BF16)
    CF = din("cf", (128, 8))
    CROW = din("crow", (1, 128))

    y = nc.dram_tensor("y", [T_OWN, HID], F32, kind="ExternalOutput").ap()

    def dint(name, shape, dt):
        return nc.dram_tensor(name, list(shape), dt, kind="Internal").ap()

    def dmaybe(name, shape, dt):
        kind = "ExternalOutput" if debug else "Internal"
        return nc.dram_tensor(name, list(shape), dt, kind=kind).ap()

    q2w = [dmaybe("q2w1", (HID, FFH), BF16), dmaybe("q2w2", (HID, FFH), BF16),
           dmaybe("q2w3", (FFH, HID), BF16)]
    hidB = dmaybe("hidB", (FFH, T_OWN), BF16)
    oSp = dmaybe("oSp", (H * 128, NCO * 128), BF16)

    DBG = {}

    def dbg_tap(name, ap, waits):
        if not debug:
            return
        if not isinstance(ap, bass.AP):
            ap = ap[:]
        t = nc.dram_tensor("dbg_" + name, [ap.shape[0], ap.shape[1]],
                           ap.dtype, kind="ExternalOutput").ap()
        DBG[name] = dma(t, ap, waits=waits)

    sbt = nc.alloc_sbuf_tensor

    cb_s = sbt("cb_s", [128, 3 * 128], BF16)
    IDENT = cb_s[:, 0:128]
    UTC = cb_s[:, 128:256]
    HALFM = cb_s[:, 256:384]
    cf_s = sbt("cf_s", [128, 8], F32)
    EPS_LN = cf_s[:, 0:1]; EPS_L2 = cf_s[:, 1:2]; ONE_F = cf_s[:, 2:3]
    crow_s = sbt("crow_s", [1, 128], F32)
    ones_bf = sbt("ones_bf", [128, 16], BF16)
    ONES_COL = ones_bf[:, 0:1]
    nw_s = sbt("nw_s", [128, 1], F32)
    bg2_s = sbt("bg2_s", [128, HID], BF16)
    wf2b = sbt("wf2b", [128, HID], BF16)
    wg2b = sbt("wg2b", [128, HID], BF16)
    cqall = sbt("cqall", [128, 3 * 64], F32)
    beta_s = sbt("beta_s", [128, NCI * H], F32)
    out1T = sbt("out1T", [128, T_IN], BF16)
    out2T = sbt("out2T", [128, T_OWN], BF16)
    lnst = sbt("lnst", [128, 8], F32)
    gcols = sbt("gcols", [128, 32], F32)
    dcols = sbt("dcols", [128, 32], F32)
    ercols = sbt("ercols", [128, 32], F32)
    sscol = sbt("sscol", [128, 48], F32)
    brcol = sbt("brcol", [128, 32], F32)
    brS = sbt("brS", [128, 32], F32)
    ncol = sbt("ncol", [128, 8], F32)
    rstd16 = sbt("rstd16", [128, 16], F32)
    gsum = sbt("gsum", [128, 8], F32)
    gcolq = sbt("gcolq", [128, 4], F32)
    gtmp = sbt("gtmp", [1, 8], F32)
    SkT = sbt("SkT", [128, 128], BF16)
    Sv = sbt("Sv", [128, 128], BF16)
    SkEP = [sbt("SkE%d" % i, [128, 128], BF16) for i in range(2)]
    SkEdP = [sbt("SkEd%d" % i, [128, 128], BF16) for i in range(2)]
    QKmP = [sbt("QKm%d" % i, [128, 128], BF16) for i in range(2)]
    stilP = [sbt("stil%d" % i, [128, 128], BF16) for i in range(2)]
    a_eP = [sbt("a_e%d" % i, [128, 128], BF16) for i in range(2)]
    aTtP = [sbt("aTt%d" % i, [128, 128], BF16) for i in range(2)]
    aHtP = [sbt("aHt%d" % i, [128, 128], BF16) for i in range(2)]
    PTmP = [sbt("PTm%d" % i, [128, 128], BF16) for i in range(2)]

    AR1 = sbt("AR1", [128, 43008], BF16)
    AR2 = sbt("AR2", [128, 47232], BF16)

    def carve(ar, off_bytes, cols, dt):
        assert off_bytes % 4 == 0
        c2 = off_bytes // 2
        if dt == BF16:
            assert (off_bytes + 2 * cols) <= 2 * ar.shape[1]
            return ar[:, c2:c2 + cols]
        assert (off_bytes + 4 * cols) <= 2 * ar.shape[1]
        return ar[:, c2:c2 + 2 * cols].bitcast(F32)

    XNT = carve(AR1, 0, 16 * T_IN, BF16)
    onT = carve(AR1, 0, 16 * T_OWN, BF16)
    wocb = carve(AR1, 65536, 16 * 512, BF16)
    wstage = carve(AR1, 81920, 512, F32)
    outrow = carve(AR1, 83968, 512, F32)

    xrow = carve(AR2, 0, HID, F32)
    sqrow = carve(AR2, 8192, HID, F32)
    xnrow = carve(AR2, 16384, HID, BF16)
    _off = [0]

    def hc(cols, dt):
        ap = carve(AR2, _off[0], cols, dt)
        _off[0] += cols * (2 if dt == BF16 else 4)
        _off[0] = (_off[0] + 63) & ~63
        return ap

    wcol = hc(HID, F32)
    wcolb = hc(HID, BF16)
    preconv = hc(T_IN + 4, F32)
    E1 = carve(AR2, 0, NCC * 128, BF16)
    E2 = carve(AR2, 4864, NCC * 128, BF16)
    ERb = carve(AR2, 9728, NCC * 128, BF16)
    expRr = carve(AR2, 14592, NCC * 128, BF16)
    projbuf = hc(T_IN, BF16)
    kT_s = hc(T_IN, BF16)
    qT_s = hc(T_IN, BF16)
    K_tm = hc(NCC * 128, BF16)
    V_tm = hc(NCC * 128, BF16)
    Vh_tm = hc(NCC * 128, BF16)
    W_tm = hc(NCC * 128, BF16)
    g_sm = hc(NCC * 128, BF16)
    wt_sm = hc(NCC * 128, BF16)
    wh_sm = hc(NCC * 128, BF16)
    grow = hc(512, F32)
    zrow = hc(512, F32)
    z2row = hc(512, F32)
    obuf = hc(NCO * 128, BF16)
    Sk_all = hc(NCO * 128, BF16)
    Sv_all = hc(NCO * 128, BF16)
    assert _off[0] <= 94464, _off[0]
    fslab = carve(AR2, 0, FFH, F32)
    fsl_a = carve(AR2, 22528, FFH, BF16)
    fsl_b = carve(AR2, 33792, FFH, BF16)
    q2a = carve(AR2, 0, HID, BF16)
    q2b = carve(AR2, 4096, HID, BF16)
    sigx = carve(AR2, 8192, T_OWN, BF16)
    hslab = carve(AR2, 12288, T_OWN, BF16)
    onrow = carve(AR2, 0, HID, BF16)
    gate_f = carve(AR2, 4096, HID, F32)
    gatesig = carve(AR2, 12288, HID, BF16)
    sqn = carve(AR2, 16384, HID, BF16)
    hidq = carve(AR2, 0, 44 * 512, BF16)
    w3cb = carve(AR2, 45056, 44 * 512, BF16)
    xstage = carve(AR2, 90112, 512, F32)

    PSF = [nc.alloc_psum_tensor("PS%d" % i, [128, 512], F32) for i in range(6)]
    PB6 = nc.alloc_psum_tensor("PB6", [128, 1024], BF16)
    PB7 = nc.alloc_psum_tensor("PB7", [128, 1024], BF16)
    TRA = PB6[:, 0:128]
    TRB = PB7[:, 0:128]
    TRC = PB7[:, 512:640]

    S = Sch()
    LDS = [S.sem("LD%d" % i) for i in range(8)]
    SV = S.sem("SV"); SS = S.sem("SS"); ST = S.sem("ST")
    _rr = [0]

    def dma(out_, in_, waits=(), transpose=False):
        if transpose:
            waits = list(waits) + [S.now(n) for n in LDS]
        sem = LDS[_rr[0] % 8]
        _rr[0] += 1
        if S.cnt[sem] > 0:
            S.op("sync", lambda e: None, waits=[S.now(sem)])

        def f(e, _o=out_, _i=in_, _t=transpose):
            return e.dma_start(out=_o, in_=_i, transpose=_t)
        tok = S.op("sync", f, waits=waits, inc=sem, amt=16)
        if transpose:
            for n in LDS:
                S.seen["sync"].pop(n, None)
            S.op("sync", lambda e: None, waits=[S.now(sem)])
        return tok

    def v(fn, waits=()):
        return S.op("vector", fn, waits=waits, inc=SV)

    def sc(fn, waits=()):
        return S.op("scalar", fn, waits=waits, inc=SS)

    def tn(fn, waits=()):
        return S.op("tensor", fn, waits=waits, inc=ST)

    def barrier_toks():
        return [S.now(SV), S.now(SS), S.now(ST)] + [S.now(n) for n in LDS]

    def mm_acc(ps_ap, pairs, waits_first):
        tok = None
        n = len(pairs)
        for k, (lt, rh) in enumerate(pairs):
            tok = tn(lambda e, _l=lt, _r=rh, _s=(k == 0), _e=(k == n - 1),
                     _p=ps_ap: e.matmul(_p, _l, _r, start=_s, stop=_e),
                     waits=waits_first if k == 0 else ())
        return tok

    dma(cb_s[:], CB)
    dma(cf_s[:], CF)
    dma(crow_s[:], CROW)
    dma(nw_s[:], NWC)
    tl = dma(gate_f[:], BG2R)
    v(lambda e: e.memset(ones_bf[:], 1.0))
    v(lambda e: e.memset(PSF[5][64:128, 0:64], 0.0))
    bgtok = v(lambda e: e.tensor_copy(bg2_s[:], gate_f[:]), waits=[tl])
    dma(cqall.rearrange("p (s k w) -> p s k w", s=3, w=4)[:, 0],
        CQ.rearrange("(k p) w -> p k w", p=128))
    dma(cqall.rearrange("p (s k w) -> p s k w", s=3, w=4)[:, 1],
        CK.rearrange("(k p) w -> p k w", p=128))
    dma(cqall.rearrange("p (s k w) -> p s k w", s=3, w=4)[:, 2],
        CV.rearrange("(k p) w -> p k w", p=128))
    tl = dma(wcol[:], WF2, waits=[bgtok])
    wf2tok = v(lambda e: e.tensor_copy(wf2b[:], wcol[:]), waits=[tl])
    tl = dma(xrow[:], WG2, waits=[wf2tok])
    wg2tok = v(lambda e: e.tensor_copy(wg2b[:], xrow[:]), waits=[tl])

    prev_used = wg2tok
    ttok = None
    for i in range(NCI):
        tl = dma(xrow[:], x_in[i * 128:(i + 1) * 128, :], waits=[prev_used])
        t1 = v(lambda e: e.reduce_sum(lnst[:, 0:1], xrow[:], AX.X), waits=[tl])
        t2 = sc(lambda e: e.activation(sqrow[:], xrow[:], AF.Square,
                                       accum_out=lnst[:, 1:2]), waits=[tl])
        t3 = v(lambda e: e.tensor_scalar(lnst[:, 2:3], lnst[:, 0:1], 1.0 / HID,
                                         None, OP.mult), waits=[t1])
        t4 = v(lambda e: e.tensor_scalar(lnst[:, 3:4], lnst[:, 1:2], 1.0 / HID,
                                         None, OP.mult), waits=[t2])
        t5 = v(lambda e: e.tensor_mul(lnst[:, 4:5], lnst[:, 2:3], lnst[:, 2:3]),
               waits=[t3])
        t6 = v(lambda e: e.tensor_sub(lnst[:, 5:6], lnst[:, 3:4], lnst[:, 4:5]),
               waits=[t4, t5])
        t7 = sc(lambda e: e.activation(lnst[:, 6:7], lnst[:, 5:6], AF.Ln,
                                       bias=EPS_LN), waits=[t6])
        t8 = sc(lambda e: e.activation(lnst[:, 7:8], lnst[:, 6:7], AF.Exp,
                                       scale=-0.5), waits=[t7])
        t9 = v(lambda e: e.tensor_scalar(xnrow[:], xrow[:], lnst[:, 2:3],
                                         lnst[:, 7:8], OP.subtract, OP.mult),
               waits=[t8, t3, ttok])
        ttok = dma(XNT.rearrange("p (k t) -> p k t", k=16)
                   [:, :, i * 128:(i + 1) * 128],
                   xnrow[:], waits=[t9], transpose=True)
        prev_used = t9
    XN_RDY = ttok
    dbg_tap("XNT", XNT, [XN_RDY])

    tl = dma(wcol.rearrange("p (k c) -> p k c", k=16),
             WF1.rearrange("(k p) c -> p k c", p=128), waits=[XN_RDY])
    cw = v(lambda e: e.tensor_copy(wcolb[:], wcol[:]), waits=[tl, wf2tok])
    evtoks = {}
    for tb in range(5):
        ps = PSF[tb % 2][:, 0:512]
        pairs = [(wcolb[:, k * 128:(k + 1) * 128],
                  XNT[:, k * T_IN + tb * 512:k * T_IN + tb * 512 + 512])
                 for k in range(16)]
        mt = mm_acc(ps, pairs, [cw, XN_RDY, evtoks.get(tb - 2)])
        evtoks[tb] = v(lambda e, _tb=tb, _ps=ps: e.tensor_copy(
            out1T[:, _tb * 512:(_tb + 1) * 512], _ps), waits=[mt])
    o1tok = S.now(SV)
    tl = dma(wcol.rearrange("p (k c) -> p k c", k=16),
             WG1.rearrange("(k p) c -> p k c", p=128), waits=[o1tok])
    cw = v(lambda e: e.tensor_copy(wcolb[:], wcol[:]), waits=[tl, o1tok])
    evtoks = {}
    for tb in range(4):
        ps = PSF[2 + tb % 2][:, 0:512]
        off = 512 + tb * 512
        pairs = [(wcolb[:, k * 128:(k + 1) * 128],
                  XNT[:, k * T_IN + off:k * T_IN + off + 512]) for k in range(16)]
        mt = mm_acc(ps, pairs, [cw, evtoks.get(tb - 2)])
        evtoks[tb] = v(lambda e, _tb=tb, _ps=ps: e.tensor_copy(
            out2T[:, _tb * 512:(_tb + 1) * 512], _ps), waits=[mt])
    o2tok = S.now(SV)
    tl = dma(wcol.rearrange("p (k c) -> p k c", c=16)[:, 0:16, :],
             WB.rearrange("(k p) c -> p k c", p=128), waits=[o2tok])
    cw = v(lambda e: e.tensor_copy(wcolb[:, 0:256], wcol[:, 0:256]),
           waits=[tl, o2tok])
    bprev = None
    for i in range(NCI):
        ps = PSF[4][:, 0:16]
        pairs = [(XNT[:, k * T_IN + i * 128:k * T_IN + i * 128 + 128],
                  wcolb[:, k * 16:(k + 1) * 16]) for k in range(16)]
        mt = mm_acc(ps, pairs, [cw, bprev])
        bprev = sc(lambda e, _i=i, _ps=ps: e.activation(
            beta_s[:, _i * 16:(_i + 1) * 16], _ps, AF.Sigmoid), waits=[mt])
    beta_tok = bprev
    dbg_tap("out1T", out1T, [o1tok])
    dbg_tap("out2T", out2T, [o2tok])
    dbg_tap("beta", beta_s, [beta_tok])

    beta3 = beta_s.rearrange("p (c h) -> p c h", h=H)
    zpad = v(lambda e: e.memset(preconv[:, 0:4], 0.0), waits=[beta_tok])

    def proj_conv(h, Wmat, cvoff, dest, extra_w):
        tl = dma(wcol.rearrange("p (k c) -> p k c", k=16),
                 Wmat.rearrange("(k p) c -> p k c", p=128)
                 [:, :, h * 128:(h + 1) * 128],
                 waits=[S.now(SV)] + list(extra_w))
        cw = v(lambda e: e.tensor_copy(wcolb[:], wcol[:]),
               waits=[tl, S.now(ST)])
        evs = {}
        for tb in range(5):
            ps = PSF[tb % 2][:, 0:512]
            pairs = [(wcolb[:, k * 128:(k + 1) * 128],
                      XNT[:, k * T_IN + tb * 512:k * T_IN + tb * 512 + 512])
                     for k in range(16)]
            mt = mm_acc(ps, pairs, [cw, evs.get(tb - 2), S.now(SV)])
            evs[tb] = v(lambda e, _tb=tb, _ps=ps: e.tensor_copy(
                preconv[:, 4 + _tb * 512:4 + (_tb + 1) * 512], _ps), waits=[mt])
        base = cvoff * 64 + h * 4
        cc = [cqall[:, base + i:base + i + 1] for i in range(4)]
        t = v(lambda e: e.tensor_scalar(dest[:], preconv[:, 1:1 + T_IN], cc[0],
                                        None, OP.mult),
              waits=[evs[4], zpad, S.now(ST)] + list(extra_w))
        for i in (1, 2, 3):
            t = v(lambda e, _i=i: e.scalar_tensor_tensor(
                dest[:], preconv[:, 1 + _i:1 + _i + T_IN], cc[_i], dest[:],
                OP.mult, OP.add), waits=[t])
        t = sc(lambda e: e.activation(dest[:], dest[:], AF.Silu), waits=[t])
        return t

    GRPS = [(0, 4), (4, 4), (8, 4), (12, 4), (16, 3)]
    spill_prev = None
    for h in range(H):
        hb = barrier_toks()
        t = proj_conv(h, WW, 2, projbuf, hb)
        trW = dma(W_tm.rearrange("p (c t) -> p c t", c=NCC),
                  projbuf[:, 128:T_IN], waits=[t, S.now(ST)], transpose=True)
        t = proj_conv(h, WV, 2, projbuf, [trW])
        trV = dma(V_tm.rearrange("p (c t) -> p c t", c=NCC),
                  projbuf[:, 128:T_IN], waits=[t, S.now(ST)], transpose=True)
        t = proj_conv(h, WK, 1, kT_s, [])
        trK = dma(K_tm.rearrange("p (c t) -> p c t", c=NCC),
                  kT_s[:, 128:T_IN], waits=[t, S.now(ST)], transpose=True)
        qtok = proj_conv(h, WQ, 0, qT_s, [])
        gevs = []
        ge_prev = None
        for tb in range(5):
            ps = PSF[2 + tb % 2][:, 0:512]
            mt = mm_acc(ps, [(wf2b[:, h * 128:(h + 1) * 128],
                              out1T[:, tb * 512:(tb + 1) * 512])],
                        [wf2tok, S.now(SV), S.now(SS)])
            e1 = sc(lambda e, _ps=ps: e.activation(zrow[:], _ps, AF.Abs),
                    waits=[mt, ge_prev])
            e2 = sc(lambda e: e.activation(grow[:], zrow[:], AF.Exp, scale=-1.0),
                    waits=[e1])
            e3 = sc(lambda e: e.activation(grow[:], grow[:], AF.Ln, bias=ONE_F),
                    waits=[e2])
            e4 = v(lambda e, _tb=tb, _ps=ps: e.scalar_tensor_tensor(
                projbuf[:, _tb * 512:(_tb + 1) * 512], _ps, 0.0, grow[:],
                OP.min, OP.subtract), waits=[e3, trV, trK])
            ge_prev = e4
            gevs.append(e4)
        trG = dma(g_sm.rearrange("p (c t) -> p c t", c=NCC),
                  projbuf[:, 128:T_IN], waits=gevs + [S.now(ST)], transpose=True)

        dtoks = []
        cp_prev = zz_prev = se1_prev = se3_prev = se4_prev = None
        for g0, gn in GRPS:
            gw = gn * 128
            mg = None
            for j in range(gn):
                mg = tn(lambda e, _j=j, _c=g0 + j: e.matmul(
                    PSF[0][:, _j * 128:(_j + 1) * 128], UTC,
                    g_sm[:, _c * 128:(_c + 1) * 128], start=True, stop=True),
                    waits=[trG, cp_prev, se1_prev, S.now(SV)] if j == 0 else ())
            mr = None
            for j in range(gn):
                mr = tn(lambda e, _j=j, _c=g0 + j: e.matmul(
                    PSF[1][:, _j * 128:(_j + 1) * 128], HALFM,
                    g_sm[:, _c * 128:(_c + 1) * 128], start=True, stop=True),
                    waits=[zz_prev, se4_prev] if j == 0 else ())
            cp = v(lambda e, _w=gw: e.tensor_copy(grow[:, 0:_w], PSF[0][:, 0:_w]),
                   waits=[mg, S.now(SS)])
            se1 = sc(lambda e, _w=gw, _o=g0 * 128: e.activation(
                E1[:, _o:_o + _w], PSF[0][:, 0:_w], AF.Exp),
                waits=[mg, S.now(SV)])
            zz = v(lambda e, _w=gw: e.tensor_sub(zrow[:, 0:_w], PSF[1][:, 0:_w],
                                                 grow[:, 0:_w]),
                   waits=[cp, mr, se3_prev])
            zh = v(lambda e, _w=gw: e.tensor_add(z2row[:, 0:_w], zrow[:, 0:_w],
                                                 PSF[1][:, 0:_w]),
                   waits=[zz, se4_prev])
            zc = v(lambda e, _w=gw: e.tensor_scalar(zrow[:, 0:_w], zrow[:, 0:_w],
                                                    80.0, -80.0, OP.min, OP.max),
                   waits=[zh])
            se2 = sc(lambda e, _w=gw, _o=g0 * 128: e.activation(
                E2[:, _o:_o + _w], zrow[:, 0:_w], AF.Exp, scale=-1.0), waits=[zc])
            se3 = sc(lambda e, _w=gw, _o=g0 * 128: e.activation(
                ERb[:, _o:_o + _w], zrow[:, 0:_w], AF.Exp))
            se4 = sc(lambda e, _w=gw, _o=g0 * 128: e.activation(
                expRr[:, _o:_o + _w], z2row[:, 0:_w], AF.Exp), waits=[zh])
            cp_prev, zz_prev, se1_prev, se3_prev, se4_prev = cp, zc, se1, se3, se4
            dtoks += [se1, se2, se3, se4]
        mgc = None
        for c in range(NCC):
            mgc = tn(lambda e, _c=c: e.matmul(PSF[2][:, _c:_c + 1],
                                              g_sm[:, _c * 128:(_c + 1) * 128],
                                              ONES_COL, start=True, stop=True),
                     waits=[trG, S.now(SV), S.now(SS)] if c == 0 else ())
        cgc = v(lambda e: e.tensor_copy(gcols[:, 0:NCC], PSF[2][:, 0:NCC]),
                waits=[mgc, S.now(SS)])
        sdc = sc(lambda e: e.activation(dcols[:, 0:NCC], gcols[:, 0:NCC], AF.Exp),
                 waits=[cgc, S.now(SV)])
        sq1 = v(lambda e: e.tensor_mul(wt_sm[:], W_tm[:], W_tm[:]),
                waits=[trW, S.now(ST)])
        l2b = v(lambda e: e.reduce_sum(
            sscol[:, 0:NCC], wt_sm.rearrange("p (c m) -> p c m", c=NCC), AX.X),
            waits=[sq1, S.now(SS)])
        l2c = sc(lambda e: e.activation(sscol[:, 20:20 + NCC], sscol[:, 0:NCC],
                                        AF.Ln, bias=EPS_L2), waits=[l2b])
        l2d = sc(lambda e: e.activation(sscol[:, 20:20 + NCC],
                                        sscol[:, 20:20 + NCC], AF.Exp,
                                        scale=-0.5), waits=[l2c])
        brt = v(lambda e, _h=h: e.tensor_mul(brcol[:, 0:NCC],
                                             sscol[:, 20:20 + NCC],
                                             beta3[:, 1:1 + NCC, _h]),
                waits=[l2d, beta_tok, S.now(ST)])
        brst = v(lambda e: e.tensor_scalar(brS[:, 0:NCC], brcol[:, 0:NCC], SCALE,
                                           None, OP.mult), waits=[brt])
        wtk = v(lambda e: e.tensor_mul(wt_sm[:], W_tm[:], ERb[:]),
                waits=[l2b] + dtoks)
        whk = v(lambda e: e.tensor_mul(wh_sm[:], W_tm[:], expRr[:]),
                waits=[wtk])
        br3 = brcol[:, 0:NCC].unsqueeze(2).broadcast_to([128, NCC, 128])
        khk = v(lambda e: e.tensor_tensor(
            K_tm.rearrange("p (c t) -> p c t", c=NCC),
            K_tm.rearrange("p (c t) -> p c t", c=NCC), br3, OP.mult),
            waits=[trK, brt])
        vhk = v(lambda e: e.tensor_tensor(
            Vh_tm.rearrange("p (c t) -> p c t", c=NCC),
            V_tm.rearrange("p (c t) -> p c t", c=NCC), br3, OP.mult),
            waits=[trV, brt])
        if h == 0:
            dbg_tap("kT", kT_s, [S.now(SS)])
            dbg_tap("qT", qT_s, [qtok])
            dbg_tap("W_tm", W_tm, [trW])
            dbg_tap("V_tm", V_tm, [trV])
            dbg_tap("g_sm", g_sm, [trG])
            dbg_tap("E1", E1, dtoks)
            dbg_tap("E2", E2, dtoks)
            dbg_tap("ERb", ERb, dtoks)
            dbg_tap("expRr", expRr, dtoks)
            dbg_tap("gcols", gcols, [cgc])
            dbg_tap("dcols", dcols, [sdc])
            dbg_tap("sscol", sscol, [l2d])
            dbg_tap("brcol", brcol, [brt])
            dbg_tap("wt_sm", wt_sm, [wtk])
            dbg_tap("wh_sm", wh_sm, [whk])
            dbg_tap("Khat", K_tm, [khk])
            dbg_tap("Vhat", Vh_tm, [vhk])
        trWT = dma(W_tm.rearrange("p (c t) -> p c t", c=NCC)[:, 0:NCO, :],
                   wt_sm[:, 3 * 128:NCC * 128],
                   waits=[wtk, khk, vhk, S.now(ST)], transpose=True)
        wtT = W_tm

        z1 = v(lambda e: e.memset(SkT[:], 0.0), waits=[S.now(ST)])
        z2 = v(lambda e: e.memset(Sv[:], 0.0))

        upK = upV = None
        up2 = {0: None, 1: None}
        snapKs = None
        for c in range(NCC):
            par = c % 2
            PK = PSF[par][:, 0:128]
            PV = PSF[2 + par][:, 0:128]
            oc = c - 3
            if oc >= 0:
                snapK = v(lambda e, _o=oc: e.tensor_copy(
                    Sk_all[:, _o * 128:(_o + 1) * 128], SkT[:]),
                    waits=[upK, z1])
                snapV = v(lambda e, _o=oc: e.tensor_copy(
                    Sv_all[:, _o * 128:(_o + 1) * 128], Sv[:]), waits=[upV, z2])
                snapKs = snapV
            mdk = tn(lambda e, _c=c, _p=PK: e.matmul(
                _p, wh_sm[:, _c * 128:(_c + 1) * 128],
                K_tm[:, _c * 128:(_c + 1) * 128], start=True, stop=True),
                waits=[whk, khk, up2[par], S.now(SV) if c == 0 else None])
            mdv = tn(lambda e, _c=c, _p=PV: e.matmul(
                _p, wh_sm[:, _c * 128:(_c + 1) * 128],
                Vh_tm[:, _c * 128:(_c + 1) * 128], start=True, stop=True),
                waits=[vhk, S.now(SV) if c == 0 else None])
            upK = v(lambda e, _c=c, _p=PK: e.scalar_tensor_tensor(
                SkT[:], SkT[:], dcols[:, _c:_c + 1], _p, OP.mult, OP.add),
                waits=[mdk, sdc, z1, snapKs])
            upV = v(lambda e, _c=c, _p=PV: e.scalar_tensor_tensor(
                Sv[:], Sv[:], dcols[:, _c:_c + 1], _p, OP.mult, OP.add),
                waits=[mdv, z2])
            up2[par] = upV

        prevp = {0: {}, 1: {}}

        def chunkB(oc, pdict):
            c = oc + 3
            gc = c + 1
            par = oc % 2
            tcol = slice(gc * 128, (gc + 1) * 128)
            ccol = slice(c * 128, (c + 1) * 128)
            Pqk = PSF[3 * par + 0][:, 0:128]
            Psc = PSF[3 * par + 1][:, 0:128]
            Ppt = PSF[3 * par + 2][:, 0:128]
            PB = PB6 if par == 0 else PB7
            trS, trA, trB = PB[:, 0:128], PB[:, 128:256], PB[:, 256:384]
            SkEd_, SkE_, QKm_ = SkEdP[par], SkEP[par], QKmP[par]
            stil_, a_e_, aTt_, aHt_, PTm_ = (stilP[par], a_eP[par], aTtP[par],
                                             aHtP[par], PTmP[par])
            nc0 = ncol[:, 4 * par + 0:4 * par + 1]
            nc1 = ncol[:, 4 * par + 1:4 * par + 2]
            nc2 = ncol[:, 4 * par + 2:4 * par + 3]
            ttr = tn(lambda e, _o=oc: e.transpose(
                trS, Sk_all[:, _o * 128:(_o + 1) * 128], IDENT),
                waits=[snapKs, upK, pdict.get("evSk")])
            yield
            eve = v(lambda e: e.tensor_copy(SkEd_[:], trS),
                    waits=[ttr, pdict.get("mt1")])
            pdict["evSk"] = eve
            yield
            mqk = tn(lambda e, _t=tcol: e.matmul(Pqk, kT_s[:, _t], qT_s[:, _t],
                                                 start=True, stop=True),
                     waits=[eve, qtok, pdict.get("evT1")])
            yield
            evQK = v(lambda e, _c=c: e.scalar_tensor_tensor(
                QKm_[:], Pqk, brS[:, _c:_c + 1], UTC, OP.mult, OP.mult),
                waits=[mqk, brst])
            yield
            ms1 = tn(lambda e, _c=c: e.matmul(
                Psc, QKm_[:], wt_sm[:, _c * 128:(_c + 1) * 128],
                start=True, stop=True), waits=[evQK, wtk, pdict.get("evS1")])
            yield
            mt1 = tn(lambda e, _t=tcol: e.matmul(Pqk, qT_s[:, _t], SkEd_[:],
                                                 start=True, stop=True),
                     waits=[eve, evQK])
            pdict["mt1"] = mt1
            yield
            evT1 = v(lambda e, _c=c: e.scalar_tensor_tensor(
                SkE_[:], Pqk, SCALE, E1[:, _c * 128:(_c + 1) * 128],
                OP.mult, OP.mult), waits=[mt1])
            pdict["evT1"] = evT1
            yield
            evS1 = v(lambda e, _c=c: e.tensor_mul(
                stil_[:], Psc, E2[:, _c * 128:(_c + 1) * 128]), waits=[ms1])
            pdict["evS1"] = evS1
            yield
            evSC = v(lambda e: e.tensor_add(stil_[:], stil_[:], SkE_[:]),
                     waits=[evS1, evT1])
            yield
            nm = v(lambda e: e.reduce_max(nc0, stil_[:], AX.X, negate=True),
                   waits=[evSC, pdict.get("ae")])
            yield
            ae = sc(lambda e: e.activation(a_e_[:], stil_[:], AF.Exp, bias=nc0),
                    waits=[nm])
            pdict["ae"] = ae
            yield
            sm = v(lambda e: e.reduce_sum(nc1, a_e_[:], AX.X), waits=[ae])
            yield
            ri = v(lambda e: e.reciprocal(nc2, nc1), waits=[sm])
            yield
            at = v(lambda e, _c=c: e.tensor_mul(
                aTt_[:], a_e_[:], E2[:, _c * 128:(_c + 1) * 128]),
                waits=[ae, pdict.get("mpA")])
            yield
            ah = v(lambda e, _c=c: e.tensor_mul(
                aHt_[:], a_e_[:], E1[:, _c * 128:(_c + 1) * 128]),
                waits=[pdict.get("mo2")])
            yield
            ttb = tn(lambda e: e.transpose(trA, aTt_[:], IDENT),
                     waits=[at, eve, pdict.get("evB")])
            yield
            evA = v(lambda e: e.tensor_copy(aTt_[:], trA), waits=[ttb])
            yield
            ttc = tn(lambda e: e.transpose(trB, aHt_[:], IDENT),
                     waits=[ah, evA])
            yield
            evB = v(lambda e: e.tensor_copy(aHt_[:], trB), waits=[ttc])
            pdict["evB"] = evB
            yield
            mp1 = tn(lambda e, _o=oc: e.matmul(
                Ppt[0:64, 0:64], wtT[:, _o * 128:_o * 128 + 64], aTt_[:, 0:64],
                start=True, stop=True), waits=[evA, trWT, pdict.get("evPT")])
            mp2 = tn(lambda e, _o=oc: e.matmul(
                Ppt[64:128, 64:128], wtT[:, _o * 128 + 64:_o * 128 + 128],
                aTt_[:, 64:128], start=True, stop=True))
            mp3 = tn(lambda e, _o=oc: e.matmul(
                Ppt[0:64, 64:128], wtT[:, _o * 128:_o * 128 + 64],
                aTt_[:, 64:128], start=True, stop=True))
            pdict["mpA"] = mp3
            yield
            evPT = v(lambda e, _c=c: e.scalar_tensor_tensor(
                PTm_[:], Ppt, brcol[:, _c:_c + 1], UTC, OP.mult, OP.mult),
                waits=[mp3])
            pdict["evPT"] = evPT
            yield
            mo1 = tn(lambda e, _c=c: e.matmul(
                Ppt, PTm_[:], V_tm[:, _c * 128:(_c + 1) * 128],
                start=True, stop=False), waits=[evPT])
            mo2 = tn(lambda e, _o=oc: e.matmul(
                Ppt, aHt_[:], Sv_all[:, _o * 128:(_o + 1) * 128],
                start=False, stop=True), waits=[evB, snapKs])
            pdict["mo2"] = mo2
            yield
            evO = v(lambda e, _o=oc: e.tensor_scalar(
                obuf[:, _o * 128:(_o + 1) * 128], Ppt, nc2, None, OP.mult),
                waits=[mo2, ri, spill_prev])
            pdict["evO"] = evO
            yield

        for co in range(0, NCO, 2):
            gens = [chunkB(co, prevp[0]), chunkB(co + 1, prevp[1])]
            live = [True, True]
            while any(live):
                for gi in (0, 1):
                    if live[gi]:
                        try:
                            next(gens[gi])
                        except StopIteration:
                            live[gi] = False
        evO = prevp[1].get("evO")
        evO2 = prevp[0].get("evO")
        spill_prev = dma(oSp[h * 128:(h + 1) * 128, :], obuf[:],
                         waits=[evO, evO2])
        if h == 0:
            dbg_tap("SkT_f", SkT, [upK])
            dbg_tap("Sv_f", Sv, [upV])
    ATTN_LD = [S.now(n) for n in LDS]

    srcs = [(W1T, 16, FFH), (W2T, 16, FFH), (W3T, 44, HID)]
    gb = barrier_toks()
    zg = v(lambda e: e.memset(gsum[:, 0:4], 0.0), waits=gb)
    gprev = zg
    for mi, (WSRC, nslab, wid) in enumerate(srcs):
        for i in range(nslab):
            tl = dma(fslab[:, 0:wid], WSRC[i * 128:(i + 1) * 128, :],
                     waits=[gprev] + (gb if (mi == 0 and i == 0) else []))
            ga = sc(lambda e, _w=wid: e.activation(
                fsl_a[:, 0:_w], fslab[:, 0:_w], AF.Abs,
                accum_out=gsum[:, 4:5]), waits=[tl, gprev])
            gprev = v(lambda e, _m=mi: e.tensor_add(
                gsum[:, _m:_m + 1], gsum[:, _m:_m + 1], gsum[:, 4:5]),
                waits=[ga])
    qprev = gprev
    qst0 = [None]
    for mi, (WSRC, nslab, wid) in enumerate(srcs):
        mt = tn(lambda e, _m=mi: e.matmul(
            PSF[2][0:1, _m:_m + 1], gsum[:, _m:_m + 1], ONE_F,
            start=True, stop=True), waits=[gprev, S.now(SV), S.now(SS)])
        cv_ = v(lambda e, _m=mi: e.tensor_copy(gtmp[0:1, _m:_m + 1],
                                               PSF[2][0:1, _m:_m + 1]),
                waits=[mt])
        mr = tn(lambda e, _m=mi: e.matmul(
            PSF[3][:, _m:_m + 1], crow_s[:], gtmp[0:1, _m:_m + 1],
            start=True, stop=True), waits=[cv_])
        tp = v(lambda e, _m=mi: e.tensor_scalar(
            gcolq[:, 0:1], PSF[3][:, _m:_m + 1], 0.5 / NELEM_W, 0.5e-8,
            OP.mult, OP.add), waits=[mr, qprev, S.now(SS)])
        tng = v(lambda e: e.tensor_scalar(gcolq[:, 1:2], gcolq[:, 0:1], -1.0,
                                          None, OP.mult), waits=[tp])
        qprev = tng
        qst = qst0[0]
        if mi == 0:
            dbg_tap("gcolq", gcolq, [tng])
            dbg_tap("gsum", gsum, [tng])
        for i in range(nslab):
            tl = dma(fslab[:, 0:wid], WSRC[i * 128:(i + 1) * 128, :],
                     waits=[qprev])
            s1 = sc(lambda e, _w=wid: e.activation(
                fsl_a[:, 0:_w], fslab[:, 0:_w], AF.Sign, bias=gcolq[:, 1:2]),
                waits=[tl, tng, qst])
            s2 = sc(lambda e, _w=wid: e.activation(
                fsl_b[:, 0:_w], fslab[:, 0:_w], AF.Sign, bias=gcolq[:, 0:1]),
                waits=[qst])
            qprev = v(lambda e, _w=wid: e.tensor_add(
                fsl_b[:, 0:_w], fsl_b[:, 0:_w], fsl_a[:, 0:_w]), waits=[s1, s2])
            qst = dma(q2w[mi][i * 128:(i + 1) * 128, :], fsl_b[:, 0:wid],
                      waits=[qprev])
            qst0[0] = qst
    QDONE = [S.now(n) for n in LDS]

    q2w1r = q2w[0].rearrange("(k p) c -> p k c", p=128)
    q2w2r = q2w[1].rearrange("(k p) c -> p k c", p=128)
    fb = barrier_toks()
    hprev = None
    hst = None
    for cs in range(44):
        tla = dma(q2a.rearrange("p (k c) -> p k c", k=16),
                  q2w1r[:, :, cs * 128:(cs + 1) * 128],
                  waits=QDONE + [hprev] + (fb if cs == 0 else []))
        tlb = dma(q2b.rearrange("p (k c) -> p k c", k=16),
                  q2w2r[:, :, cs * 128:(cs + 1) * 128])
        for tb in range(4):
            off = 512 + tb * 512
            p0 = PSF[0][:, 0:512]
            p1 = PSF[1][:, 0:512]
            pairs0 = [(q2a[:, k * 128:(k + 1) * 128],
                       XNT[:, k * T_IN + off:k * T_IN + off + 512])
                      for k in range(16)]
            pairs1 = [(q2b[:, k * 128:(k + 1) * 128],
                       XNT[:, k * T_IN + off:k * T_IN + off + 512])
                      for k in range(16)]
            m0 = mm_acc(p0, pairs0, [tla, hprev] + (fb if cs == 0 else []))
            m1 = mm_acc(p1, pairs1, [tlb])
            sg = sc(lambda e, _tb=tb, _p=p0: e.activation(
                sigx[:, _tb * 512:(_tb + 1) * 512], _p, AF.Sigmoid, scale=0.5),
                waits=[m0, hprev])
            hprev = v(lambda e, _tb=tb, _p=p1: e.scalar_tensor_tensor(
                hslab[:, _tb * 512:(_tb + 1) * 512], _p, 0.25,
                sigx[:, _tb * 512:(_tb + 1) * 512], OP.mult, OP.mult),
                waits=[m1, sg, hst])
        hst = dma(hidB[cs * 128:(cs + 1) * 128, :], hslab[:], waits=[hprev])
    F1_LD = [S.now(n) for n in LDS]

    oSpr = oSp.rearrange("(h p) (c d) -> p c h d", p=128, d=128)
    nb = barrier_toks() + F1_LD + ATTN_LD
    ntok = None
    gf_prev = None
    on2 = None
    for oc in range(NCO):
        tl = dma(onrow.rearrange("p (g d) -> p g d", g=H), oSpr[:, oc],
                 waits=[ntok] + (nb if oc == 0 else []))
        gts = []
        for cb in range(4):
            ps = PSF[cb][:, 0:512]
            mg = tn(lambda e, _o=oc, _cb=cb, _p=ps: e.matmul(
                _p, out2T[:, _o * 128:(_o + 1) * 128],
                wg2b[:, _cb * 512:(_cb + 1) * 512], start=True, stop=True),
                waits=[gf_prev, o2tok] + (nb if oc == 0 else []))
            gf = v(lambda e, _cb=cb, _p=ps: e.tensor_add(
                gate_f[:, _cb * 512:(_cb + 1) * 512], _p,
                bg2_s[:, _cb * 512:(_cb + 1) * 512]),
                waits=[mg] + ([on2] if on2 else []))
            gts.append(sc(lambda e, _cb=cb: e.activation(
                gatesig[:, _cb * 512:(_cb + 1) * 512],
                gate_f[:, _cb * 512:(_cb + 1) * 512], AF.Sigmoid), waits=[gf]))
        gf_prev = gts[-1]
        sq = v(lambda e: e.tensor_mul(sqn[:], onrow[:], onrow[:]), waits=[tl])
        rs2 = v(lambda e: e.reduce_sum(
            rstd16[:, 0:16], sqn.rearrange("p (g d) -> p g d", g=H), AX.X),
            waits=[sq, S.now(SS)])
        rs3 = v(lambda e: e.tensor_scalar(rstd16[:, 0:16], rstd16[:, 0:16],
                                          1.0 / 128.0, 1e-5, OP.mult, OP.add),
                waits=[rs2])
        rs4 = sc(lambda e: e.activation(rstd16[:, 0:16], rstd16[:, 0:16],
                                        AF.Ln), waits=[rs3])
        rs5 = sc(lambda e: e.activation(rstd16[:, 0:16], rstd16[:, 0:16],
                                        AF.Exp, scale=-0.5), waits=[rs4])
        on1 = v(lambda e: e.tensor_mul(onrow[:], onrow[:], gatesig[:]),
                waits=gts + [rs2, sq])
        rb3 = rstd16[:, 0:16].unsqueeze(2).broadcast_to([128, 16, 128])
        on2 = v(lambda e: e.tensor_tensor(
            onrow.rearrange("p (g d) -> p g d", g=H),
            onrow.rearrange("p (g d) -> p g d", g=H), rb3, OP.mult),
            waits=[rs5, on1])
        ntok = dma(onT.rearrange("p (j t) -> p j t", j=16)
                   [:, :, oc * 128:(oc + 1) * 128],
                   onrow[:], waits=[on2], transpose=True)
        if oc == 0:
            dbg_tap("gatesig", gatesig, [gts[-1]])
            dbg_tap("rstd16", rstd16, [rs5])
            dbg_tap("onrow0", onrow, [on2])
    NORM_LD = [S.now(n) for n in LDS]
    dbg_tap("onT", onT, NORM_LD)

    q2w3r = q2w[2].rearrange("(k p) c -> p k c", p=128)
    f2b = barrier_toks() + NORM_LD
    fprev = None
    wcast_prev = None
    yst = None
    for tq in range(4):
        tlh = dma(hidq.rearrange("p (k c) -> p k c", k=44),
                  hidB.rearrange("(k p) c -> p k c", p=128)
                  [:, :, tq * 512:(tq + 1) * 512],
                  waits=[fprev] + (f2b if tq == 0 else []))
        for cbk in range(4):
            tl3 = dma(w3cb.rearrange("p (k c) -> p k c", k=44),
                      q2w3r[:, :, cbk * 512:(cbk + 1) * 512],
                      waits=[fprev] + (f2b if tq == 0 and cbk == 0 else []))
            wtoks = []
            for j in range(16):
                tw = dma(wstage[:], WOp[j * 128:(j + 1) * 128,
                                        cbk * 512:(cbk + 1) * 512],
                         waits=[wcast_prev] + (f2b if tq == 0 and cbk == 0
                                               else []))
                wcast_prev = v(lambda e, _j=j: e.tensor_scalar(
                    wocb[:, _j * 512:(_j + 1) * 512], wstage[:], nw_s[:, 0:1],
                    None, OP.mult),
                    waits=[tw] + ([S.now(ST)] if j == 0 else []))
                wtoks.append(wcast_prev)
            for tc in range(4):
                co = tq * 4 + tc
                ps = PSF[tc % 2][:, 0:512]
                tx = dma(xstage[:], x_in[HALO + co * 128:HALO + (co + 1) * 128,
                                         cbk * 512:(cbk + 1) * 512],
                         waits=[fprev])
                pairs = [(hidq[:, k * 512 + tc * 128:k * 512 + tc * 128 + 128],
                          w3cb[:, k * 512:(k + 1) * 512]) for k in range(44)]
                pairs += [(onT[:, j * T_OWN + co * 128:j * T_OWN + co * 128 + 128],
                           wocb[:, j * 512:(j + 1) * 512]) for j in range(16)]
                mt = mm_acc(ps, pairs, [tlh, tl3, fprev] + wtoks)
                fprev = v(lambda e, _p=ps: e.tensor_add(outrow[:], _p,
                                                        xstage[:]),
                          waits=[mt, tx, yst])
                yst = dma(y[co * 128:(co + 1) * 128, cbk * 512:(cbk + 1) * 512],
                          outrow[:, 0:512], waits=[fprev])
    S.op("sync", lambda e: None, waits=[S.now(n) for n in LDS])

    from contextlib import ExitStack
    es = ExitStack()
    sems = {n: es.enter_context(nc.semaphore(n))
            for n in (["SV", "SS", "ST"] + ["LD%d" % i for i in range(8)])}
    with nc.Block() as blk:
        @blk.sync
        def _(eng):
            S.emit("sync", eng, sems)

        @blk.vector
        def _(eng):
            S.emit("vector", eng, sems)

        @blk.scalar
        def _(eng):
            S.emit("scalar", eng, sems)

        @blk.tensor
        def _(eng):
            S.emit("tensor", eng, sems)

        @blk.gpsimd
        def _(eng):
            S.emit("gpsimd", eng, sems)
    es.close()
    return nc


_NC_CACHE = None


def kernel(L0_attn_hidden_states, L0_attn_Wq, L0_attn_Wk, L0_attn_Wv, L0_attn_Ww,
           L0_attn_cq, L0_attn_ck, L0_attn_cv, L0_attn_Wf1, L0_attn_Wf2,
           L0_attn_Wb, L0_attn_Wg1, L0_attn_Wg2, L0_attn_bg2, L0_attn_norm_w,
           L0_attn_Wo, L0_ffn_w1, L0_ffn_w2, L0_ffn_w3):
    global _NC_CACHE
    x = np.asarray(L0_attn_hidden_states, dtype=np.float32)
    B, T, _ = x.shape
    assert (B, T) == (2, 8192)

    shared = {
        "Wq": np.ascontiguousarray(np.asarray(L0_attn_Wq, np.float32)),
        "Wk": np.ascontiguousarray(np.asarray(L0_attn_Wk, np.float32)),
        "Wv": np.ascontiguousarray(np.asarray(L0_attn_Wv, np.float32)),
        "Ww": np.ascontiguousarray(np.asarray(L0_attn_Ww, np.float32)),
        "Wo": np.ascontiguousarray(np.asarray(L0_attn_Wo, np.float32)),
        "Wf1": np.ascontiguousarray(np.asarray(L0_attn_Wf1, np.float32)),
        "Wf2": np.ascontiguousarray(np.asarray(L0_attn_Wf2, np.float32)),
        "Wb": np.ascontiguousarray(np.asarray(L0_attn_Wb, np.float32)),
        "Wg1": np.ascontiguousarray(np.asarray(L0_attn_Wg1, np.float32)),
        "Wg2": np.ascontiguousarray(np.asarray(L0_attn_Wg2, np.float32)),
        "cq": np.ascontiguousarray(np.asarray(L0_attn_cq, np.float32)),
        "ck": np.ascontiguousarray(np.asarray(L0_attn_ck, np.float32)),
        "cv": np.ascontiguousarray(np.asarray(L0_attn_cv, np.float32)),
        "w1T": np.ascontiguousarray(np.asarray(L0_ffn_w1, np.float32).T),
        "w2T": np.ascontiguousarray(np.asarray(L0_ffn_w2, np.float32).T),
        "w3T": np.ascontiguousarray(np.asarray(L0_ffn_w3, np.float32).T),
        "bg2r": np.ascontiguousarray(
            np.broadcast_to(np.asarray(L0_attn_bg2, np.float32), (128, HID))),
        "nw_col": np.ascontiguousarray(
            np.asarray(L0_attn_norm_w, np.float32).reshape(128, 1)),
        "cb": np.ascontiguousarray(np.concatenate(
            [np.eye(128, dtype=BFD),
             np.triu(np.ones((128, 128), dtype=BFD)),
             np.full((128, 128), 0.5, dtype=BFD)], axis=1)),
        "cf": np.ascontiguousarray(np.tile(
            np.array([1e-5, 1e-6, 1.0, 0.0, 0.0, 0.0, 0.0, 0.0],
                     np.float32), (128, 1))),
        "crow": np.ones((1, 128), np.float32),
    }

    in_maps = []
    for c in range(8):
        b, qi = c // 4, c % 4
        lo = qi * T_OWN
        if qi == 0:
            xs = np.concatenate([np.zeros((HALO, HID), np.float32),
                                 x[b, 0:T_OWN]], axis=0)
        else:
            xs = x[b, lo - HALO: lo + T_OWN]
        m = dict(shared)
        m["x_in"] = np.ascontiguousarray(xs)
        in_maps.append(m)

    if _NC_CACHE is None:
        _NC_CACHE = build_program(debug=bool(globals().get("_DBG_MODE")))
    res = run_bass_kernel_spmd(_NC_CACHE, in_maps, core_ids=list(range(8)))
    if globals().get("_DBG_MODE"):
        globals()["_DBG_RES"] = res

    out = np.empty((2, 8192, HID), np.float32)
    for c in range(8):
        b, qi = c // 4, c % 4
        out[b, qi * T_OWN:(qi + 1) * T_OWN] = res.results[c]["y"]
    return out


if __name__ == "__main__":
    nc = build_program()
    print("build ok, instructions:", nc.n_instructions())


# version 13
# speedup vs baseline: 1.0898x; 1.0440x over previous
import numpy as np
import ml_dtypes
import concourse.bass as bass
import concourse.mybir as mybir
from concourse.bass_utils import run_bass_kernel_spmd
from concourse.alu_op_type import AluOpType as OP

F32 = mybir.dt.float32
BF16 = mybir.dt.bfloat16
AF = mybir.ActivationFunctionType
AX = mybir.AxisListType
BFD = ml_dtypes.bfloat16

HID = 2048
H = 16
T_OWN = 2048
HALO = 512
T_IN = T_OWN + HALO
NCI = T_IN // 128
NCC = 19
NCO = 16
FFH = 5632
SCALE = 128.0 ** -0.5
NELEM_W = float(HID * FFH)


class Sch:
    ENGS = ("sync", "vector", "scalar", "tensor", "gpsimd")

    def __init__(self):
        self.ops = {e: [] for e in self.ENGS}
        self.cnt = {}
        self.seen = {e: {} for e in self.ENGS}

    def sem(self, name):
        self.cnt.setdefault(name, 0)
        return name

    DMA_SEMS = tuple("LD%d" % i for i in range(8))

    def op(self, eng, fn, waits=(), inc=None, amt=1):
        w = []
        for t in waits:
            if t is None:
                continue
            s, vv = t
            if s is None or vv is None or vv <= 0:
                continue
            if self.seen[eng].get(s, -1) >= vv:
                continue
            self.seen[eng][s] = vv
            w.append((s, vv))
        tgt = None
        if inc is not None:
            self.cnt[inc] += amt
            tgt = self.cnt[inc]
        self.ops[eng].append((w, fn, inc, amt))
        return (inc, tgt) if inc is not None else None

    def now(self, sem):
        return (sem, self.cnt.get(sem, 0))

    def emit(self, eng_name, eng, sems):
        for w, fn, inc, amt in self.ops[eng_name]:
            for s, vv in w:
                eng.wait_ge(sems[s], vv)
            ins = fn(eng)
            if inc is not None:
                if ins is None:
                    ins = eng.nop()
                ins.then_inc(sems[inc], amt)


def build_program(debug=False):
    nc = bass.Bass("TRN2", target_bir_lowering=False)

    def din(name, shape, dt=F32):
        return nc.dram_tensor(name, list(shape), dt, kind="ExternalInput").ap()

    x_in = din("x_in", (T_IN, HID))
    WQ = din("Wq", (HID, HID)); WK = din("Wk", (HID, HID))
    WV = din("Wv", (HID, HID)); WW = din("Ww", (HID, HID))
    WOp = din("Wo", (HID, HID))
    WF1 = din("Wf1", (HID, 128)); WF2 = din("Wf2", (128, HID))
    WB = din("Wb", (HID, H))
    WG1 = din("Wg1", (HID, 128)); WG2 = din("Wg2", (128, HID))
    CQ = din("cq", (HID, 4)); CK = din("ck", (HID, 4)); CV = din("cv", (HID, 4))
    W1T = din("w1T", (HID, FFH)); W2T = din("w2T", (HID, FFH))
    W3T = din("w3T", (FFH, HID))
    BG2R = din("bg2r", (128, HID))
    NWC = din("nw_col", (128, 1))
    CB = din("cb", (128, 3 * 128), BF16)
    CF = din("cf", (128, 8))
    CROW = din("crow", (1, 128))

    y = nc.dram_tensor("y", [T_OWN, HID], F32, kind="ExternalOutput").ap()

    def dint(name, shape, dt):
        return nc.dram_tensor(name, list(shape), dt, kind="Internal").ap()

    def dmaybe(name, shape, dt):
        kind = "ExternalOutput" if debug else "Internal"
        return nc.dram_tensor(name, list(shape), dt, kind=kind).ap()

    q2w = [dmaybe("q2w1", (HID, FFH), BF16), dmaybe("q2w2", (HID, FFH), BF16),
           dmaybe("q2w3", (FFH, HID), BF16)]
    hidB = dmaybe("hidB", (FFH, T_OWN), BF16)
    oSp = dmaybe("oSp", (H * 128, NCO * 128), BF16)

    DBG = {}

    def dbg_tap(name, ap, waits):
        if not debug:
            return
        if not isinstance(ap, bass.AP):
            ap = ap[:]
        t = nc.dram_tensor("dbg_" + name, [ap.shape[0], ap.shape[1]],
                           ap.dtype, kind="ExternalOutput").ap()
        DBG[name] = dma(t, ap, waits=waits)

    sbt = nc.alloc_sbuf_tensor

    cb_s = sbt("cb_s", [128, 3 * 128], BF16)
    IDENT = cb_s[:, 0:128]
    UTC = cb_s[:, 128:256]
    HALFM = cb_s[:, 256:384]
    cf_s = sbt("cf_s", [128, 8], F32)
    EPS_LN = cf_s[:, 0:1]; EPS_L2 = cf_s[:, 1:2]; ONE_F = cf_s[:, 2:3]
    crow_s = sbt("crow_s", [1, 128], F32)
    ones_bf = sbt("ones_bf", [128, 16], BF16)
    ONES_COL = ones_bf[:, 0:1]
    nw_s = sbt("nw_s", [128, 1], F32)
    bg2_s = sbt("bg2_s", [128, HID], BF16)
    wf2b = sbt("wf2b", [128, HID], BF16)
    wg2b = sbt("wg2b", [128, HID], BF16)
    cqall = sbt("cqall", [128, 3 * 64], F32)
    beta_s = sbt("beta_s", [128, NCI * H], F32)
    out1T = sbt("out1T", [128, T_IN], BF16)
    out2T = sbt("out2T", [128, T_OWN], BF16)
    lnst = sbt("lnst", [128, 8], F32)
    gcols = sbt("gcols", [128, 32], F32)
    dcols = sbt("dcols", [128, 32], F32)
    ercols = sbt("ercols", [128, 32], F32)
    sscol = sbt("sscol", [128, 48], F32)
    brcol = sbt("brcol", [128, 32], F32)
    brS = sbt("brS", [128, 32], F32)
    ncol = sbt("ncol", [128, 8], F32)
    rstd16 = sbt("rstd16", [128, 16], F32)
    gsum = sbt("gsum", [128, 8], F32)
    gcolq = sbt("gcolq", [128, 4], F32)
    gtmp = sbt("gtmp", [1, 8], F32)
    SkT = sbt("SkT", [128, 128], BF16)
    Sv = sbt("Sv", [128, 128], BF16)
    SkEP = [sbt("SkE%d" % i, [128, 128], BF16) for i in range(2)]
    SkEdP = [sbt("SkEd%d" % i, [128, 128], BF16) for i in range(2)]
    QKmP = [sbt("QKm%d" % i, [128, 128], BF16) for i in range(2)]
    stilP = [sbt("stil%d" % i, [128, 128], BF16) for i in range(2)]
    a_eP = [sbt("a_e%d" % i, [128, 128], BF16) for i in range(2)]
    aTtP = [sbt("aTt%d" % i, [128, 128], BF16) for i in range(2)]
    aHtP = [sbt("aHt%d" % i, [128, 128], BF16) for i in range(2)]
    PTmP = [sbt("PTm%d" % i, [128, 128], BF16) for i in range(2)]

    AR1 = sbt("AR1", [128, 43008], BF16)
    AR2 = sbt("AR2", [128, 47232], BF16)

    def carve(ar, off_bytes, cols, dt):
        assert off_bytes % 4 == 0
        c2 = off_bytes // 2
        if dt == BF16:
            assert (off_bytes + 2 * cols) <= 2 * ar.shape[1]
            return ar[:, c2:c2 + cols]
        assert (off_bytes + 4 * cols) <= 2 * ar.shape[1]
        return ar[:, c2:c2 + 2 * cols].bitcast(F32)

    XNT = carve(AR1, 0, 16 * T_IN, BF16)
    onT = carve(AR1, 0, 16 * T_OWN, BF16)
    wocb = carve(AR1, 65536, 16 * 512, BF16)
    wstage = carve(AR1, 81920, 512, F32)
    outrow = carve(AR1, 83968, 512, F32)

    xrow = carve(AR2, 0, HID, F32)
    sqrow = carve(AR2, 8192, HID, F32)
    xnrow = carve(AR2, 16384, HID, BF16)
    _off = [0]

    def hc(cols, dt):
        ap = carve(AR2, _off[0], cols, dt)
        _off[0] += cols * (2 if dt == BF16 else 4)
        _off[0] = (_off[0] + 63) & ~63
        return ap

    wcb = [hc(HID, BF16), hc(HID, BF16)]
    preconv = hc(T_IN + 4 + 540, BF16)
    projbuf = hc(T_IN, BF16)
    E1 = carve(AR2, 0, NCC * 128, BF16)
    E2 = carve(AR2, 4864, NCC * 128, BF16)
    ERb = carve(AR2, 9728, NCC * 128, BF16)
    expRr = carve(AR2, 14592, NCC * 128, BF16)
    kT_s = hc(T_IN, BF16)
    qT_s = hc(T_IN, BF16)
    K_tm = hc(NCC * 128, BF16)
    V_tm = hc(NCC * 128, BF16)
    Vh_tm = hc(NCC * 128, BF16)
    W_tm = hc(NCC * 128, BF16)
    g_sm = hc(NCC * 128, BF16)
    wt_sm = hc(NCC * 128, BF16)
    wh_sm = hc(NCC * 128, BF16)
    grow = hc(512, F32)
    zrow = hc(512, F32)
    obuf = carve(AR1, 81920, NCO * 128, BF16)
    Sk_all = hc(NCO * 128, BF16)
    Sv_all = hc(NCO * 128, BF16)
    assert _off[0] <= 94464, _off[0]
    fslab = carve(AR2, 0, FFH, F32)
    fsl_a = carve(AR2, 22528, FFH, BF16)
    fsl_b = carve(AR2, 33792, FFH, BF16)
    q2a = carve(AR2, 0, HID, BF16)
    q2b = carve(AR2, 4096, HID, BF16)
    sigx = carve(AR2, 8192, T_OWN, BF16)
    hslab = carve(AR2, 12288, T_OWN, BF16)
    onrow = carve(AR2, 0, HID, BF16)
    gate_f = carve(AR2, 4096, HID, F32)
    gatesig = carve(AR2, 12288, HID, BF16)
    sqn = carve(AR2, 16384, HID, BF16)
    hidq = carve(AR2, 0, 44 * 512, BF16)
    w3cb = carve(AR2, 45056, 44 * 512, BF16)
    xstage = carve(AR2, 90112, 512, F32)

    PSF = [nc.alloc_psum_tensor("PS%d" % i, [128, 512], F32) for i in range(6)]
    PB6 = nc.alloc_psum_tensor("PB6", [128, 1024], BF16)
    PB7 = nc.alloc_psum_tensor("PB7", [128, 1024], BF16)
    TRA = PB6[:, 0:128]
    TRB = PB7[:, 0:128]
    TRC = PB7[:, 512:640]

    S = Sch()
    LDS = [S.sem("LD%d" % i) for i in range(8)]
    GDS = [S.sem("GD%d" % i) for i in range(4)]
    SV = S.sem("SV"); SS = S.sem("SS"); ST = S.sem("ST")
    _rr = [0]
    _gr = [0]

    def dma(out_, in_, waits=(), transpose=False):
        if transpose:
            waits = list(waits) + [S.now(n) for n in LDS]
        sem = LDS[_rr[0] % 8]
        _rr[0] += 1
        if S.cnt[sem] > 0:
            S.op("sync", lambda e: None, waits=[S.now(sem)])

        def f(e, _o=out_, _i=in_, _t=transpose):
            return e.dma_start(out=_o, in_=_i, transpose=_t)
        tok = S.op("sync", f, waits=waits, inc=sem, amt=16)
        if transpose:
            for n in LDS:
                S.seen["sync"].pop(n, None)
            S.op("sync", lambda e: None, waits=[S.now(sem)])
        return tok

    def gdma(out_, in_, waits=()):
        sem = GDS[_gr[0] % 4]
        _gr[0] += 1
        if S.cnt[sem] > 0:
            S.op("gpsimd", lambda e: None, waits=[S.now(sem)])

        def f(e, _o=out_, _i=in_):
            return e.dma_start(out=_o, in_=_i)
        return S.op("gpsimd", f, waits=waits, inc=sem, amt=16)

    def v(fn, waits=()):
        return S.op("vector", fn, waits=waits, inc=SV)

    def sc(fn, waits=()):
        return S.op("scalar", fn, waits=waits, inc=SS)

    def tn(fn, waits=()):
        return S.op("tensor", fn, waits=waits, inc=ST)

    def barrier_toks():
        return ([S.now(SV), S.now(SS), S.now(ST)]
                + [S.now(n) for n in LDS] + [S.now(n) for n in GDS])

    def mm_acc(ps_ap, pairs, waits_first):
        tok = None
        n = len(pairs)
        for k, (lt, rh) in enumerate(pairs):
            tok = tn(lambda e, _l=lt, _r=rh, _s=(k == 0), _e=(k == n - 1),
                     _p=ps_ap: e.matmul(_p, _l, _r, start=_s, stop=_e),
                     waits=waits_first if k == 0 else ())
        return tok

    dma(cb_s[:], CB)
    dma(cf_s[:], CF)
    dma(crow_s[:], CROW)
    dma(nw_s[:], NWC)
    v(lambda e: e.memset(ones_bf[:], 1.0))
    v(lambda e: e.memset(PSF[5][64:128, 0:64], 0.0))
    v(lambda e: e.memset(PSF[2][64:128, 0:64], 0.0))
    bgtok = gdma(bg2_s[:], BG2R)
    dma(cqall.rearrange("p (s k w) -> p s k w", s=3, w=4)[:, 0],
        CQ.rearrange("(k p) w -> p k w", p=128))
    dma(cqall.rearrange("p (s k w) -> p s k w", s=3, w=4)[:, 1],
        CK.rearrange("(k p) w -> p k w", p=128))
    dma(cqall.rearrange("p (s k w) -> p s k w", s=3, w=4)[:, 2],
        CV.rearrange("(k p) w -> p k w", p=128))
    wf2tok = gdma(wf2b[:], WF2)
    wg2tok = gdma(wg2b[:], WG2)

    prev_used = wg2tok
    ttok = None
    for i in range(NCI):
        tl = dma(xrow[:], x_in[i * 128:(i + 1) * 128, :], waits=[prev_used])
        t1 = v(lambda e: e.reduce_sum(lnst[:, 0:1], xrow[:], AX.X), waits=[tl])
        t2 = sc(lambda e: e.activation(sqrow[:], xrow[:], AF.Square,
                                       accum_out=lnst[:, 1:2]), waits=[tl])
        t3 = v(lambda e: e.tensor_scalar(lnst[:, 2:3], lnst[:, 0:1], 1.0 / HID,
                                         None, OP.mult), waits=[t1])
        t4 = v(lambda e: e.tensor_scalar(lnst[:, 3:4], lnst[:, 1:2], 1.0 / HID,
                                         None, OP.mult), waits=[t2])
        t5 = v(lambda e: e.tensor_mul(lnst[:, 4:5], lnst[:, 2:3], lnst[:, 2:3]),
               waits=[t3])
        t6 = v(lambda e: e.tensor_sub(lnst[:, 5:6], lnst[:, 3:4], lnst[:, 4:5]),
               waits=[t4, t5])
        t7 = sc(lambda e: e.activation(lnst[:, 6:7], lnst[:, 5:6], AF.Ln,
                                       bias=EPS_LN), waits=[t6])
        t8 = sc(lambda e: e.activation(lnst[:, 7:8], lnst[:, 6:7], AF.Exp,
                                       scale=-0.5), waits=[t7])
        t9 = v(lambda e: e.tensor_scalar(xnrow[:], xrow[:], lnst[:, 2:3],
                                         lnst[:, 7:8], OP.subtract, OP.mult),
               waits=[t8, t3, ttok])
        ttok = dma(XNT.rearrange("p (k t) -> p k t", k=16)
                   [:, :, i * 128:(i + 1) * 128],
                   xnrow[:], waits=[t9], transpose=True)
        prev_used = t9
    XN_RDY = ttok
    dbg_tap("XNT", XNT, [XN_RDY])

    cwf = gdma(wcb[0].rearrange("p (k c) -> p k c", k=16),
               WF1.rearrange("(k p) c -> p k c", p=128), waits=[XN_RDY])
    cwg = gdma(wcb[1].rearrange("p (k c) -> p k c", k=16),
               WG1.rearrange("(k p) c -> p k c", p=128), waits=[XN_RDY])
    evtoks = {}
    for tb in range(5):
        ps = PSF[tb % 2][:, 0:512]
        pairs = [(wcb[0][:, k * 128:(k + 1) * 128],
                  XNT[:, k * T_IN + tb * 512:k * T_IN + tb * 512 + 512])
                 for k in range(16)]
        mt = mm_acc(ps, pairs, [cwf, XN_RDY, evtoks.get(tb - 2)])
        evtoks[tb] = v(lambda e, _tb=tb, _ps=ps: e.tensor_copy(
            out1T[:, _tb * 512:(_tb + 1) * 512], _ps), waits=[mt])
    o1tok = S.now(SV)
    f1mm = mt
    evtoks = {}
    for tb in range(4):
        ps = PSF[2 + tb % 2][:, 0:512]
        off = 512 + tb * 512
        pairs = [(wcb[1][:, k * 128:(k + 1) * 128],
                  XNT[:, k * T_IN + off:k * T_IN + off + 512]) for k in range(16)]
        mt = mm_acc(ps, pairs, [cwg, evtoks.get(tb - 2)])
        evtoks[tb] = v(lambda e, _tb=tb, _ps=ps: e.tensor_copy(
            out2T[:, _tb * 512:(_tb + 1) * 512], _ps), waits=[mt])
    o2tok = S.now(SV)
    cwb = gdma(wcb[0].rearrange("p (k c) -> p k c", c=16)[:, 0:16, :],
               WB.rearrange("(k p) c -> p k c", p=128), waits=[f1mm])
    bprev = None
    for i in range(NCI):
        ps = PSF[4][:, 0:16]
        pairs = [(XNT[:, k * T_IN + i * 128:k * T_IN + i * 128 + 128],
                  wcb[0][:, k * 16:(k + 1) * 16]) for k in range(16)]
        mt = mm_acc(ps, pairs, [cwb, bprev])
        bprev = sc(lambda e, _i=i, _ps=ps: e.activation(
            beta_s[:, _i * 16:(_i + 1) * 16], _ps, AF.Sigmoid), waits=[mt])
    beta_tok = bprev
    dbg_tap("out1T", out1T, [o1tok])
    dbg_tap("out2T", out2T, [o2tok])
    dbg_tap("beta", beta_s, [beta_tok])

    beta3 = beta_s.rearrange("p (c h) -> p c h", h=H)
    zpad = v(lambda e: e.memset(preconv[:, 0:4], 0.0), waits=[beta_tok])

    TRREG = [(PB6, 0), (PB7, 0), (PB6, 128), (PB7, 128)]
    trst = {"rot": 0, "ev": {}}

    def pe_transpose_seq(src, n, dst, waits_first):
        ev = None
        for i in range(n):
            bank, off = TRREG[trst["rot"] % 4]
            reg = bank[:, off:off + 128]
            w = [trst["ev"].get(trst["rot"] - 2)]
            if i == 0:
                w += list(waits_first)
            tt = tn(lambda e, _s=src[:, 128 + i * 128:128 + (i + 1) * 128],
                    _r=reg: e.transpose(_r, _s, IDENT), waits=w)
            ev = v(lambda e, _i=i, _r=reg, _d=dst: e.tensor_copy(
                _d[:, _i * 128:(_i + 1) * 128], _r), waits=[tt])
            trst["ev"][trst["rot"]] = ev
            trst["rot"] += 1
        return ev

    WSEQ = [(WW, 2), (WV, 2), (WK, 1), (WQ, 0)]
    PDST = [projbuf, projbuf, kT_s, qT_s]
    PTM = [W_tm, V_tm, K_tm, None]

    GRPS = [(0, 4), (4, 4), (8, 4), (12, 4), (16, 3)]
    spill_prev = None
    for h in range(H):
        hb = barrier_toks()
        ldw = [None] * 4
        for j in (0, 1):
            ldw[j] = gdma(wcb[j].rearrange("p (k c) -> p k c", k=16),
                          WSEQ[j][0].rearrange("(k p) c -> p k c", p=128)
                          [:, :, h * 128:(h + 1) * 128], waits=hb)
        trtok = [None] * 4
        silu_t = [None] * 4
        prev_tap3 = None
        for j in range(4):
            buf = wcb[j % 2]
            evs = {}
            mt = None
            for tb in range(5):
                ps = PSF[4 + tb % 2][:, 0:512]
                pairs = [(buf[:, k * 128:(k + 1) * 128],
                          XNT[:, k * T_IN + tb * 512:k * T_IN + tb * 512 + 512])
                         for k in range(16)]
                mt = mm_acc(ps, pairs,
                            [ldw[j], evs.get(tb - 2)]
                            + (hb if (h == 0 and j == 0 and tb == 0) else []))
                evs[tb] = v(lambda e, _tb=tb, _ps=ps: e.tensor_copy(
                    preconv[:, 4 + _tb * 512:4 + (_tb + 1) * 512], _ps),
                    waits=[mt] + ([prev_tap3] if tb == 0 else []))
            if j + 2 < 4:
                ldw[j + 2] = gdma(wcb[j % 2].rearrange("p (k c) -> p k c", k=16),
                                  WSEQ[j + 2][0]
                                  .rearrange("(k p) c -> p k c", p=128)
                                  [:, :, h * 128:(h + 1) * 128], waits=[mt])
            base = WSEQ[j][1] * 64 + h * 4
            cc = [cqall[:, base + i:base + i + 1] for i in range(4)]
            dest = PDST[j]
            t = v(lambda e, _d=dest: e.tensor_scalar(
                _d[:], preconv[:, 1:1 + T_IN], cc[0], None, OP.mult),
                waits=[evs[4], zpad, S.now(ST)])
            for i in (1, 2, 3):
                t = v(lambda e, _i=i, _d=dest: e.scalar_tensor_tensor(
                    _d[:], preconv[:, 1 + _i:1 + _i + T_IN], cc[_i], _d[:],
                    OP.mult, OP.add), waits=[t])
            prev_tap3 = t
            silu_t[j] = sc(lambda e, _d=dest: e.activation(_d[:], _d[:],
                                                           AF.Silu), waits=[t])
            if PTM[j] is not None:
                trtok[j] = pe_transpose_seq(
                    dest, NCC, PTM[j],
                    [silu_t[j]] + ([S.now(SV)] if j == 0 else []))
        qtok = silu_t[3]
        trW, trV, trK = trtok[0], trtok[1], trtok[2]
        ge_prev = None
        gevs = []
        for tb in range(5):
            ps = PSF[2 + tb % 2][:, 0:512]
            mt = mm_acc(ps, [(wf2b[:, h * 128:(h + 1) * 128],
                              out1T[:, tb * 512:(tb + 1) * 512])],
                        [wf2tok, S.now(SV), S.now(SS)])
            e1 = sc(lambda e, _ps=ps: e.activation(zrow[:], _ps, AF.Abs),
                    waits=[mt, ge_prev])
            e2 = sc(lambda e: e.activation(grow[:], zrow[:], AF.Exp, scale=-1.0),
                    waits=[e1])
            e3 = sc(lambda e: e.activation(grow[:], grow[:], AF.Ln, bias=ONE_F),
                    waits=[e2])
            e4 = v(lambda e, _tb=tb, _ps=ps: e.scalar_tensor_tensor(
                projbuf[:, _tb * 512:(_tb + 1) * 512], _ps, 0.0, grow[:],
                OP.min, OP.subtract), waits=[e3, S.now(ST)])
            ge_prev = e4
            gevs.append(e4)
        trG = pe_transpose_seq(projbuf, NCC, g_sm, gevs)

        dtoks = []
        cp_prev = zz_prev = se1_prev = se3_prev = se4_prev = None
        for g0, gn in GRPS:
            gw = gn * 128
            mg = None
            for j in range(gn):
                mg = tn(lambda e, _j=j, _c=g0 + j: e.matmul(
                    PSF[0][:, _j * 128:(_j + 1) * 128], UTC,
                    g_sm[:, _c * 128:(_c + 1) * 128], start=True, stop=True),
                    waits=[trG, cp_prev, se1_prev, S.now(SV)] if j == 0 else ())
            mr = None
            for j in range(gn):
                mr = tn(lambda e, _j=j, _c=g0 + j: e.matmul(
                    PSF[1][:, _j * 128:(_j + 1) * 128], HALFM,
                    g_sm[:, _c * 128:(_c + 1) * 128], start=True, stop=True),
                    waits=[zz_prev, se4_prev] if j == 0 else ())
            cp = v(lambda e, _w=gw: e.tensor_copy(grow[:, 0:_w], PSF[0][:, 0:_w]),
                   waits=[mg, S.now(SS)])
            se1 = sc(lambda e, _w=gw, _o=g0 * 128: e.activation(
                E1[:, _o:_o + _w], PSF[0][:, 0:_w], AF.Exp),
                waits=[mg, S.now(SV)])
            zz = v(lambda e, _w=gw: e.tensor_sub(zrow[:, 0:_w], PSF[1][:, 0:_w],
                                                 grow[:, 0:_w]),
                   waits=[cp, mr, se3_prev])
            zh = v(lambda e, _w=gw: e.tensor_add(grow[:, 0:_w], zrow[:, 0:_w],
                                                 PSF[1][:, 0:_w]),
                   waits=[zz, se4_prev, se1])
            zc = v(lambda e, _w=gw: e.tensor_scalar(zrow[:, 0:_w], zrow[:, 0:_w],
                                                    80.0, -80.0, OP.min, OP.max),
                   waits=[zh])
            se2 = sc(lambda e, _w=gw, _o=g0 * 128: e.activation(
                E2[:, _o:_o + _w], zrow[:, 0:_w], AF.Exp, scale=-1.0), waits=[zc])
            se3 = sc(lambda e, _w=gw, _o=g0 * 128: e.activation(
                ERb[:, _o:_o + _w], zrow[:, 0:_w], AF.Exp))
            se4 = sc(lambda e, _w=gw, _o=g0 * 128: e.activation(
                expRr[:, _o:_o + _w], grow[:, 0:_w], AF.Exp), waits=[zh])
            cp_prev, zz_prev, se1_prev, se3_prev, se4_prev = cp, zc, se1, se3, se4
            dtoks += [se1, se2, se3, se4]
        mgc = None
        for c in range(NCC):
            mgc = tn(lambda e, _c=c: e.matmul(PSF[2][:, _c:_c + 1],
                                              g_sm[:, _c * 128:(_c + 1) * 128],
                                              ONES_COL, start=True, stop=True),
                     waits=[trG, S.now(SV), S.now(SS)] if c == 0 else ())
        cgc = v(lambda e: e.tensor_copy(gcols[:, 0:NCC], PSF[2][:, 0:NCC]),
                waits=[mgc, S.now(SS)])
        sdc = sc(lambda e: e.activation(dcols[:, 0:NCC], gcols[:, 0:NCC], AF.Exp),
                 waits=[cgc, S.now(SV)])
        sq1 = v(lambda e: e.tensor_mul(wt_sm[:], W_tm[:], W_tm[:]),
                waits=[trW, S.now(ST)])
        l2b = v(lambda e: e.reduce_sum(
            sscol[:, 0:NCC], wt_sm.rearrange("p (c m) -> p c m", c=NCC), AX.X),
            waits=[sq1, S.now(SS)])
        l2c = sc(lambda e: e.activation(sscol[:, 20:20 + NCC], sscol[:, 0:NCC],
                                        AF.Ln, bias=EPS_L2), waits=[l2b])
        l2d = sc(lambda e: e.activation(sscol[:, 20:20 + NCC],
                                        sscol[:, 20:20 + NCC], AF.Exp,
                                        scale=-0.5), waits=[l2c])
        brt = v(lambda e, _h=h: e.tensor_mul(brcol[:, 0:NCC],
                                             sscol[:, 20:20 + NCC],
                                             beta3[:, 1:1 + NCC, _h]),
                waits=[l2d, beta_tok, S.now(ST)])
        brst = v(lambda e: e.tensor_scalar(brS[:, 0:NCC], brcol[:, 0:NCC], SCALE,
                                           None, OP.mult), waits=[brt])
        wtk = v(lambda e: e.tensor_mul(wt_sm[:], W_tm[:], ERb[:]),
                waits=[l2b] + dtoks)
        whk = v(lambda e: e.tensor_mul(wh_sm[:], W_tm[:], expRr[:]),
                waits=[wtk])
        br3 = brcol[:, 0:NCC].unsqueeze(2).broadcast_to([128, NCC, 128])
        khk = v(lambda e: e.tensor_tensor(
            K_tm.rearrange("p (c t) -> p c t", c=NCC),
            K_tm.rearrange("p (c t) -> p c t", c=NCC), br3, OP.mult),
            waits=[trK, brt])
        vhk = v(lambda e: e.tensor_tensor(
            Vh_tm.rearrange("p (c t) -> p c t", c=NCC),
            V_tm.rearrange("p (c t) -> p c t", c=NCC), br3, OP.mult),
            waits=[trV, brt])
        if h == 0:
            dbg_tap("kT", kT_s, [S.now(SS)])
            dbg_tap("qT", qT_s, [qtok])
            dbg_tap("W_tm", W_tm, [trW])
            dbg_tap("V_tm", V_tm, [trV])
            dbg_tap("g_sm", g_sm, [trG])
            dbg_tap("E1", E1, dtoks)
            dbg_tap("E2", E2, dtoks)
            dbg_tap("ERb", ERb, dtoks)
            dbg_tap("expRr", expRr, dtoks)
            dbg_tap("gcols", gcols, [cgc])
            dbg_tap("dcols", dcols, [sdc])
            dbg_tap("sscol", sscol, [l2d])
            dbg_tap("brcol", brcol, [brt])
            dbg_tap("wt_sm", wt_sm, [wtk])
            dbg_tap("wh_sm", wh_sm, [whk])
            dbg_tap("Khat", K_tm, [khk])
            dbg_tap("Vhat", Vh_tm, [vhk])
        trWT = dma(W_tm.rearrange("p (c t) -> p c t", c=NCC)[:, 0:NCO, :],
                   wt_sm[:, 3 * 128:NCC * 128],
                   waits=[wtk, khk, vhk, S.now(ST)], transpose=True)
        wtT = W_tm

        z1 = v(lambda e: e.memset(SkT[:], 0.0), waits=[S.now(ST)])
        z2 = v(lambda e: e.memset(Sv[:], 0.0))

        upK = upV = None
        up2 = {0: None, 1: None}
        snapKs = None
        for c in range(NCC):
            par = c % 2
            PK = PSF[par][:, 0:128]
            PV = PSF[2 + par][:, 0:128]
            oc = c - 3
            if oc >= 0:
                snapK = v(lambda e, _o=oc: e.tensor_copy(
                    Sk_all[:, _o * 128:(_o + 1) * 128], SkT[:]),
                    waits=[upK, z1])
                snapV = v(lambda e, _o=oc: e.tensor_copy(
                    Sv_all[:, _o * 128:(_o + 1) * 128], Sv[:]), waits=[upV, z2])
                snapKs = snapV
            mdk = tn(lambda e, _c=c, _p=PK: e.matmul(
                _p, wh_sm[:, _c * 128:(_c + 1) * 128],
                K_tm[:, _c * 128:(_c + 1) * 128], start=True, stop=True),
                waits=[whk, khk, up2[par], S.now(SV) if c == 0 else None])
            mdv = tn(lambda e, _c=c, _p=PV: e.matmul(
                _p, wh_sm[:, _c * 128:(_c + 1) * 128],
                Vh_tm[:, _c * 128:(_c + 1) * 128], start=True, stop=True),
                waits=[vhk, S.now(SV) if c == 0 else None])
            upK = v(lambda e, _c=c, _p=PK: e.scalar_tensor_tensor(
                SkT[:], SkT[:], dcols[:, _c:_c + 1], _p, OP.mult, OP.add),
                waits=[mdk, sdc, z1, snapKs])
            upV = v(lambda e, _c=c, _p=PV: e.scalar_tensor_tensor(
                Sv[:], Sv[:], dcols[:, _c:_c + 1], _p, OP.mult, OP.add),
                waits=[mdv, z2])
            up2[par] = upV

        prevp = {0: {}, 1: {}}

        def chunkB(oc, pdict):
            c = oc + 3
            gc = c + 1
            par = oc % 2
            tcol = slice(gc * 128, (gc + 1) * 128)
            ccol = slice(c * 128, (c + 1) * 128)
            Pqk = PSF[3 * par + 0][:, 0:128]
            Psc = PSF[3 * par + 1][:, 0:128]
            Ppt = PSF[3 * par + 2][:, 0:128]
            PB = PB6 if par == 0 else PB7
            trS, trA, trB = PB[:, 0:128], PB[:, 128:256], PB[:, 256:384]
            SkEd_, SkE_, QKm_ = SkEdP[par], SkEP[par], QKmP[par]
            stil_, a_e_, aTt_, aHt_, PTm_ = (stilP[par], a_eP[par], aTtP[par],
                                             aHtP[par], PTmP[par])
            nc0 = ncol[:, 4 * par + 0:4 * par + 1]
            nc1 = ncol[:, 4 * par + 1:4 * par + 2]
            nc2 = ncol[:, 4 * par + 2:4 * par + 3]
            ttr = tn(lambda e, _o=oc: e.transpose(
                trS, Sk_all[:, _o * 128:(_o + 1) * 128], IDENT),
                waits=[snapKs, upK, pdict.get("evSk")])
            yield
            eve = v(lambda e: e.tensor_copy(SkEd_[:], trS),
                    waits=[ttr, pdict.get("mt1")])
            pdict["evSk"] = eve
            yield
            mqk = tn(lambda e, _t=tcol: e.matmul(Pqk, kT_s[:, _t], qT_s[:, _t],
                                                 start=True, stop=True),
                     waits=[eve, qtok, pdict.get("evT1")])
            yield
            evQK = v(lambda e, _c=c: e.scalar_tensor_tensor(
                QKm_[:], Pqk, brS[:, _c:_c + 1], UTC, OP.mult, OP.mult),
                waits=[mqk, brst])
            yield
            ms1 = tn(lambda e, _c=c: e.matmul(
                Psc, QKm_[:], wt_sm[:, _c * 128:(_c + 1) * 128],
                start=True, stop=True), waits=[evQK, wtk, pdict.get("evS1")])
            yield
            mt1 = tn(lambda e, _t=tcol: e.matmul(Pqk, qT_s[:, _t], SkEd_[:],
                                                 start=True, stop=True),
                     waits=[eve, evQK])
            pdict["mt1"] = mt1
            yield
            evT1 = v(lambda e, _c=c: e.scalar_tensor_tensor(
                SkE_[:], Pqk, SCALE, E1[:, _c * 128:(_c + 1) * 128],
                OP.mult, OP.mult), waits=[mt1])
            pdict["evT1"] = evT1
            yield
            evS1 = v(lambda e, _c=c: e.tensor_mul(
                stil_[:], Psc, E2[:, _c * 128:(_c + 1) * 128]), waits=[ms1])
            pdict["evS1"] = evS1
            yield
            evSC = v(lambda e: e.tensor_add(stil_[:], stil_[:], SkE_[:]),
                     waits=[evS1, evT1])
            yield
            nm = v(lambda e: e.reduce_max(nc0, stil_[:], AX.X, negate=True),
                   waits=[evSC, pdict.get("ae")])
            yield
            ae = sc(lambda e: e.activation(a_e_[:], stil_[:], AF.Exp, bias=nc0),
                    waits=[nm])
            pdict["ae"] = ae
            yield
            sm = v(lambda e: e.reduce_sum(nc1, a_e_[:], AX.X), waits=[ae])
            yield
            ri = v(lambda e: e.reciprocal(nc2, nc1), waits=[sm])
            yield
            at = v(lambda e, _c=c: e.tensor_mul(
                aTt_[:], a_e_[:], E2[:, _c * 128:(_c + 1) * 128]),
                waits=[ae, pdict.get("mpA")])
            yield
            ah = v(lambda e, _c=c: e.tensor_mul(
                aHt_[:], a_e_[:], E1[:, _c * 128:(_c + 1) * 128]),
                waits=[pdict.get("mo2")])
            yield
            ttb = tn(lambda e: e.transpose(trA, aTt_[:], IDENT),
                     waits=[at, eve, pdict.get("evB")])
            yield
            evA = v(lambda e: e.tensor_copy(aTt_[:], trA), waits=[ttb])
            yield
            ttc = tn(lambda e: e.transpose(trB, aHt_[:], IDENT),
                     waits=[ah, evA])
            yield
            evB = v(lambda e: e.tensor_copy(aHt_[:], trB), waits=[ttc])
            pdict["evB"] = evB
            yield
            mp1 = tn(lambda e, _o=oc: e.matmul(
                Ppt[0:64, 0:64], wtT[:, _o * 128:_o * 128 + 64], aTt_[:, 0:64],
                start=True, stop=True), waits=[evA, trWT, pdict.get("evPT")])
            mp2 = tn(lambda e, _o=oc: e.matmul(
                Ppt[64:128, 64:128], wtT[:, _o * 128 + 64:_o * 128 + 128],
                aTt_[:, 64:128], start=True, stop=True))
            mp3 = tn(lambda e, _o=oc: e.matmul(
                Ppt[0:64, 64:128], wtT[:, _o * 128:_o * 128 + 64],
                aTt_[:, 64:128], start=True, stop=True))
            pdict["mpA"] = mp3
            yield
            evPT = v(lambda e, _c=c: e.scalar_tensor_tensor(
                PTm_[:], Ppt, brcol[:, _c:_c + 1], UTC, OP.mult, OP.mult),
                waits=[mp3])
            pdict["evPT"] = evPT
            yield
            mo1 = tn(lambda e, _c=c: e.matmul(
                Ppt, PTm_[:], V_tm[:, _c * 128:(_c + 1) * 128],
                start=True, stop=False), waits=[evPT])
            mo2 = tn(lambda e, _o=oc: e.matmul(
                Ppt, aHt_[:], Sv_all[:, _o * 128:(_o + 1) * 128],
                start=False, stop=True), waits=[evB, snapKs])
            pdict["mo2"] = mo2
            yield
            evO = v(lambda e, _o=oc: e.tensor_scalar(
                obuf[:, _o * 128:(_o + 1) * 128], Ppt, nc2, None, OP.mult),
                waits=[mo2, ri, spill_prev])
            pdict["evO"] = evO
            yield

        for co in range(0, NCO, 2):
            gens = [chunkB(co, prevp[0]), chunkB(co + 1, prevp[1])]
            live = [True, True]
            while any(live):
                for gi in (0, 1):
                    if live[gi]:
                        try:
                            next(gens[gi])
                        except StopIteration:
                            live[gi] = False
        evO = prevp[1].get("evO")
        evO2 = prevp[0].get("evO")
        spill_prev = dma(oSp[h * 128:(h + 1) * 128, :], obuf[:],
                         waits=[evO, evO2])
        if h == 0:
            dbg_tap("SkT_f", SkT, [upK])
            dbg_tap("Sv_f", Sv, [upV])
    ATTN_LD = [S.now(n) for n in LDS]

    srcs = [(W1T, 16, FFH), (W2T, 16, FFH), (W3T, 44, HID)]
    gb = barrier_toks()
    zg = v(lambda e: e.memset(gsum[:, 0:4], 0.0), waits=gb)
    gprev = zg
    for mi, (WSRC, nslab, wid) in enumerate(srcs):
        for i in range(nslab):
            tl = dma(fslab[:, 0:wid], WSRC[i * 128:(i + 1) * 128, :],
                     waits=[gprev] + (gb if (mi == 0 and i == 0) else []))
            ga = sc(lambda e, _w=wid: e.activation(
                fsl_a[:, 0:_w], fslab[:, 0:_w], AF.Abs,
                accum_out=gsum[:, 4:5]), waits=[tl, gprev])
            gprev = v(lambda e, _m=mi: e.tensor_add(
                gsum[:, _m:_m + 1], gsum[:, _m:_m + 1], gsum[:, 4:5]),
                waits=[ga])
    qprev = gprev
    qst0 = [None]
    for mi, (WSRC, nslab, wid) in enumerate(srcs):
        mt = tn(lambda e, _m=mi: e.matmul(
            PSF[2][0:1, _m:_m + 1], gsum[:, _m:_m + 1], ONE_F,
            start=True, stop=True), waits=[gprev, S.now(SV), S.now(SS)])
        cv_ = v(lambda e, _m=mi: e.tensor_copy(gtmp[0:1, _m:_m + 1],
                                               PSF[2][0:1, _m:_m + 1]),
                waits=[mt])
        mr = tn(lambda e, _m=mi: e.matmul(
            PSF[3][:, _m:_m + 1], crow_s[:], gtmp[0:1, _m:_m + 1],
            start=True, stop=True), waits=[cv_])
        tp = v(lambda e, _m=mi: e.tensor_scalar(
            gcolq[:, 0:1], PSF[3][:, _m:_m + 1], 0.5 / NELEM_W, 0.5e-8,
            OP.mult, OP.add), waits=[mr, qprev, S.now(SS)])
        tng = v(lambda e: e.tensor_scalar(gcolq[:, 1:2], gcolq[:, 0:1], -1.0,
                                          None, OP.mult), waits=[tp])
        qprev = tng
        qst = qst0[0]
        if mi == 0:
            dbg_tap("gcolq", gcolq, [tng])
            dbg_tap("gsum", gsum, [tng])
        for i in range(nslab):
            tl = dma(fslab[:, 0:wid], WSRC[i * 128:(i + 1) * 128, :],
                     waits=[qprev])
            s1 = sc(lambda e, _w=wid: e.activation(
                fsl_a[:, 0:_w], fslab[:, 0:_w], AF.Sign, bias=gcolq[:, 1:2]),
                waits=[tl, tng, qst])
            s2 = sc(lambda e, _w=wid: e.activation(
                fsl_b[:, 0:_w], fslab[:, 0:_w], AF.Sign, bias=gcolq[:, 0:1]),
                waits=[qst])
            qprev = v(lambda e, _w=wid: e.tensor_add(
                fsl_b[:, 0:_w], fsl_b[:, 0:_w], fsl_a[:, 0:_w]), waits=[s1, s2])
            qst = dma(q2w[mi][i * 128:(i + 1) * 128, :], fsl_b[:, 0:wid],
                      waits=[qprev])
            qst0[0] = qst
    QDONE = [S.now(n) for n in LDS]

    q2w1r = q2w[0].rearrange("(k p) c -> p k c", p=128)
    q2w2r = q2w[1].rearrange("(k p) c -> p k c", p=128)
    fb = barrier_toks()
    hprev = None
    hst = None
    for cs in range(44):
        tla = dma(q2a.rearrange("p (k c) -> p k c", k=16),
                  q2w1r[:, :, cs * 128:(cs + 1) * 128],
                  waits=QDONE + [hprev] + (fb if cs == 0 else []))
        tlb = dma(q2b.rearrange("p (k c) -> p k c", k=16),
                  q2w2r[:, :, cs * 128:(cs + 1) * 128])
        for tb in range(4):
            off = 512 + tb * 512
            p0 = PSF[0][:, 0:512]
            p1 = PSF[1][:, 0:512]
            pairs0 = [(q2a[:, k * 128:(k + 1) * 128],
                       XNT[:, k * T_IN + off:k * T_IN + off + 512])
                      for k in range(16)]
            pairs1 = [(q2b[:, k * 128:(k + 1) * 128],
                       XNT[:, k * T_IN + off:k * T_IN + off + 512])
                      for k in range(16)]
            m0 = mm_acc(p0, pairs0, [tla, hprev] + (fb if cs == 0 else []))
            m1 = mm_acc(p1, pairs1, [tlb])
            sg = sc(lambda e, _tb=tb, _p=p0: e.activation(
                sigx[:, _tb * 512:(_tb + 1) * 512], _p, AF.Sigmoid, scale=0.5),
                waits=[m0, hprev])
            hprev = v(lambda e, _tb=tb, _p=p1: e.scalar_tensor_tensor(
                hslab[:, _tb * 512:(_tb + 1) * 512], _p, 0.25,
                sigx[:, _tb * 512:(_tb + 1) * 512], OP.mult, OP.mult),
                waits=[m1, sg, hst])
        hst = dma(hidB[cs * 128:(cs + 1) * 128, :], hslab[:], waits=[hprev])
    F1_LD = [S.now(n) for n in LDS]

    oSpr = oSp.rearrange("(h p) (c d) -> p c h d", p=128, d=128)
    nb = barrier_toks() + F1_LD + ATTN_LD
    ntok = None
    gf_prev = None
    on2 = None
    for oc in range(NCO):
        tl = dma(onrow.rearrange("p (g d) -> p g d", g=H), oSpr[:, oc],
                 waits=[ntok] + (nb if oc == 0 else []))
        gts = []
        for cb in range(4):
            ps = PSF[cb][:, 0:512]
            mg = tn(lambda e, _o=oc, _cb=cb, _p=ps: e.matmul(
                _p, out2T[:, _o * 128:(_o + 1) * 128],
                wg2b[:, _cb * 512:(_cb + 1) * 512], start=True, stop=True),
                waits=[gf_prev, o2tok] + (nb if oc == 0 else []))
            gf = v(lambda e, _cb=cb, _p=ps: e.tensor_add(
                gate_f[:, _cb * 512:(_cb + 1) * 512], _p,
                bg2_s[:, _cb * 512:(_cb + 1) * 512]),
                waits=[mg] + ([on2] if on2 else []))
            gts.append(sc(lambda e, _cb=cb: e.activation(
                gatesig[:, _cb * 512:(_cb + 1) * 512],
                gate_f[:, _cb * 512:(_cb + 1) * 512], AF.Sigmoid), waits=[gf]))
        gf_prev = gts[-1]
        sq = v(lambda e: e.tensor_mul(sqn[:], onrow[:], onrow[:]), waits=[tl])
        rs2 = v(lambda e: e.reduce_sum(
            rstd16[:, 0:16], sqn.rearrange("p (g d) -> p g d", g=H), AX.X),
            waits=[sq, S.now(SS)])
        rs3 = v(lambda e: e.tensor_scalar(rstd16[:, 0:16], rstd16[:, 0:16],
                                          1.0 / 128.0, 1e-5, OP.mult, OP.add),
                waits=[rs2])
        rs4 = sc(lambda e: e.activation(rstd16[:, 0:16], rstd16[:, 0:16],
                                        AF.Ln), waits=[rs3])
        rs5 = sc(lambda e: e.activation(rstd16[:, 0:16], rstd16[:, 0:16],
                                        AF.Exp, scale=-0.5), waits=[rs4])
        on1 = v(lambda e: e.tensor_mul(onrow[:], onrow[:], gatesig[:]),
                waits=gts + [rs2, sq])
        rb3 = rstd16[:, 0:16].unsqueeze(2).broadcast_to([128, 16, 128])
        on2 = v(lambda e: e.tensor_tensor(
            onrow.rearrange("p (g d) -> p g d", g=H),
            onrow.rearrange("p (g d) -> p g d", g=H), rb3, OP.mult),
            waits=[rs5, on1])
        ntok = dma(onT.rearrange("p (j t) -> p j t", j=16)
                   [:, :, oc * 128:(oc + 1) * 128],
                   onrow[:], waits=[on2], transpose=True)
        if oc == 0:
            dbg_tap("gatesig", gatesig, [gts[-1]])
            dbg_tap("rstd16", rstd16, [rs5])
            dbg_tap("onrow0", onrow, [on2])
    NORM_LD = [S.now(n) for n in LDS]
    dbg_tap("onT", onT, NORM_LD)

    q2w3r = q2w[2].rearrange("(k p) c -> p k c", p=128)
    f2b = barrier_toks() + NORM_LD
    fprev = None
    wcast_prev = None
    yst = None
    for tq in range(4):
        tlh = dma(hidq.rearrange("p (k c) -> p k c", k=44),
                  hidB.rearrange("(k p) c -> p k c", p=128)
                  [:, :, tq * 512:(tq + 1) * 512],
                  waits=[fprev] + (f2b if tq == 0 else []))
        for cbk in range(4):
            tl3 = dma(w3cb.rearrange("p (k c) -> p k c", k=44),
                      q2w3r[:, :, cbk * 512:(cbk + 1) * 512],
                      waits=[fprev] + (f2b if tq == 0 and cbk == 0 else []))
            wtoks = []
            for j in range(16):
                tw = dma(wstage[:], WOp[j * 128:(j + 1) * 128,
                                        cbk * 512:(cbk + 1) * 512],
                         waits=[wcast_prev] + (f2b if tq == 0 and cbk == 0
                                               else []))
                wcast_prev = v(lambda e, _j=j: e.tensor_scalar(
                    wocb[:, _j * 512:(_j + 1) * 512], wstage[:], nw_s[:, 0:1],
                    None, OP.mult),
                    waits=[tw] + ([S.now(ST)] if j == 0 else []))
                wtoks.append(wcast_prev)
            for tc in range(4):
                co = tq * 4 + tc
                ps = PSF[tc % 2][:, 0:512]
                tx = dma(xstage[:], x_in[HALO + co * 128:HALO + (co + 1) * 128,
                                         cbk * 512:(cbk + 1) * 512],
                         waits=[fprev])
                pairs = [(hidq[:, k * 512 + tc * 128:k * 512 + tc * 128 + 128],
                          w3cb[:, k * 512:(k + 1) * 512]) for k in range(44)]
                pairs += [(onT[:, j * T_OWN + co * 128:j * T_OWN + co * 128 + 128],
                           wocb[:, j * 512:(j + 1) * 512]) for j in range(16)]
                mt = mm_acc(ps, pairs, [tlh, tl3, fprev] + wtoks)
                fprev = v(lambda e, _p=ps: e.tensor_add(outrow[:], _p,
                                                        xstage[:]),
                          waits=[mt, tx, yst])
                yst = dma(y[co * 128:(co + 1) * 128, cbk * 512:(cbk + 1) * 512],
                          outrow[:, 0:512], waits=[fprev])
    S.op("sync", lambda e: None, waits=[S.now(n) for n in LDS])
    S.op("gpsimd", lambda e: None, waits=[S.now(n) for n in GDS])

    from contextlib import ExitStack
    es = ExitStack()
    sems = {n: es.enter_context(nc.semaphore(n))
            for n in (["SV", "SS", "ST"] + ["LD%d" % i for i in range(8)]
                      + ["GD%d" % i for i in range(4)])}
    with nc.Block() as blk:
        @blk.sync
        def _(eng):
            S.emit("sync", eng, sems)

        @blk.vector
        def _(eng):
            S.emit("vector", eng, sems)

        @blk.scalar
        def _(eng):
            S.emit("scalar", eng, sems)

        @blk.tensor
        def _(eng):
            S.emit("tensor", eng, sems)

        @blk.gpsimd
        def _(eng):
            S.emit("gpsimd", eng, sems)
    es.close()
    return nc


_NC_CACHE = None


def kernel(L0_attn_hidden_states, L0_attn_Wq, L0_attn_Wk, L0_attn_Wv, L0_attn_Ww,
           L0_attn_cq, L0_attn_ck, L0_attn_cv, L0_attn_Wf1, L0_attn_Wf2,
           L0_attn_Wb, L0_attn_Wg1, L0_attn_Wg2, L0_attn_bg2, L0_attn_norm_w,
           L0_attn_Wo, L0_ffn_w1, L0_ffn_w2, L0_ffn_w3):
    global _NC_CACHE
    x = np.asarray(L0_attn_hidden_states, dtype=np.float32)
    B, T, _ = x.shape
    assert (B, T) == (2, 8192)

    shared = {
        "Wq": np.ascontiguousarray(np.asarray(L0_attn_Wq, np.float32)),
        "Wk": np.ascontiguousarray(np.asarray(L0_attn_Wk, np.float32)),
        "Wv": np.ascontiguousarray(np.asarray(L0_attn_Wv, np.float32)),
        "Ww": np.ascontiguousarray(np.asarray(L0_attn_Ww, np.float32)),
        "Wo": np.ascontiguousarray(np.asarray(L0_attn_Wo, np.float32)),
        "Wf1": np.ascontiguousarray(np.asarray(L0_attn_Wf1, np.float32)),
        "Wf2": np.ascontiguousarray(np.asarray(L0_attn_Wf2, np.float32)),
        "Wb": np.ascontiguousarray(np.asarray(L0_attn_Wb, np.float32)),
        "Wg1": np.ascontiguousarray(np.asarray(L0_attn_Wg1, np.float32)),
        "Wg2": np.ascontiguousarray(np.asarray(L0_attn_Wg2, np.float32)),
        "cq": np.ascontiguousarray(np.asarray(L0_attn_cq, np.float32)),
        "ck": np.ascontiguousarray(np.asarray(L0_attn_ck, np.float32)),
        "cv": np.ascontiguousarray(np.asarray(L0_attn_cv, np.float32)),
        "w1T": np.ascontiguousarray(np.asarray(L0_ffn_w1, np.float32).T),
        "w2T": np.ascontiguousarray(np.asarray(L0_ffn_w2, np.float32).T),
        "w3T": np.ascontiguousarray(np.asarray(L0_ffn_w3, np.float32).T),
        "bg2r": np.ascontiguousarray(
            np.broadcast_to(np.asarray(L0_attn_bg2, np.float32), (128, HID))),
        "nw_col": np.ascontiguousarray(
            np.asarray(L0_attn_norm_w, np.float32).reshape(128, 1)),
        "cb": np.ascontiguousarray(np.concatenate(
            [np.eye(128, dtype=BFD),
             np.triu(np.ones((128, 128), dtype=BFD)),
             np.full((128, 128), 0.5, dtype=BFD)], axis=1)),
        "cf": np.ascontiguousarray(np.tile(
            np.array([1e-5, 1e-6, 1.0, 0.0, 0.0, 0.0, 0.0, 0.0],
                     np.float32), (128, 1))),
        "crow": np.ones((1, 128), np.float32),
    }

    in_maps = []
    for c in range(8):
        b, qi = c // 4, c % 4
        lo = qi * T_OWN
        if qi == 0:
            xs = np.concatenate([np.zeros((HALO, HID), np.float32),
                                 x[b, 0:T_OWN]], axis=0)
        else:
            xs = x[b, lo - HALO: lo + T_OWN]
        m = dict(shared)
        m["x_in"] = np.ascontiguousarray(xs)
        in_maps.append(m)

    if _NC_CACHE is None:
        _NC_CACHE = build_program(debug=bool(globals().get("_DBG_MODE")))
    res = run_bass_kernel_spmd(_NC_CACHE, in_maps, core_ids=list(range(8)))
    if globals().get("_DBG_MODE"):
        globals()["_DBG_RES"] = res

    out = np.empty((2, 8192, HID), np.float32)
    for c in range(8):
        b, qi = c // 4, c % 4
        out[b, qi * T_OWN:(qi + 1) * T_OWN] = res.results[c]["y"]
    return out


if __name__ == "__main__":
    nc = build_program()
    print("build ok, instructions:", nc.n_instructions())


# version 14
# speedup vs baseline: 1.1977x; 1.0991x over previous
import numpy as np
import ml_dtypes
import concourse.bass as bass
import concourse.mybir as mybir
from concourse.bass_utils import run_bass_kernel_spmd
from concourse.alu_op_type import AluOpType as OP

F32 = mybir.dt.float32
BF16 = mybir.dt.bfloat16
AF = mybir.ActivationFunctionType
AX = mybir.AxisListType
BFD = ml_dtypes.bfloat16

HID = 2048
H = 16
T_OWN = 2048
HALO = 512
T_IN = T_OWN + HALO
NCI = T_IN // 128
NCC = 19
NCO = 16
FFH = 5632
SCALE = 128.0 ** -0.5
NELEM_W = float(HID * FFH)


class Sch:
    ENGS = ("sync", "vector", "scalar", "tensor", "gpsimd")

    def __init__(self):
        self.ops = {e: [] for e in self.ENGS}
        self.cnt = {}
        self.seen = {e: {} for e in self.ENGS}

    def sem(self, name):
        self.cnt.setdefault(name, 0)
        return name

    DMA_SEMS = tuple("LD%d" % i for i in range(8))

    def op(self, eng, fn, waits=(), inc=None, amt=1):
        w = []
        for t in waits:
            if t is None:
                continue
            s, vv = t
            if s is None or vv is None or vv <= 0:
                continue
            if self.seen[eng].get(s, -1) >= vv:
                continue
            self.seen[eng][s] = vv
            w.append((s, vv))
        tgt = None
        if inc is not None:
            self.cnt[inc] += amt
            tgt = self.cnt[inc]
        self.ops[eng].append((w, fn, inc, amt))
        return (inc, tgt) if inc is not None else None

    def now(self, sem):
        return (sem, self.cnt.get(sem, 0))

    def emit(self, eng_name, eng, sems):
        for w, fn, inc, amt in self.ops[eng_name]:
            for s, vv in w:
                eng.wait_ge(sems[s], vv)
            ins = fn(eng)
            if inc is not None:
                if ins is None:
                    ins = eng.nop()
                ins.then_inc(sems[inc], amt)


def build_program(debug=False):
    nc = bass.Bass("TRN2", target_bir_lowering=False)

    def din(name, shape, dt=F32):
        return nc.dram_tensor(name, list(shape), dt, kind="ExternalInput").ap()

    x_in = din("x_in", (T_IN, HID))
    WQ = din("Wq", (HID, HID)); WK = din("Wk", (HID, HID))
    WV = din("Wv", (HID, HID)); WW = din("Ww", (HID, HID))
    WOp = din("Wo", (HID, HID))
    WF1 = din("Wf1", (HID, 128)); WF2 = din("Wf2", (128, HID))
    WB = din("Wb", (HID, H))
    WG1 = din("Wg1", (HID, 128)); WG2 = din("Wg2", (128, HID))
    CQ = din("cq", (HID, 4)); CK = din("ck", (HID, 4)); CV = din("cv", (HID, 4))
    W1T = din("w1T", (HID, FFH)); W2T = din("w2T", (HID, FFH))
    W3T = din("w3T", (FFH, HID))
    BG2R = din("bg2r", (128, HID))
    NWC = din("nw_col", (128, 1))
    CB = din("cb", (128, 3 * 128), BF16)
    CF = din("cf", (128, 8))
    CROW = din("crow", (1, 128))

    y = nc.dram_tensor("y", [T_OWN, HID], F32, kind="ExternalOutput").ap()

    def dint(name, shape, dt):
        return nc.dram_tensor(name, list(shape), dt, kind="Internal").ap()

    def dmaybe(name, shape, dt):
        kind = "ExternalOutput" if debug else "Internal"
        return nc.dram_tensor(name, list(shape), dt, kind=kind).ap()

    q2w = [dmaybe("q2w1", (HID, FFH), BF16), dmaybe("q2w2", (HID, FFH), BF16),
           dmaybe("q2w3", (FFH, HID), BF16)]
    hidB = dmaybe("hidB", (FFH, T_OWN), BF16)
    woB = dint("woB", (HID, HID), BF16)
    oSp = dmaybe("oSp", (H * 128, NCO * 128), BF16)

    DBG = {}

    def dbg_tap(name, ap, waits):
        if not debug:
            return
        if not isinstance(ap, bass.AP):
            ap = ap[:]
        t = nc.dram_tensor("dbg_" + name, [ap.shape[0], ap.shape[1]],
                           ap.dtype, kind="ExternalOutput").ap()
        DBG[name] = dma(t, ap, waits=waits)

    sbt = nc.alloc_sbuf_tensor

    cb_s = sbt("cb_s", [128, 3 * 128], BF16)
    IDENT = cb_s[:, 0:128]
    UTC = cb_s[:, 128:256]
    HALFM = cb_s[:, 256:384]
    cf_s = sbt("cf_s", [128, 8], F32)
    EPS_LN = cf_s[:, 0:1]; EPS_L2 = cf_s[:, 1:2]; ONE_F = cf_s[:, 2:3]
    crow_s = sbt("crow_s", [1, 128], F32)
    ones_bf = sbt("ones_bf", [128, 16], BF16)
    ONES_COL = ones_bf[:, 0:1]
    nw_s = sbt("nw_s", [128, 1], F32)
    bg2_s = sbt("bg2_s", [128, HID], BF16)
    wf2b = sbt("wf2b", [128, HID], BF16)
    wg2b = sbt("wg2b", [128, HID], BF16)
    cqall = sbt("cqall", [128, 3 * 64], F32)
    beta_s = sbt("beta_s", [128, NCI * H], F32)
    out1T = sbt("out1T", [128, T_IN], BF16)
    out2T = sbt("out2T", [128, T_OWN], BF16)
    lnst = sbt("lnst", [128, 8], F32)
    gcols = sbt("gcols", [128, 32], F32)
    dcols = sbt("dcols", [128, 32], F32)
    ercols = sbt("ercols", [128, 32], F32)
    sscol = sbt("sscol", [128, 48], F32)
    brcol = sbt("brcol", [128, 32], F32)
    brS = sbt("brS", [128, 32], F32)
    ncol = sbt("ncol", [128, 8], F32)
    rstd16 = sbt("rstd16", [128, 16], F32)
    gsum = sbt("gsum", [128, 8], F32)
    gcolq = sbt("gcolq", [128, 8], F32)
    gtmp = sbt("gtmp", [1, 8], F32)
    SkT = sbt("SkT", [128, 128], BF16)
    Sv = sbt("Sv", [128, 128], BF16)
    SkEP = [sbt("SkE%d" % i, [128, 128], BF16) for i in range(2)]
    SkEdP = [sbt("SkEd%d" % i, [128, 128], BF16) for i in range(2)]
    QKmP = [sbt("QKm%d" % i, [128, 128], BF16) for i in range(2)]
    stilP = [sbt("stil%d" % i, [128, 128], BF16) for i in range(2)]
    a_eP = [sbt("a_e%d" % i, [128, 128], BF16) for i in range(2)]
    aTtP = [sbt("aTt%d" % i, [128, 128], BF16) for i in range(2)]
    aHtP = [sbt("aHt%d" % i, [128, 128], BF16) for i in range(2)]
    PTmP = [sbt("PTm%d" % i, [128, 128], BF16) for i in range(2)]

    AR1 = sbt("AR1", [128, 43008], BF16)
    AR2 = sbt("AR2", [128, 47232], BF16)

    def carve(ar, off_bytes, cols, dt):
        assert off_bytes % 4 == 0
        c2 = off_bytes // 2
        if dt == BF16:
            assert (off_bytes + 2 * cols) <= 2 * ar.shape[1]
            return ar[:, c2:c2 + cols]
        assert (off_bytes + 4 * cols) <= 2 * ar.shape[1]
        return ar[:, c2:c2 + 2 * cols].bitcast(F32)

    XNT = carve(AR1, 0, 16 * T_IN, BF16)
    onT = carve(AR1, 0, 16 * T_OWN, BF16)
    wocb = carve(AR1, 65536, 16 * 512, BF16)
    wstage = carve(AR1, 81920, 512, F32)
    outrow = carve(AR1, 83968, 512, F32)

    xrow = carve(AR2, 0, HID, F32)
    sqrow = carve(AR2, 8192, HID, F32)
    xnrow = carve(AR2, 16384, HID, BF16)
    _off = [0]

    def hc(cols, dt):
        ap = carve(AR2, _off[0], cols, dt)
        _off[0] += cols * (2 if dt == BF16 else 4)
        _off[0] = (_off[0] + 63) & ~63
        return ap

    wcb = [hc(HID, BF16), hc(HID, BF16)]
    preconv = hc(T_IN + 4 + 540, BF16)
    projbuf = hc(T_IN, BF16)
    E1 = carve(AR2, 0, NCC * 128, BF16)
    E2 = carve(AR2, 4864, NCC * 128, BF16)
    ERb = carve(AR2, 9728, NCC * 128, BF16)
    expRr = carve(AR2, 14592, NCC * 128, BF16)
    kT_s = hc(T_IN, BF16)
    qT_s = hc(T_IN, BF16)
    K_tm = hc(NCC * 128, BF16)
    V_tm = hc(NCC * 128, BF16)
    Vh_tm = hc(NCC * 128, BF16)
    W_tm = hc(NCC * 128, BF16)
    g_sm = hc(NCC * 128, BF16)
    wt_sm = hc(NCC * 128, BF16)
    wh_sm = hc(NCC * 128, BF16)
    grow = hc(512, F32)
    zrow = hc(512, F32)
    obuf = carve(AR1, 81920, NCO * 128, BF16)
    Sk_all = hc(NCO * 128, BF16)
    Sv_all = hc(NCO * 128, BF16)
    assert _off[0] <= 94464, _off[0]
    fslab = carve(AR2, 0, FFH, F32)
    fsl_a = carve(AR2, 22528, FFH, BF16)
    fsl_b = carve(AR2, 33792, FFH, BF16)
    q2a = carve(AR2, 0, HID, BF16)
    q2b = carve(AR2, 4096, HID, BF16)
    sigx = carve(AR2, 8192, T_OWN, BF16)
    hslab = carve(AR2, 12288, T_OWN, BF16)
    onrow = carve(AR2, 0, HID, BF16)
    gate_f = carve(AR2, 4096, HID, F32)
    gatesig = carve(AR2, 12288, HID, BF16)
    sqn = carve(AR2, 16384, HID, BF16)
    hidq = carve(AR2, 0, 44 * 512, BF16)
    w3cb = carve(AR2, 45056, 44 * 512, BF16)
    xstage = carve(AR2, 90112, 512, F32)

    PSF = [nc.alloc_psum_tensor("PS%d" % i, [128, 512], F32) for i in range(6)]
    PB6 = nc.alloc_psum_tensor("PB6", [128, 1024], BF16)
    PB7 = nc.alloc_psum_tensor("PB7", [128, 1024], BF16)
    TRA = PB6[:, 0:128]
    TRB = PB7[:, 0:128]
    TRC = PB7[:, 512:640]

    S = Sch()
    LDS = [S.sem("LD%d" % i) for i in range(8)]
    GDS = [S.sem("GD%d" % i) for i in range(4)]
    SV = S.sem("SV"); SS = S.sem("SS"); ST = S.sem("ST")
    _rr = [0]
    _gr = [0]

    def dma(out_, in_, waits=(), transpose=False):
        if transpose:
            waits = list(waits) + [S.now(n) for n in LDS]
        sem = LDS[_rr[0] % 8]
        _rr[0] += 1
        if S.cnt[sem] > 0:
            S.op("sync", lambda e: None, waits=[S.now(sem)])

        def f(e, _o=out_, _i=in_, _t=transpose):
            return e.dma_start(out=_o, in_=_i, transpose=_t)
        tok = S.op("sync", f, waits=waits, inc=sem, amt=16)
        if transpose:
            for n in LDS:
                S.seen["sync"].pop(n, None)
            S.op("sync", lambda e: None, waits=[S.now(sem)])
        return tok

    def gdma(out_, in_, waits=()):
        sem = GDS[_gr[0] % 4]
        _gr[0] += 1
        if S.cnt[sem] > 0:
            S.op("gpsimd", lambda e: None, waits=[S.now(sem)])

        def f(e, _o=out_, _i=in_):
            return e.dma_start(out=_o, in_=_i)
        return S.op("gpsimd", f, waits=waits, inc=sem, amt=16)

    def v(fn, waits=()):
        return S.op("vector", fn, waits=waits, inc=SV)

    def sc(fn, waits=()):
        return S.op("scalar", fn, waits=waits, inc=SS)

    def tn(fn, waits=()):
        return S.op("tensor", fn, waits=waits, inc=ST)

    def barrier_toks():
        return ([S.now(SV), S.now(SS), S.now(ST)]
                + [S.now(n) for n in LDS] + [S.now(n) for n in GDS])

    def mm_acc(ps_ap, pairs, waits_first):
        tok = None
        n = len(pairs)
        for k, (lt, rh) in enumerate(pairs):
            tok = tn(lambda e, _l=lt, _r=rh, _s=(k == 0), _e=(k == n - 1),
                     _p=ps_ap: e.matmul(_p, _l, _r, start=_s, stop=_e),
                     waits=waits_first if k == 0 else ())
        return tok

    dma(cb_s[:], CB)
    dma(cf_s[:], CF)
    dma(crow_s[:], CROW)
    dma(nw_s[:], NWC)
    v(lambda e: e.memset(ones_bf[:], 1.0))
    v(lambda e: e.memset(PSF[5][64:128, 0:64], 0.0))
    v(lambda e: e.memset(PSF[2][64:128, 0:64], 0.0))
    bgtok = gdma(bg2_s[:], BG2R)
    dma(cqall.rearrange("p (s k w) -> p s k w", s=3, w=4)[:, 0],
        CQ.rearrange("(k p) w -> p k w", p=128))
    dma(cqall.rearrange("p (s k w) -> p s k w", s=3, w=4)[:, 1],
        CK.rearrange("(k p) w -> p k w", p=128))
    dma(cqall.rearrange("p (s k w) -> p s k w", s=3, w=4)[:, 2],
        CV.rearrange("(k p) w -> p k w", p=128))
    wf2tok = gdma(wf2b[:], WF2)
    wg2tok = gdma(wg2b[:], WG2)

    prev_used = wg2tok
    ttok = None
    for i in range(NCI):
        tl = dma(xrow[:], x_in[i * 128:(i + 1) * 128, :], waits=[prev_used])
        t1 = v(lambda e: e.reduce_sum(lnst[:, 0:1], xrow[:], AX.X), waits=[tl])
        t2 = sc(lambda e: e.activation(sqrow[:], xrow[:], AF.Square,
                                       accum_out=lnst[:, 1:2]), waits=[tl])
        t3 = v(lambda e: e.tensor_scalar(lnst[:, 2:3], lnst[:, 0:1], 1.0 / HID,
                                         None, OP.mult), waits=[t1])
        t4 = v(lambda e: e.tensor_scalar(lnst[:, 3:4], lnst[:, 1:2], 1.0 / HID,
                                         None, OP.mult), waits=[t2])
        t5 = v(lambda e: e.tensor_mul(lnst[:, 4:5], lnst[:, 2:3], lnst[:, 2:3]),
               waits=[t3])
        t6 = v(lambda e: e.tensor_sub(lnst[:, 5:6], lnst[:, 3:4], lnst[:, 4:5]),
               waits=[t4, t5])
        t7 = sc(lambda e: e.activation(lnst[:, 6:7], lnst[:, 5:6], AF.Ln,
                                       bias=EPS_LN), waits=[t6])
        t8 = sc(lambda e: e.activation(lnst[:, 7:8], lnst[:, 6:7], AF.Exp,
                                       scale=-0.5), waits=[t7])
        t9 = v(lambda e: e.tensor_scalar(xnrow[:], xrow[:], lnst[:, 2:3],
                                         lnst[:, 7:8], OP.subtract, OP.mult),
               waits=[t8, t3, ttok])
        ttok = dma(XNT.rearrange("p (k t) -> p k t", k=16)
                   [:, :, i * 128:(i + 1) * 128],
                   xnrow[:], waits=[t9], transpose=True)
        prev_used = t9
    XN_RDY = ttok
    dbg_tap("XNT", XNT, [XN_RDY])

    cwf = gdma(wcb[0].rearrange("p (k c) -> p k c", k=16),
               WF1.rearrange("(k p) c -> p k c", p=128), waits=[XN_RDY])
    cwg = gdma(wcb[1].rearrange("p (k c) -> p k c", k=16),
               WG1.rearrange("(k p) c -> p k c", p=128), waits=[XN_RDY])
    evtoks = {}
    for tb in range(5):
        ps = PSF[tb % 2][:, 0:512]
        pairs = [(wcb[0][:, k * 128:(k + 1) * 128],
                  XNT[:, k * T_IN + tb * 512:k * T_IN + tb * 512 + 512])
                 for k in range(16)]
        mt = mm_acc(ps, pairs, [cwf, XN_RDY, evtoks.get(tb - 2)])
        evtoks[tb] = v(lambda e, _tb=tb, _ps=ps: e.tensor_copy(
            out1T[:, _tb * 512:(_tb + 1) * 512], _ps), waits=[mt])
    o1tok = S.now(SV)
    f1mm = mt
    evtoks = {}
    for tb in range(4):
        ps = PSF[2 + tb % 2][:, 0:512]
        off = 512 + tb * 512
        pairs = [(wcb[1][:, k * 128:(k + 1) * 128],
                  XNT[:, k * T_IN + off:k * T_IN + off + 512]) for k in range(16)]
        mt = mm_acc(ps, pairs, [cwg, evtoks.get(tb - 2)])
        evtoks[tb] = v(lambda e, _tb=tb, _ps=ps: e.tensor_copy(
            out2T[:, _tb * 512:(_tb + 1) * 512], _ps), waits=[mt])
    o2tok = S.now(SV)
    cwb = gdma(wcb[0].rearrange("p (k c) -> p k c", c=16)[:, 0:16, :],
               WB.rearrange("(k p) c -> p k c", p=128), waits=[f1mm])
    bprev = None
    for i in range(NCI):
        ps = PSF[4][:, 0:16]
        pairs = [(XNT[:, k * T_IN + i * 128:k * T_IN + i * 128 + 128],
                  wcb[0][:, k * 16:(k + 1) * 16]) for k in range(16)]
        mt = mm_acc(ps, pairs, [cwb, bprev])
        bprev = sc(lambda e, _i=i, _ps=ps: e.activation(
            beta_s[:, _i * 16:(_i + 1) * 16], _ps, AF.Sigmoid), waits=[mt])
    beta_tok = bprev
    dbg_tap("out1T", out1T, [o1tok])
    dbg_tap("out2T", out2T, [o2tok])
    dbg_tap("beta", beta_s, [beta_tok])

    beta3 = beta_s.rearrange("p (c h) -> p c h", h=H)
    zpad = v(lambda e: e.memset(preconv[:, 0:4], 0.0), waits=[beta_tok])

    TRREG = [(PB6, 0), (PB7, 0), (PB6, 128), (PB7, 128)]
    trst = {"rot": 0, "ev": {}}

    def pe_transpose_seq(src, n, dst, waits_first):
        ev = None
        for i in range(n):
            bank, off = TRREG[trst["rot"] % 4]
            reg = bank[:, off:off + 128]
            w = [trst["ev"].get(trst["rot"] - 2)]
            if i == 0:
                w += list(waits_first)
            tt = tn(lambda e, _s=src[:, 128 + i * 128:128 + (i + 1) * 128],
                    _r=reg: e.transpose(_r, _s, IDENT), waits=w)
            ev = v(lambda e, _i=i, _r=reg, _d=dst: e.tensor_copy(
                _d[:, _i * 128:(_i + 1) * 128], _r), waits=[tt])
            trst["ev"][trst["rot"]] = ev
            trst["rot"] += 1
        return ev

    WSEQ = [(WW, 2), (WV, 2), (WK, 1), (WQ, 0)]
    PDST = [projbuf, projbuf, kT_s, qT_s]
    PTM = [W_tm, V_tm, K_tm, None]

    GRPS = [(0, 4), (4, 4), (8, 4), (12, 4), (16, 3)]
    spill_prev = None
    for h in range(H):
        hb = barrier_toks()
        ldw = [None] * 4
        for j in (0, 1):
            ldw[j] = gdma(wcb[j].rearrange("p (k c) -> p k c", k=16),
                          WSEQ[j][0].rearrange("(k p) c -> p k c", p=128)
                          [:, :, h * 128:(h + 1) * 128], waits=hb)
        trtok = [None] * 4
        silu_t = [None] * 4
        prev_tap3 = None
        for j in range(4):
            buf = wcb[j % 2]
            evs = {}
            mt = None
            for tb in range(5):
                ps = PSF[4 + tb % 2][:, 0:512]
                pairs = [(buf[:, k * 128:(k + 1) * 128],
                          XNT[:, k * T_IN + tb * 512:k * T_IN + tb * 512 + 512])
                         for k in range(16)]
                mt = mm_acc(ps, pairs,
                            [ldw[j], evs.get(tb - 2)]
                            + (hb if (h == 0 and j == 0 and tb == 0) else []))
                evs[tb] = v(lambda e, _tb=tb, _ps=ps: e.tensor_copy(
                    preconv[:, 4 + _tb * 512:4 + (_tb + 1) * 512], _ps),
                    waits=[mt] + ([prev_tap3] if tb == 0 else []))
            if j + 2 < 4:
                ldw[j + 2] = gdma(wcb[j % 2].rearrange("p (k c) -> p k c", k=16),
                                  WSEQ[j + 2][0]
                                  .rearrange("(k p) c -> p k c", p=128)
                                  [:, :, h * 128:(h + 1) * 128], waits=[mt])
            base = WSEQ[j][1] * 64 + h * 4
            cc = [cqall[:, base + i:base + i + 1] for i in range(4)]
            dest = PDST[j]
            t = v(lambda e, _d=dest: e.tensor_scalar(
                _d[:], preconv[:, 1:1 + T_IN], cc[0], None, OP.mult),
                waits=[evs[4], zpad, S.now(ST)])
            for i in (1, 2, 3):
                t = v(lambda e, _i=i, _d=dest: e.scalar_tensor_tensor(
                    _d[:], preconv[:, 1 + _i:1 + _i + T_IN], cc[_i], _d[:],
                    OP.mult, OP.add), waits=[t])
            prev_tap3 = t
            silu_t[j] = sc(lambda e, _d=dest: e.activation(_d[:], _d[:],
                                                           AF.Silu), waits=[t])
            if PTM[j] is not None:
                trtok[j] = pe_transpose_seq(
                    dest, NCC, PTM[j],
                    [silu_t[j]] + ([S.now(SV)] if j == 0 else []))
        qtok = silu_t[3]
        trW, trV, trK = trtok[0], trtok[1], trtok[2]
        ge_prev = None
        gevs = []
        for tb in range(5):
            ps = PSF[2 + tb % 2][:, 0:512]
            mt = mm_acc(ps, [(wf2b[:, h * 128:(h + 1) * 128],
                              out1T[:, tb * 512:(tb + 1) * 512])],
                        [wf2tok, S.now(SV), S.now(SS)])
            e1 = sc(lambda e, _ps=ps: e.activation(zrow[:], _ps, AF.Abs),
                    waits=[mt, ge_prev])
            e2 = sc(lambda e: e.activation(grow[:], zrow[:], AF.Exp, scale=-1.0),
                    waits=[e1])
            e3 = sc(lambda e: e.activation(grow[:], grow[:], AF.Ln, bias=ONE_F),
                    waits=[e2])
            e4 = v(lambda e, _tb=tb, _ps=ps: e.scalar_tensor_tensor(
                projbuf[:, _tb * 512:(_tb + 1) * 512], _ps, 0.0, grow[:],
                OP.min, OP.subtract), waits=[e3, S.now(ST)])
            ge_prev = e4
            gevs.append(e4)
        trG = pe_transpose_seq(projbuf, NCC, g_sm, gevs)

        dtoks = []
        cp_prev = zz_prev = se1_prev = se3_prev = se4_prev = None
        for g0, gn in GRPS:
            gw = gn * 128
            mg = None
            for j in range(gn):
                mg = tn(lambda e, _j=j, _c=g0 + j: e.matmul(
                    PSF[0][:, _j * 128:(_j + 1) * 128], UTC,
                    g_sm[:, _c * 128:(_c + 1) * 128], start=True, stop=True),
                    waits=[trG, cp_prev, se1_prev, S.now(SV)] if j == 0 else ())
            mr = None
            for j in range(gn):
                mr = tn(lambda e, _j=j, _c=g0 + j: e.matmul(
                    PSF[1][:, _j * 128:(_j + 1) * 128], HALFM,
                    g_sm[:, _c * 128:(_c + 1) * 128], start=True, stop=True),
                    waits=[zz_prev, se4_prev] if j == 0 else ())
            cp = v(lambda e, _w=gw: e.tensor_copy(grow[:, 0:_w], PSF[0][:, 0:_w]),
                   waits=[mg, S.now(SS)])
            se1 = sc(lambda e, _w=gw, _o=g0 * 128: e.activation(
                E1[:, _o:_o + _w], PSF[0][:, 0:_w], AF.Exp),
                waits=[mg, S.now(SV)])
            zz = v(lambda e, _w=gw: e.tensor_sub(zrow[:, 0:_w], PSF[1][:, 0:_w],
                                                 grow[:, 0:_w]),
                   waits=[cp, mr, se3_prev])
            zh = v(lambda e, _w=gw: e.tensor_add(grow[:, 0:_w], zrow[:, 0:_w],
                                                 PSF[1][:, 0:_w]),
                   waits=[zz, se4_prev, se1])
            zc = v(lambda e, _w=gw: e.tensor_scalar(zrow[:, 0:_w], zrow[:, 0:_w],
                                                    80.0, -80.0, OP.min, OP.max),
                   waits=[zh])
            se2 = sc(lambda e, _w=gw, _o=g0 * 128: e.activation(
                E2[:, _o:_o + _w], zrow[:, 0:_w], AF.Exp, scale=-1.0), waits=[zc])
            se3 = sc(lambda e, _w=gw, _o=g0 * 128: e.activation(
                ERb[:, _o:_o + _w], zrow[:, 0:_w], AF.Exp))
            se4 = sc(lambda e, _w=gw, _o=g0 * 128: e.activation(
                expRr[:, _o:_o + _w], grow[:, 0:_w], AF.Exp), waits=[zh])
            cp_prev, zz_prev, se1_prev, se3_prev, se4_prev = cp, zc, se1, se3, se4
            dtoks += [se1, se2, se3, se4]
        mgc = None
        for c in range(NCC):
            mgc = tn(lambda e, _c=c: e.matmul(PSF[2][:, _c:_c + 1],
                                              g_sm[:, _c * 128:(_c + 1) * 128],
                                              ONES_COL, start=True, stop=True),
                     waits=[trG, S.now(SV), S.now(SS)] if c == 0 else ())
        cgc = v(lambda e: e.tensor_copy(gcols[:, 0:NCC], PSF[2][:, 0:NCC]),
                waits=[mgc, S.now(SS)])
        sdc = sc(lambda e: e.activation(dcols[:, 0:NCC], gcols[:, 0:NCC], AF.Exp),
                 waits=[cgc, S.now(SV)])
        sq1 = v(lambda e: e.tensor_mul(wt_sm[:], W_tm[:], W_tm[:]),
                waits=[trW, S.now(ST)])
        l2b = v(lambda e: e.reduce_sum(
            sscol[:, 0:NCC], wt_sm.rearrange("p (c m) -> p c m", c=NCC), AX.X),
            waits=[sq1, S.now(SS)])
        l2c = sc(lambda e: e.activation(sscol[:, 20:20 + NCC], sscol[:, 0:NCC],
                                        AF.Ln, bias=EPS_L2), waits=[l2b])
        l2d = sc(lambda e: e.activation(sscol[:, 20:20 + NCC],
                                        sscol[:, 20:20 + NCC], AF.Exp,
                                        scale=-0.5), waits=[l2c])
        brt = v(lambda e, _h=h: e.tensor_mul(brcol[:, 0:NCC],
                                             sscol[:, 20:20 + NCC],
                                             beta3[:, 1:1 + NCC, _h]),
                waits=[l2d, beta_tok, S.now(ST)])
        brst = v(lambda e: e.tensor_scalar(brS[:, 0:NCC], brcol[:, 0:NCC], SCALE,
                                           None, OP.mult), waits=[brt])
        wtk = v(lambda e: e.tensor_mul(wt_sm[:], W_tm[:], ERb[:]),
                waits=[l2b] + dtoks)
        whk = v(lambda e: e.tensor_mul(wh_sm[:], W_tm[:], expRr[:]),
                waits=[wtk])
        br3 = brcol[:, 0:NCC].unsqueeze(2).broadcast_to([128, NCC, 128])
        khk = v(lambda e: e.tensor_tensor(
            K_tm.rearrange("p (c t) -> p c t", c=NCC),
            K_tm.rearrange("p (c t) -> p c t", c=NCC), br3, OP.mult),
            waits=[trK, brt])
        vhk = v(lambda e: e.tensor_tensor(
            Vh_tm.rearrange("p (c t) -> p c t", c=NCC),
            V_tm.rearrange("p (c t) -> p c t", c=NCC), br3, OP.mult),
            waits=[trV, brt])
        if h == 0:
            dbg_tap("kT", kT_s, [S.now(SS)])
            dbg_tap("qT", qT_s, [qtok])
            dbg_tap("W_tm", W_tm, [trW])
            dbg_tap("V_tm", V_tm, [trV])
            dbg_tap("g_sm", g_sm, [trG])
            dbg_tap("E1", E1, dtoks)
            dbg_tap("E2", E2, dtoks)
            dbg_tap("ERb", ERb, dtoks)
            dbg_tap("expRr", expRr, dtoks)
            dbg_tap("gcols", gcols, [cgc])
            dbg_tap("dcols", dcols, [sdc])
            dbg_tap("sscol", sscol, [l2d])
            dbg_tap("brcol", brcol, [brt])
            dbg_tap("wt_sm", wt_sm, [wtk])
            dbg_tap("wh_sm", wh_sm, [whk])
            dbg_tap("Khat", K_tm, [khk])
            dbg_tap("Vhat", Vh_tm, [vhk])
        trWT = dma(W_tm.rearrange("p (c t) -> p c t", c=NCC)[:, 0:NCO, :],
                   wt_sm[:, 3 * 128:NCC * 128],
                   waits=[wtk, khk, vhk, S.now(ST)], transpose=True)
        wtT = W_tm

        z1 = v(lambda e: e.memset(SkT[:], 0.0), waits=[S.now(ST)])
        z2 = v(lambda e: e.memset(Sv[:], 0.0))

        upK = upV = None
        up2 = {0: None, 1: None}
        snapKs = None
        for c in range(NCC):
            par = c % 2
            PK = PSF[par][:, 0:128]
            PV = PSF[2 + par][:, 0:128]
            oc = c - 3
            if oc >= 0:
                snapK = v(lambda e, _o=oc: e.tensor_copy(
                    Sk_all[:, _o * 128:(_o + 1) * 128], SkT[:]),
                    waits=[upK, z1])
                snapV = v(lambda e, _o=oc: e.tensor_copy(
                    Sv_all[:, _o * 128:(_o + 1) * 128], Sv[:]), waits=[upV, z2])
                snapKs = snapV
            mdk = tn(lambda e, _c=c, _p=PK: e.matmul(
                _p, wh_sm[:, _c * 128:(_c + 1) * 128],
                K_tm[:, _c * 128:(_c + 1) * 128], start=True, stop=True),
                waits=[whk, khk, up2[par], S.now(SV) if c == 0 else None])
            mdv = tn(lambda e, _c=c, _p=PV: e.matmul(
                _p, wh_sm[:, _c * 128:(_c + 1) * 128],
                Vh_tm[:, _c * 128:(_c + 1) * 128], start=True, stop=True),
                waits=[vhk, S.now(SV) if c == 0 else None])
            upK = v(lambda e, _c=c, _p=PK: e.scalar_tensor_tensor(
                SkT[:], SkT[:], dcols[:, _c:_c + 1], _p, OP.mult, OP.add),
                waits=[mdk, sdc, z1, snapKs])
            upV = v(lambda e, _c=c, _p=PV: e.scalar_tensor_tensor(
                Sv[:], Sv[:], dcols[:, _c:_c + 1], _p, OP.mult, OP.add),
                waits=[mdv, z2])
            up2[par] = upV

        prevp = {0: {}, 1: {}}

        def chunkB(oc, pdict):
            c = oc + 3
            gc = c + 1
            par = oc % 2
            tcol = slice(gc * 128, (gc + 1) * 128)
            ccol = slice(c * 128, (c + 1) * 128)
            Pqk = PSF[3 * par + 0][:, 0:128]
            Psc = PSF[3 * par + 1][:, 0:128]
            Ppt = PSF[3 * par + 2][:, 0:128]
            PB = PB6 if par == 0 else PB7
            trS, trA, trB = PB[:, 0:128], PB[:, 128:256], PB[:, 256:384]
            SkEd_, SkE_, QKm_ = SkEdP[par], SkEP[par], QKmP[par]
            stil_, a_e_, aTt_, aHt_, PTm_ = (stilP[par], a_eP[par], aTtP[par],
                                             aHtP[par], PTmP[par])
            nc0 = ncol[:, 4 * par + 0:4 * par + 1]
            nc1 = ncol[:, 4 * par + 1:4 * par + 2]
            nc2 = ncol[:, 4 * par + 2:4 * par + 3]
            ttr = tn(lambda e, _o=oc: e.transpose(
                trS, Sk_all[:, _o * 128:(_o + 1) * 128], IDENT),
                waits=[snapKs, upK, pdict.get("evSk")])
            yield
            eve = v(lambda e: e.tensor_copy(SkEd_[:], trS),
                    waits=[ttr, pdict.get("mt1")])
            pdict["evSk"] = eve
            yield
            mqk = tn(lambda e, _t=tcol: e.matmul(Pqk, kT_s[:, _t], qT_s[:, _t],
                                                 start=True, stop=True),
                     waits=[eve, qtok, pdict.get("evT1")])
            yield
            evQK = v(lambda e, _c=c: e.scalar_tensor_tensor(
                QKm_[:], Pqk, brS[:, _c:_c + 1], UTC, OP.mult, OP.mult),
                waits=[mqk, brst])
            yield
            ms1 = tn(lambda e, _c=c: e.matmul(
                Psc, QKm_[:], wt_sm[:, _c * 128:(_c + 1) * 128],
                start=True, stop=True), waits=[evQK, wtk, pdict.get("evS1")])
            yield
            mt1 = tn(lambda e, _t=tcol: e.matmul(Pqk, qT_s[:, _t], SkEd_[:],
                                                 start=True, stop=True),
                     waits=[eve, evQK])
            pdict["mt1"] = mt1
            yield
            evT1 = v(lambda e, _c=c: e.scalar_tensor_tensor(
                SkE_[:], Pqk, SCALE, E1[:, _c * 128:(_c + 1) * 128],
                OP.mult, OP.mult), waits=[mt1])
            pdict["evT1"] = evT1
            yield
            evS1 = v(lambda e, _c=c: e.tensor_mul(
                stil_[:], Psc, E2[:, _c * 128:(_c + 1) * 128]), waits=[ms1])
            pdict["evS1"] = evS1
            yield
            evSC = v(lambda e: e.tensor_add(stil_[:], stil_[:], SkE_[:]),
                     waits=[evS1, evT1])
            yield
            nm = v(lambda e: e.reduce_max(nc0, stil_[:], AX.X, negate=True),
                   waits=[evSC, pdict.get("ae")])
            yield
            ae = sc(lambda e: e.activation(a_e_[:], stil_[:], AF.Exp, bias=nc0),
                    waits=[nm])
            pdict["ae"] = ae
            yield
            sm = v(lambda e: e.reduce_sum(nc1, a_e_[:], AX.X), waits=[ae])
            yield
            ri = v(lambda e: e.reciprocal(nc2, nc1), waits=[sm])
            yield
            at = v(lambda e, _c=c: e.tensor_mul(
                aTt_[:], a_e_[:], E2[:, _c * 128:(_c + 1) * 128]),
                waits=[ae, pdict.get("mpA")])
            yield
            ah = v(lambda e, _c=c: e.tensor_mul(
                aHt_[:], a_e_[:], E1[:, _c * 128:(_c + 1) * 128]),
                waits=[pdict.get("mo2")])
            yield
            ttb = tn(lambda e: e.transpose(trA, aTt_[:], IDENT),
                     waits=[at, eve, pdict.get("evB")])
            yield
            evA = v(lambda e: e.tensor_copy(aTt_[:], trA), waits=[ttb])
            yield
            ttc = tn(lambda e: e.transpose(trB, aHt_[:], IDENT),
                     waits=[ah, evA])
            yield
            evB = v(lambda e: e.tensor_copy(aHt_[:], trB), waits=[ttc])
            pdict["evB"] = evB
            yield
            mp1 = tn(lambda e, _o=oc: e.matmul(
                Ppt[0:64, 0:64], wtT[:, _o * 128:_o * 128 + 64], aTt_[:, 0:64],
                start=True, stop=True), waits=[evA, trWT, pdict.get("evPT")])
            mp2 = tn(lambda e, _o=oc: e.matmul(
                Ppt[64:128, 64:128], wtT[:, _o * 128 + 64:_o * 128 + 128],
                aTt_[:, 64:128], start=True, stop=True))
            mp3 = tn(lambda e, _o=oc: e.matmul(
                Ppt[0:64, 64:128], wtT[:, _o * 128:_o * 128 + 64],
                aTt_[:, 64:128], start=True, stop=True))
            pdict["mpA"] = mp3
            yield
            evPT = v(lambda e, _c=c: e.scalar_tensor_tensor(
                PTm_[:], Ppt, brcol[:, _c:_c + 1], UTC, OP.mult, OP.mult),
                waits=[mp3])
            pdict["evPT"] = evPT
            yield
            mo1 = tn(lambda e, _c=c: e.matmul(
                Ppt, PTm_[:], V_tm[:, _c * 128:(_c + 1) * 128],
                start=True, stop=False), waits=[evPT])
            mo2 = tn(lambda e, _o=oc: e.matmul(
                Ppt, aHt_[:], Sv_all[:, _o * 128:(_o + 1) * 128],
                start=False, stop=True), waits=[evB, snapKs])
            pdict["mo2"] = mo2
            yield
            evO = v(lambda e, _o=oc: e.tensor_scalar(
                obuf[:, _o * 128:(_o + 1) * 128], Ppt, nc2, None, OP.mult),
                waits=[mo2, ri, spill_prev])
            pdict["evO"] = evO
            yield

        for co in range(0, NCO, 2):
            gens = [chunkB(co, prevp[0]), chunkB(co + 1, prevp[1])]
            live = [True, True]
            while any(live):
                for gi in (0, 1):
                    if live[gi]:
                        try:
                            next(gens[gi])
                        except StopIteration:
                            live[gi] = False
        evO = prevp[1].get("evO")
        evO2 = prevp[0].get("evO")
        spill_prev = dma(oSp[h * 128:(h + 1) * 128, :], obuf[:],
                         waits=[evO, evO2])
        if h == 0:
            dbg_tap("SkT_f", SkT, [upK])
            dbg_tap("Sv_f", Sv, [upV])
    ATTN_LD = [S.now(n) for n in LDS]

    srcs = [(W1T, 16, FFH), (W2T, 16, FFH), (W3T, 44, HID)]
    slabs = [(mi, i) for mi, (_, n, _w) in enumerate(srcs) for i in range(n)]
    gb = barrier_toks()
    zg = v(lambda e: e.memset(gsum[:, 0:4], 0.0), waits=gb)
    fslabD = [carve(AR2, 0, FFH, F32), carve(AR2, 45056, FFH, F32)]
    fslaD = [carve(AR2, 22528, FFH, BF16), carve(AR2, 67584, FFH, BF16)]
    fslbD = [carve(AR2, 33792, FFH, BF16), carve(AR2, 78848, FFH, BF16)]
    cons = {}
    gprev = zg
    for si, (mi, i) in enumerate(slabs):
        WSRC, nslab, wid = srcs[mi]
        bi = si % 2
        tl = dma(fslabD[bi][:, 0:wid], WSRC[i * 128:(i + 1) * 128, :],
                 waits=[cons.get(si - 2)] + (gb if si == 0 else []))
        ga = sc(lambda e, _w=wid, _b=bi: e.activation(
            fslaD[_b][:, 0:_w], fslabD[_b][:, 0:_w], AF.Abs,
            accum_out=gsum[:, 4:5]), waits=[tl, gprev])
        gprev = v(lambda e, _m=mi: e.tensor_add(
            gsum[:, _m:_m + 1], gsum[:, _m:_m + 1], gsum[:, 4:5]), waits=[ga])
        cons[si] = ga
    thr = {}
    tprev = gprev
    for mi in range(3):
        mt = tn(lambda e, _m=mi: e.matmul(
            PSF[2][0:1, _m:_m + 1], gsum[:, _m:_m + 1], ONE_F,
            start=True, stop=True), waits=[gprev, S.now(SV), S.now(SS)])
        cv_ = v(lambda e, _m=mi: e.tensor_copy(gtmp[0:1, _m:_m + 1],
                                               PSF[2][0:1, _m:_m + 1]),
                waits=[mt])
        mr = tn(lambda e, _m=mi: e.matmul(
            PSF[3][:, _m:_m + 1], crow_s[:], gtmp[0:1, _m:_m + 1],
            start=True, stop=True), waits=[cv_])
        tp = v(lambda e, _m=mi: e.tensor_scalar(
            gcolq[:, 2 * _m:2 * _m + 1], PSF[3][:, _m:_m + 1], 0.5 / NELEM_W,
            0.5e-8, OP.mult, OP.add), waits=[mr, tprev])
        tng = v(lambda e, _m=mi: e.tensor_scalar(
            gcolq[:, 2 * _m + 1:2 * _m + 2], gcolq[:, 2 * _m:2 * _m + 1],
            -1.0, None, OP.mult), waits=[tp])
        thr[mi] = tng
        tprev = tng
    cons = {}
    stok = {}
    for si, (mi, i) in enumerate(slabs):
        WSRC, nslab, wid = srcs[mi]
        bi = si % 2
        tl = dma(fslabD[bi][:, 0:wid], WSRC[i * 128:(i + 1) * 128, :],
                 waits=[cons.get(si - 2), thr[mi] if i == 0 else None])
        s1 = sc(lambda e, _w=wid, _b=bi, _m=mi: e.activation(
            fslaD[_b][:, 0:_w], fslabD[_b][:, 0:_w], AF.Sign,
            bias=gcolq[:, 2 * _m + 1:2 * _m + 2]),
            waits=[tl, thr[mi], stok.get(si - 2)])
        s2 = sc(lambda e, _w=wid, _b=bi, _m=mi: e.activation(
            fslbD[_b][:, 0:_w], fslabD[_b][:, 0:_w], AF.Sign,
            bias=gcolq[:, 2 * _m:2 * _m + 1]))
        qa = v(lambda e, _w=wid, _b=bi: e.tensor_add(
            fslbD[_b][:, 0:_w], fslbD[_b][:, 0:_w], fslaD[_b][:, 0:_w]),
            waits=[s1, s2, stok.get(si - 2)])
        cons[si] = qa
        stok[si] = dma(q2w[mi][i * 128:(i + 1) * 128, :], fslbD[bi][:, 0:wid],
                       waits=[qa])
    wo_cons = None
    wo_st = {}
    for i in range(16):
        bi = i % 2
        tl = dma(fslabD[bi][:, 0:HID], WOp[i * 128:(i + 1) * 128, :],
                 waits=[wo_cons, cons.get(len(slabs) - 2 + bi)])
        wo_cons = v(lambda e, _b=bi: e.tensor_scalar(
            fslaD[_b][:, 0:HID], fslabD[_b][:, 0:HID], nw_s[:, 0:1], None,
            OP.mult), waits=[tl, wo_st.get(i - 2)])
        wo_st[i] = dma(woB[i * 128:(i + 1) * 128, :], fslaD[bi][:, 0:HID],
                       waits=[wo_cons])
    QDONE = [S.now(n) for n in LDS]

    q2w1r = q2w[0].rearrange("(k p) c -> p k c", p=128)
    q2w2r = q2w[1].rearrange("(k p) c -> p k c", p=128)
    fb = barrier_toks()
    hprev = None
    hst = None
    for cs in range(44):
        tla = dma(q2a.rearrange("p (k c) -> p k c", k=16),
                  q2w1r[:, :, cs * 128:(cs + 1) * 128],
                  waits=QDONE + [hprev] + (fb if cs == 0 else []))
        tlb = dma(q2b.rearrange("p (k c) -> p k c", k=16),
                  q2w2r[:, :, cs * 128:(cs + 1) * 128])
        for tb in range(4):
            off = 512 + tb * 512
            p0 = PSF[0][:, 0:512]
            p1 = PSF[1][:, 0:512]
            pairs0 = [(q2a[:, k * 128:(k + 1) * 128],
                       XNT[:, k * T_IN + off:k * T_IN + off + 512])
                      for k in range(16)]
            pairs1 = [(q2b[:, k * 128:(k + 1) * 128],
                       XNT[:, k * T_IN + off:k * T_IN + off + 512])
                      for k in range(16)]
            m0 = mm_acc(p0, pairs0, [tla, hprev] + (fb if cs == 0 else []))
            m1 = mm_acc(p1, pairs1, [tlb])
            sg = sc(lambda e, _tb=tb, _p=p0: e.activation(
                sigx[:, _tb * 512:(_tb + 1) * 512], _p, AF.Sigmoid, scale=0.5),
                waits=[m0, hprev])
            hprev = v(lambda e, _tb=tb, _p=p1: e.scalar_tensor_tensor(
                hslab[:, _tb * 512:(_tb + 1) * 512], _p, 0.25,
                sigx[:, _tb * 512:(_tb + 1) * 512], OP.mult, OP.mult),
                waits=[m1, sg, hst])
        hst = dma(hidB[cs * 128:(cs + 1) * 128, :], hslab[:], waits=[hprev])
    F1_LD = [S.now(n) for n in LDS]

    oSpr = oSp.rearrange("(h p) (c d) -> p c h d", p=128, d=128)
    nb = barrier_toks() + F1_LD + ATTN_LD
    ntok = None
    gf_prev = None
    on2 = None
    for oc in range(NCO):
        tl = dma(onrow.rearrange("p (g d) -> p g d", g=H), oSpr[:, oc],
                 waits=[ntok] + (nb if oc == 0 else []))
        gts = []
        for cb in range(4):
            ps = PSF[cb][:, 0:512]
            mg = tn(lambda e, _o=oc, _cb=cb, _p=ps: e.matmul(
                _p, out2T[:, _o * 128:(_o + 1) * 128],
                wg2b[:, _cb * 512:(_cb + 1) * 512], start=True, stop=True),
                waits=[gf_prev, o2tok] + (nb if oc == 0 else []))
            gf = v(lambda e, _cb=cb, _p=ps: e.tensor_add(
                gate_f[:, _cb * 512:(_cb + 1) * 512], _p,
                bg2_s[:, _cb * 512:(_cb + 1) * 512]),
                waits=[mg] + ([on2] if on2 else []))
            gts.append(sc(lambda e, _cb=cb: e.activation(
                gatesig[:, _cb * 512:(_cb + 1) * 512],
                gate_f[:, _cb * 512:(_cb + 1) * 512], AF.Sigmoid), waits=[gf]))
        gf_prev = gts[-1]
        sq = v(lambda e: e.tensor_mul(sqn[:], onrow[:], onrow[:]), waits=[tl])
        rs2 = v(lambda e: e.reduce_sum(
            rstd16[:, 0:16], sqn.rearrange("p (g d) -> p g d", g=H), AX.X),
            waits=[sq, S.now(SS)])
        rs3 = v(lambda e: e.tensor_scalar(rstd16[:, 0:16], rstd16[:, 0:16],
                                          1.0 / 128.0, 1e-5, OP.mult, OP.add),
                waits=[rs2])
        rs4 = sc(lambda e: e.activation(rstd16[:, 0:16], rstd16[:, 0:16],
                                        AF.Ln), waits=[rs3])
        rs5 = sc(lambda e: e.activation(rstd16[:, 0:16], rstd16[:, 0:16],
                                        AF.Exp, scale=-0.5), waits=[rs4])
        on1 = v(lambda e: e.tensor_mul(onrow[:], onrow[:], gatesig[:]),
                waits=gts + [rs2, sq])
        rb3 = rstd16[:, 0:16].unsqueeze(2).broadcast_to([128, 16, 128])
        on2 = v(lambda e: e.tensor_tensor(
            onrow.rearrange("p (g d) -> p g d", g=H),
            onrow.rearrange("p (g d) -> p g d", g=H), rb3, OP.mult),
            waits=[rs5, on1])
        ntok = dma(onT.rearrange("p (j t) -> p j t", j=16)
                   [:, :, oc * 128:(oc + 1) * 128],
                   onrow[:], waits=[on2], transpose=True)
        if oc == 0:
            dbg_tap("gatesig", gatesig, [gts[-1]])
            dbg_tap("rstd16", rstd16, [rs5])
            dbg_tap("onrow0", onrow, [on2])
    NORM_LD = [S.now(n) for n in LDS]
    dbg_tap("onT", onT, NORM_LD)

    q2w3r = q2w[2].rearrange("(k p) c -> p k c", p=128)
    woBr = woB.rearrange("(k p) c -> p k c", p=128)
    w3cbD = [carve(AR2, 45056, 44 * 256, BF16), carve(AR2, 67584, 44 * 256, BF16)]
    wocbD = [carve(AR1, 65536, 16 * 256, BF16), carve(AR1, 73728, 16 * 256, BF16)]
    xstg = [carve(AR1, 81920 + 1024 * i, 256, F32) for i in range(4)]
    outr2 = [carve(AR2, 90112 + 1024 * i, 256, F32) for i in range(4)]
    f2b = barrier_toks() + NORM_LD
    NCB = 8
    fprev = {}
    yst = {}
    ld3 = {}
    ldo = {}
    ldx = {}
    ldh = None
    for tq in range(4):
        ldh = dma(hidq.rearrange("p (k c) -> p k c", k=44),
                  hidB.rearrange("(k p) c -> p k c", p=128)
                  [:, :, tq * 512:(tq + 1) * 512],
                  waits=[fprev.get((tq - 1, NCB - 1, 3))]
                  + (f2b if tq == 0 else []))
        for cbk in range(NCB):
            bi = cbk % 2
            ld3[cbk] = dma(w3cbD[bi].rearrange("p (k c) -> p k c", k=44),
                           q2w3r[:, :, cbk * 256:(cbk + 1) * 256],
                           waits=[fprev.get((tq, cbk - 2, 3)),
                                  fprev.get((tq - 1, NCB - 1, 3))]
                           + (f2b if tq == 0 and cbk < 2 else []))
            ldo[cbk] = dma(wocbD[bi].rearrange("p (k c) -> p k c", k=16),
                           woBr[:, :, cbk * 256:(cbk + 1) * 256],
                           waits=[])
            for tc in range(4):
                co = tq * 4 + tc
                ps = PSF[tc % 2 + 2 * bi][:, 0:256]
                ldx[(cbk, tc)] = dma(
                    xstg[tc][:], x_in[HALO + co * 128:HALO + (co + 1) * 128,
                                      cbk * 256:(cbk + 1) * 256],
                    waits=[fprev.get((tq, cbk - 1, tc)),
                           fprev.get((tq - 1, NCB - 1, tc) if cbk == 0
                                     else (tq, cbk - 1, tc))])
                pairs = [(hidq[:, k * 512 + tc * 128:k * 512 + tc * 128 + 128],
                          w3cbD[bi][:, k * 256:(k + 1) * 256])
                         for k in range(44)]
                pairs += [(onT[:, j * T_OWN + co * 128:j * T_OWN + co * 128
                               + 128], wocbD[bi][:, j * 256:(j + 1) * 256])
                          for j in range(16)]
                mt = mm_acc(ps, pairs,
                            [ldh, ld3[cbk], ldo[cbk],
                             fprev.get((tq, cbk - 2, tc)),
                             fprev.get((tq, cbk, tc - 2))])
                fprev[(tq, cbk, tc)] = v(
                    lambda e, _p=ps, _t=tc: e.tensor_add(
                        outr2[_t][:], _p, xstg[_t][:]),
                    waits=[mt, ldx[(cbk, tc)],
                           yst.get(((cbk - 1) % NCB, tc))])
                yst[(cbk, tc)] = dma(
                    y[co * 128:(co + 1) * 128, cbk * 256:(cbk + 1) * 256],
                    outr2[tc][:, 0:256], waits=[fprev[(tq, cbk, tc)]])
    S.op("sync", lambda e: None, waits=[S.now(n) for n in LDS])
    S.op("gpsimd", lambda e: None, waits=[S.now(n) for n in GDS])

    from contextlib import ExitStack
    es = ExitStack()
    sems = {n: es.enter_context(nc.semaphore(n))
            for n in (["SV", "SS", "ST"] + ["LD%d" % i for i in range(8)]
                      + ["GD%d" % i for i in range(4)])}
    with nc.Block() as blk:
        @blk.sync
        def _(eng):
            S.emit("sync", eng, sems)

        @blk.vector
        def _(eng):
            S.emit("vector", eng, sems)

        @blk.scalar
        def _(eng):
            S.emit("scalar", eng, sems)

        @blk.tensor
        def _(eng):
            S.emit("tensor", eng, sems)

        @blk.gpsimd
        def _(eng):
            S.emit("gpsimd", eng, sems)
    es.close()
    return nc


_NC_CACHE = None


def kernel(L0_attn_hidden_states, L0_attn_Wq, L0_attn_Wk, L0_attn_Wv, L0_attn_Ww,
           L0_attn_cq, L0_attn_ck, L0_attn_cv, L0_attn_Wf1, L0_attn_Wf2,
           L0_attn_Wb, L0_attn_Wg1, L0_attn_Wg2, L0_attn_bg2, L0_attn_norm_w,
           L0_attn_Wo, L0_ffn_w1, L0_ffn_w2, L0_ffn_w3):
    global _NC_CACHE
    x = np.asarray(L0_attn_hidden_states, dtype=np.float32)
    B, T, _ = x.shape
    assert (B, T) == (2, 8192)

    shared = {
        "Wq": np.ascontiguousarray(np.asarray(L0_attn_Wq, np.float32)),
        "Wk": np.ascontiguousarray(np.asarray(L0_attn_Wk, np.float32)),
        "Wv": np.ascontiguousarray(np.asarray(L0_attn_Wv, np.float32)),
        "Ww": np.ascontiguousarray(np.asarray(L0_attn_Ww, np.float32)),
        "Wo": np.ascontiguousarray(np.asarray(L0_attn_Wo, np.float32)),
        "Wf1": np.ascontiguousarray(np.asarray(L0_attn_Wf1, np.float32)),
        "Wf2": np.ascontiguousarray(np.asarray(L0_attn_Wf2, np.float32)),
        "Wb": np.ascontiguousarray(np.asarray(L0_attn_Wb, np.float32)),
        "Wg1": np.ascontiguousarray(np.asarray(L0_attn_Wg1, np.float32)),
        "Wg2": np.ascontiguousarray(np.asarray(L0_attn_Wg2, np.float32)),
        "cq": np.ascontiguousarray(np.asarray(L0_attn_cq, np.float32)),
        "ck": np.ascontiguousarray(np.asarray(L0_attn_ck, np.float32)),
        "cv": np.ascontiguousarray(np.asarray(L0_attn_cv, np.float32)),
        "w1T": np.ascontiguousarray(np.asarray(L0_ffn_w1, np.float32).T),
        "w2T": np.ascontiguousarray(np.asarray(L0_ffn_w2, np.float32).T),
        "w3T": np.ascontiguousarray(np.asarray(L0_ffn_w3, np.float32).T),
        "bg2r": np.ascontiguousarray(
            np.broadcast_to(np.asarray(L0_attn_bg2, np.float32), (128, HID))),
        "nw_col": np.ascontiguousarray(
            np.asarray(L0_attn_norm_w, np.float32).reshape(128, 1)),
        "cb": np.ascontiguousarray(np.concatenate(
            [np.eye(128, dtype=BFD),
             np.triu(np.ones((128, 128), dtype=BFD)),
             np.full((128, 128), 0.5, dtype=BFD)], axis=1)),
        "cf": np.ascontiguousarray(np.tile(
            np.array([1e-5, 1e-6, 1.0, 0.0, 0.0, 0.0, 0.0, 0.0],
                     np.float32), (128, 1))),
        "crow": np.ones((1, 128), np.float32),
    }

    in_maps = []
    for c in range(8):
        b, qi = c // 4, c % 4
        lo = qi * T_OWN
        if qi == 0:
            xs = np.concatenate([np.zeros((HALO, HID), np.float32),
                                 x[b, 0:T_OWN]], axis=0)
        else:
            xs = x[b, lo - HALO: lo + T_OWN]
        m = dict(shared)
        m["x_in"] = np.ascontiguousarray(xs)
        in_maps.append(m)

    if _NC_CACHE is None:
        _NC_CACHE = build_program(debug=bool(globals().get("_DBG_MODE")))
    res = run_bass_kernel_spmd(_NC_CACHE, in_maps, core_ids=list(range(8)))
    if globals().get("_DBG_MODE"):
        globals()["_DBG_RES"] = res

    out = np.empty((2, 8192, HID), np.float32)
    for c in range(8):
        b, qi = c // 4, c % 4
        out[b, qi * T_OWN:(qi + 1) * T_OWN] = res.results[c]["y"]
    return out


if __name__ == "__main__":
    nc = build_program()
    print("build ok, instructions:", nc.n_instructions())


# version 18
# speedup vs baseline: 1.2213x; 1.0197x over previous
import numpy as np
import ml_dtypes
import concourse.bass as bass
import concourse.mybir as mybir
from concourse.bass_utils import run_bass_kernel_spmd
from concourse.alu_op_type import AluOpType as OP

F32 = mybir.dt.float32
BF16 = mybir.dt.bfloat16
AF = mybir.ActivationFunctionType
AX = mybir.AxisListType
BFD = ml_dtypes.bfloat16

HID = 2048
H = 16
T_OWN = 2048
HALO = 512
T_IN = T_OWN + HALO
NCI = T_IN // 128
NCC = 19
NCO = 16
FFH = 5632
SCALE = 128.0 ** -0.5
NELEM_W = float(HID * FFH)


class Sch:
    ENGS = ("sync", "vector", "scalar", "tensor", "gpsimd")

    def __init__(self):
        self.ops = {e: [] for e in self.ENGS}
        self.cnt = {}
        self.seen = {e: {} for e in self.ENGS}

    def sem(self, name):
        self.cnt.setdefault(name, 0)
        return name

    DMA_SEMS = tuple("LD%d" % i for i in range(8))

    def op(self, eng, fn, waits=(), inc=None, amt=1):
        w = []
        for t in waits:
            if t is None:
                continue
            s, vv = t
            if s is None or vv is None or vv <= 0:
                continue
            if self.seen[eng].get(s, -1) >= vv:
                continue
            self.seen[eng][s] = vv
            w.append((s, vv))
        tgt = None
        if inc is not None:
            self.cnt[inc] += amt
            tgt = self.cnt[inc]
        self.ops[eng].append((w, fn, inc, amt))
        return (inc, tgt) if inc is not None else None

    def now(self, sem):
        return (sem, self.cnt.get(sem, 0))

    def emit(self, eng_name, eng, sems):
        for w, fn, inc, amt in self.ops[eng_name]:
            for s, vv in w:
                eng.wait_ge(sems[s], vv)
            ins = fn(eng)
            if inc is not None:
                if ins is None:
                    ins = eng.nop()
                ins.then_inc(sems[inc], amt)


def build_program(debug=False):
    nc = bass.Bass("TRN2", target_bir_lowering=False)

    def din(name, shape, dt=F32):
        return nc.dram_tensor(name, list(shape), dt, kind="ExternalInput").ap()

    x_in = din("x_in", (T_IN, HID))
    WQ = din("Wq", (HID, HID)); WK = din("Wk", (HID, HID))
    WV = din("Wv", (HID, HID)); WW = din("Ww", (HID, HID))
    WOp = din("Wo", (HID, HID))
    WF1 = din("Wf1", (HID, 128)); WF2 = din("Wf2", (128, HID))
    WB = din("Wb", (HID, H))
    WG1 = din("Wg1", (HID, 128)); WG2 = din("Wg2", (128, HID))
    CQ = din("cq", (HID, 4)); CK = din("ck", (HID, 4)); CV = din("cv", (HID, 4))
    W1T = din("w1T", (HID, FFH)); W2T = din("w2T", (HID, FFH))
    W3T = din("w3T", (FFH, HID))
    BG2R = din("bg2r", (128, HID))
    NWC = din("nw_col", (128, 1))
    CB = din("cb", (128, 3 * 128), BF16)
    CF = din("cf", (128, 8))
    CROW = din("crow", (1, 128))

    y = nc.dram_tensor("y", [T_OWN, HID], F32, kind="ExternalOutput").ap()

    def dint(name, shape, dt):
        return nc.dram_tensor(name, list(shape), dt, kind="Internal").ap()

    def dmaybe(name, shape, dt):
        kind = "ExternalOutput" if debug else "Internal"
        return nc.dram_tensor(name, list(shape), dt, kind=kind).ap()

    q2w = [dmaybe("q2w1", (HID, FFH), BF16), dmaybe("q2w2", (HID, FFH), BF16),
           dmaybe("q2w3", (FFH, HID), BF16)]
    hidB = dmaybe("hidB", (FFH, T_OWN), BF16)
    woB = dint("woB", (HID, HID), BF16)
    oSp = dmaybe("oSp", (H * 128, NCO * 128), BF16)

    DBG = {}

    def dbg_tap(name, ap, waits):
        if not debug:
            return
        if not isinstance(ap, bass.AP):
            ap = ap[:]
        t = nc.dram_tensor("dbg_" + name, [ap.shape[0], ap.shape[1]],
                           ap.dtype, kind="ExternalOutput").ap()
        DBG[name] = dma(t, ap, waits=waits)

    sbt = nc.alloc_sbuf_tensor

    cb_s = sbt("cb_s", [128, 3 * 128], BF16)
    IDENT = cb_s[:, 0:128]
    UTC = cb_s[:, 128:256]
    HALFM = cb_s[:, 256:384]
    cf_s = sbt("cf_s", [128, 8], F32)
    EPS_LN = cf_s[:, 0:1]; EPS_L2 = cf_s[:, 1:2]; ONE_F = cf_s[:, 2:3]
    crow_s = sbt("crow_s", [1, 128], F32)
    ones_bf = sbt("ones_bf", [128, 16], BF16)
    ONES_COL = ones_bf[:, 0:1]
    nw_s = sbt("nw_s", [128, 1], F32)
    bg2_s = sbt("bg2_s", [128, HID], BF16)
    wf2b = sbt("wf2b", [128, HID], BF16)
    wg2b = sbt("wg2b", [128, HID], BF16)
    cqall = sbt("cqall", [128, 3 * 64], F32)
    beta_s = sbt("beta_s", [128, NCI * H], F32)
    out1T = sbt("out1T", [128, T_IN], BF16)
    out2T = sbt("out2T", [128, T_OWN], BF16)
    lnst = sbt("lnst", [128, 8], F32)
    gcols = sbt("gcols", [128, 32], F32)
    dcols = sbt("dcols", [128, 32], F32)
    ercols = sbt("ercols", [128, 32], F32)
    sscol = sbt("sscol", [128, 48], F32)
    brcol = sbt("brcol", [128, 32], F32)
    brS = sbt("brS", [128, 32], F32)
    ncol = sbt("ncol", [128, 8], F32)
    rstd16 = sbt("rstd16", [128, 16], F32)
    gsum = sbt("gsum", [128, 8], F32)
    gcolq = sbt("gcolq", [128, 8], F32)
    gtmp = sbt("gtmp", [1, 8], F32)
    SkT = sbt("SkT", [128, 128], BF16)
    Sv = sbt("Sv", [128, 128], BF16)
    SkEP = [sbt("SkE%d" % i, [128, 128], BF16) for i in range(2)]
    SkEdP = [sbt("SkEd%d" % i, [128, 128], BF16) for i in range(2)]
    QKmP = [sbt("QKm%d" % i, [128, 128], BF16) for i in range(2)]
    stilP = [sbt("stil%d" % i, [128, 128], BF16) for i in range(2)]
    a_eP = [sbt("a_e%d" % i, [128, 128], BF16) for i in range(2)]
    aTtP = [sbt("aTt%d" % i, [128, 128], BF16) for i in range(2)]
    aHtP = [sbt("aHt%d" % i, [128, 128], BF16) for i in range(2)]
    PTmP = [sbt("PTm%d" % i, [128, 128], BF16) for i in range(2)]

    AR1 = sbt("AR1", [128, 43008], BF16)
    AR2 = sbt("AR2", [128, 47232], BF16)

    def carve(ar, off_bytes, cols, dt):
        assert off_bytes % 4 == 0
        c2 = off_bytes // 2
        if dt == BF16:
            assert (off_bytes + 2 * cols) <= 2 * ar.shape[1]
            return ar[:, c2:c2 + cols]
        assert (off_bytes + 4 * cols) <= 2 * ar.shape[1]
        return ar[:, c2:c2 + 2 * cols].bitcast(F32)

    XNT = carve(AR1, 0, 16 * T_IN, BF16)
    onT = carve(AR1, 0, 16 * T_OWN, BF16)
    wocb = carve(AR1, 65536, 16 * 512, BF16)
    wstage = carve(AR1, 81920, 512, F32)
    outrow = carve(AR1, 83968, 512, F32)

    xrow = carve(AR2, 0, HID, F32)
    sqrow = carve(AR2, 8192, HID, F32)
    xnrow = carve(AR2, 16384, HID, BF16)
    _off = [0]

    def hc(cols, dt):
        ap = carve(AR2, _off[0], cols, dt)
        _off[0] += cols * (2 if dt == BF16 else 4)
        _off[0] = (_off[0] + 63) & ~63
        return ap

    wcb = [hc(HID, BF16), hc(HID, BF16)]
    preconv = hc(T_IN + 4 + 540, BF16)
    projbuf = hc(T_IN, BF16)
    E1 = carve(AR2, 0, NCC * 128, BF16)
    E2 = carve(AR2, 4864, NCC * 128, BF16)
    ERb = carve(AR2, 9728, NCC * 128, BF16)
    expRr = carve(AR2, 14592, NCC * 128, BF16)
    kT_s = hc(T_IN, BF16)
    qT_s = hc(T_IN, BF16)
    K_tm = hc(NCC * 128, BF16)
    V_tm = hc(NCC * 128, BF16)
    Vh_tm = hc(NCC * 128, BF16)
    W_tm = hc(NCC * 128, BF16)
    g_sm = hc(NCC * 128, BF16)
    wt_sm = hc(NCC * 128, BF16)
    wh_sm = hc(NCC * 128, BF16)
    grow = hc(512, F32)
    zrow = hc(512, F32)
    growB = hc(512, F32)
    zrowB = hc(512, F32)
    vbuf = hc(T_IN, BF16)
    obuf = carve(AR1, 81920, NCO * 128, BF16)
    Sk_all = hc(NCO * 128, BF16)
    Sv_all = hc(NCO * 128, BF16)
    assert _off[0] <= 94464, _off[0]
    fslab = carve(AR2, 0, FFH, F32)
    fsl_a = carve(AR2, 22528, FFH, BF16)
    fsl_b = carve(AR2, 33792, FFH, BF16)
    q2aD = [carve(AR2, 0, HID, BF16), carve(AR2, 16384, HID, BF16)]
    q2bD = [carve(AR2, 4096, HID, BF16), carve(AR2, 20480, HID, BF16)]
    sigx = carve(AR2, 8192, T_OWN, BF16)
    hslab = carve(AR2, 12288, T_OWN, BF16)
    onrow = carve(AR2, 0, HID, BF16)
    gate_f = carve(AR2, 4096, HID, F32)
    gatesig = carve(AR2, 12288, HID, BF16)
    sqn = carve(AR2, 16384, HID, BF16)
    hidq = carve(AR2, 0, 44 * 512, BF16)
    w3cb = carve(AR2, 45056, 44 * 512, BF16)
    xstage = carve(AR2, 90112, 512, F32)

    PSF = [nc.alloc_psum_tensor("PS%d" % i, [128, 512], F32) for i in range(6)]
    PB6 = nc.alloc_psum_tensor("PB6", [128, 1024], BF16)
    PB7 = nc.alloc_psum_tensor("PB7", [128, 1024], BF16)
    TRA = PB6[:, 0:128]
    TRB = PB7[:, 0:128]
    TRC = PB7[:, 512:640]

    S = Sch()
    LDS = [S.sem("LD%d" % i) for i in range(8)]
    GDS = [S.sem("GD%d" % i) for i in range(4)]
    SV = S.sem("SV"); SS = S.sem("SS"); ST = S.sem("ST")
    _rr = [0]
    _gr = [0]

    def dma(out_, in_, waits=(), transpose=False):
        if transpose:
            waits = list(waits) + [S.now(n) for n in LDS]
        sem = LDS[_rr[0] % 8]
        _rr[0] += 1
        if S.cnt[sem] > 0:
            S.op("sync", lambda e: None, waits=[S.now(sem)])

        def f(e, _o=out_, _i=in_, _t=transpose):
            return e.dma_start(out=_o, in_=_i, transpose=_t)
        tok = S.op("sync", f, waits=waits, inc=sem, amt=16)
        if transpose:
            for n in LDS:
                S.seen["sync"].pop(n, None)
            S.op("sync", lambda e: None, waits=[S.now(sem)])
        return tok

    def gdma(out_, in_, waits=()):
        sem = GDS[_gr[0] % 4]
        _gr[0] += 1
        if S.cnt[sem] > 0:
            S.op("gpsimd", lambda e: None, waits=[S.now(sem)])

        def f(e, _o=out_, _i=in_):
            return e.dma_start(out=_o, in_=_i)
        return S.op("gpsimd", f, waits=waits, inc=sem, amt=16)

    def v(fn, waits=()):
        return S.op("vector", fn, waits=waits, inc=SV)

    def sc(fn, waits=()):
        return S.op("scalar", fn, waits=waits, inc=SS)

    def tn(fn, waits=()):
        return S.op("tensor", fn, waits=waits, inc=ST)

    def barrier_toks():
        return ([S.now(SV), S.now(SS), S.now(ST)]
                + [S.now(n) for n in LDS] + [S.now(n) for n in GDS])

    def mm_acc(ps_ap, pairs, waits_first):
        tok = None
        n = len(pairs)
        for k, (lt, rh) in enumerate(pairs):
            tok = tn(lambda e, _l=lt, _r=rh, _s=(k == 0), _e=(k == n - 1),
                     _p=ps_ap: e.matmul(_p, _l, _r, start=_s, stop=_e),
                     waits=waits_first if k == 0 else ())
        return tok

    dma(cb_s[:], CB)
    dma(cf_s[:], CF)
    dma(crow_s[:], CROW)
    dma(nw_s[:], NWC)
    v(lambda e: e.memset(ones_bf[:], 1.0))
    v(lambda e: e.memset(PSF[5][64:128, 0:64], 0.0))
    v(lambda e: e.memset(PSF[2][64:128, 0:64], 0.0))
    bgtok = gdma(bg2_s[:], BG2R)
    dma(cqall.rearrange("p (s k w) -> p s k w", s=3, w=4)[:, 0],
        CQ.rearrange("(k p) w -> p k w", p=128))
    dma(cqall.rearrange("p (s k w) -> p s k w", s=3, w=4)[:, 1],
        CK.rearrange("(k p) w -> p k w", p=128))
    dma(cqall.rearrange("p (s k w) -> p s k w", s=3, w=4)[:, 2],
        CV.rearrange("(k p) w -> p k w", p=128))
    wf2tok = gdma(wf2b[:], WF2)
    wg2tok = gdma(wg2b[:], WG2)

    consts_rdy = [S.now(n) for n in LDS] + [wg2tok, wf2tok, bgtok]
    prev_used = wg2tok
    ttok = None
    for i in range(NCI):
        tl = dma(xrow[:], x_in[i * 128:(i + 1) * 128, :],
                 waits=[prev_used] + (consts_rdy if i == 0 else []))
        t1 = v(lambda e: e.reduce_sum(lnst[:, 0:1], xrow[:], AX.X), waits=[tl])
        t2 = sc(lambda e: e.activation(sqrow[:], xrow[:], AF.Square,
                                       accum_out=lnst[:, 1:2]), waits=[tl])
        t3 = v(lambda e: e.tensor_scalar(lnst[:, 2:3], lnst[:, 0:1], 1.0 / HID,
                                         None, OP.mult), waits=[t1])
        t4 = v(lambda e: e.tensor_scalar(lnst[:, 3:4], lnst[:, 1:2], 1.0 / HID,
                                         None, OP.mult), waits=[t2])
        t5 = v(lambda e: e.tensor_mul(lnst[:, 4:5], lnst[:, 2:3], lnst[:, 2:3]),
               waits=[t3])
        t6 = v(lambda e: e.tensor_sub(lnst[:, 5:6], lnst[:, 3:4], lnst[:, 4:5]),
               waits=[t4, t5])
        t7 = sc(lambda e: e.activation(lnst[:, 6:7], lnst[:, 5:6], AF.Ln,
                                       bias=EPS_LN), waits=[t6])
        t8 = sc(lambda e: e.activation(lnst[:, 7:8], lnst[:, 6:7], AF.Exp,
                                       scale=-0.5), waits=[t7])
        t9 = v(lambda e: e.tensor_scalar(xnrow[:], xrow[:], lnst[:, 2:3],
                                         lnst[:, 7:8], OP.subtract, OP.mult),
               waits=[t8, t3, ttok])
        ttok = dma(XNT.rearrange("p (k t) -> p k t", k=16)
                   [:, :, i * 128:(i + 1) * 128],
                   xnrow[:], waits=[t9], transpose=True)
        prev_used = t9
    XN_RDY = ttok
    dbg_tap("XNT", XNT, [XN_RDY])

    cwf = gdma(wcb[0].rearrange("p (k c) -> p k c", k=16),
               WF1.rearrange("(k p) c -> p k c", p=128), waits=[XN_RDY])
    cwg = gdma(wcb[1].rearrange("p (k c) -> p k c", k=16),
               WG1.rearrange("(k p) c -> p k c", p=128), waits=[XN_RDY])
    evtoks = {}
    for tb in range(5):
        ps = PSF[tb % 2][:, 0:512]
        pairs = [(wcb[0][:, k * 128:(k + 1) * 128],
                  XNT[:, k * T_IN + tb * 512:k * T_IN + tb * 512 + 512])
                 for k in range(16)]
        mt = mm_acc(ps, pairs, [cwf, XN_RDY, evtoks.get(tb - 2)])
        evtoks[tb] = v(lambda e, _tb=tb, _ps=ps: e.tensor_copy(
            out1T[:, _tb * 512:(_tb + 1) * 512], _ps), waits=[mt])
    o1tok = S.now(SV)
    f1mm = mt
    evtoks = {}
    for tb in range(4):
        ps = PSF[2 + tb % 2][:, 0:512]
        off = 512 + tb * 512
        pairs = [(wcb[1][:, k * 128:(k + 1) * 128],
                  XNT[:, k * T_IN + off:k * T_IN + off + 512]) for k in range(16)]
        mt = mm_acc(ps, pairs, [cwg, evtoks.get(tb - 2)])
        evtoks[tb] = v(lambda e, _tb=tb, _ps=ps: e.tensor_copy(
            out2T[:, _tb * 512:(_tb + 1) * 512], _ps), waits=[mt])
    o2tok = S.now(SV)
    cwb = gdma(wcb[0].rearrange("p (k c) -> p k c", c=16)[:, 0:16, :],
               WB.rearrange("(k p) c -> p k c", p=128), waits=[f1mm])
    bprev = None
    for i in range(NCI):
        ps = PSF[4][:, 0:16]
        pairs = [(XNT[:, k * T_IN + i * 128:k * T_IN + i * 128 + 128],
                  wcb[0][:, k * 16:(k + 1) * 16]) for k in range(16)]
        mt = mm_acc(ps, pairs, [cwb, bprev])
        bprev = sc(lambda e, _i=i, _ps=ps: e.activation(
            beta_s[:, _i * 16:(_i + 1) * 16], _ps, AF.Sigmoid), waits=[mt])
    beta_tok = bprev
    dbg_tap("out1T", out1T, [o1tok])
    dbg_tap("out2T", out2T, [o2tok])
    dbg_tap("beta", beta_s, [beta_tok])

    beta3 = beta_s.rearrange("p (c h) -> p c h", h=H)
    zpad = v(lambda e: e.memset(preconv[:, 0:4], 0.0), waits=[beta_tok])

    TRREG = [(PB6, 0), (PB7, 0), (PB6, 128), (PB7, 128)]
    trst = {"rot": 0, "ev": {}}

    def pe_transpose_seq(src, n, dst, waits_first):
        ev = None
        for i in range(n):
            bank, off = TRREG[trst["rot"] % 4]
            reg = bank[:, off:off + 128]
            w = [trst["ev"].get(trst["rot"] - 2)]
            if i == 0:
                w += list(waits_first)
            tt = tn(lambda e, _s=src[:, 128 + i * 128:128 + (i + 1) * 128],
                    _r=reg: e.transpose(_r, _s, IDENT), waits=w)
            ev = v(lambda e, _i=i, _r=reg, _d=dst: e.tensor_copy(
                _d[:, _i * 128:(_i + 1) * 128], _r), waits=[tt])
            trst["ev"][trst["rot"]] = ev
            trst["rot"] += 1
        return ev

    WSEQ = [(WW, 2), (WV, 2), (WK, 1), (WQ, 0)]
    PDST = [projbuf, vbuf, kT_s, qT_s]
    PTM = [W_tm, V_tm, K_tm, None]

    GRPS = [(0, 4), (4, 4), (8, 4), (12, 4), (16, 3)]
    spill_prev = None
    prev_evs = {}
    for h in range(H):
        hb = barrier_toks()
        ldw = [None] * 4
        for j in (0, 1):
            ldw[j] = gdma(wcb[j].rearrange("p (k c) -> p k c", k=16),
                          WSEQ[j][0].rearrange("(k p) c -> p k c", p=128)
                          [:, :, h * 128:(h + 1) * 128], waits=hb)
        trtok = [None] * 4
        silu_t = [None] * 4
        prev_tap3 = None
        for j in range(4):
            buf = wcb[j % 2]
            evs = {}
            mt = None
            for tb in range(5):
                ps = PSF[4 + tb % 2][:, 0:512]
                pairs = [(buf[:, k * 128:(k + 1) * 128],
                          XNT[:, k * T_IN + tb * 512:k * T_IN + tb * 512 + 512])
                         for k in range(16)]
                bank_prev = (evs.get(tb - 2)
                             or prev_evs.get(4 if tb % 2 == 0 else 3))
                mt = mm_acc(ps, pairs,
                            [ldw[j], bank_prev]
                            + (hb if (h == 0 and j == 0 and tb == 0) else []))
                evs[tb] = v(lambda e, _tb=tb, _ps=ps: e.tensor_copy(
                    preconv[:, 4 + _tb * 512:4 + (_tb + 1) * 512], _ps),
                    waits=[mt] + ([prev_tap3] if tb == 0 else []))
            if j + 2 < 4:
                ldw[j + 2] = gdma(wcb[j % 2].rearrange("p (k c) -> p k c", k=16),
                                  WSEQ[j + 2][0]
                                  .rearrange("(k p) c -> p k c", p=128)
                                  [:, :, h * 128:(h + 1) * 128], waits=[mt])
            base = WSEQ[j][1] * 64 + h * 4
            cc = [cqall[:, base + i:base + i + 1] for i in range(4)]
            dest = PDST[j]
            t = v(lambda e, _d=dest: e.tensor_scalar(
                _d[:], preconv[:, 1:1 + T_IN], cc[0], None, OP.mult),
                waits=[evs[4], zpad, S.now(ST)])
            for i in (1, 2, 3):
                t = v(lambda e, _i=i, _d=dest: e.scalar_tensor_tensor(
                    _d[:], preconv[:, 1 + _i:1 + _i + T_IN], cc[_i], _d[:],
                    OP.mult, OP.add), waits=[t])
            prev_tap3 = t
            silu_t[j] = sc(lambda e, _d=dest: e.activation(_d[:], _d[:],
                                                           AF.Silu), waits=[t])
            if j > 0 and PTM[j - 1] is not None:
                trtok[j - 1] = pe_transpose_seq(
                    PDST[j - 1], NCC, PTM[j - 1],
                    [silu_t[j - 1]] + ([S.now(SV)] if j == 1 else []))
            prev_evs = evs
        trtok[2] = pe_transpose_seq(PDST[2], NCC, PTM[2], [silu_t[2]])
        qtok = silu_t[3]
        trW, trV, trK = trtok[0], trtok[1], trtok[2]
        gtoks = {}
        gevs = []
        gbar = [S.now(SV), S.now(SS), S.now(ST)]
        for tb in range(5):
            ps = PSF[2 + tb % 2][:, 0:512]
            half = (tb % 2) * 256
            zr = zrow[:, half:half + 256]
            zr2 = zrow[:, half + 128:half + 128 + 128]
            gr = grow[:, half:half + 256]
            mt = mm_acc(ps, [(wf2b[:, h * 128:(h + 1) * 128],
                              out1T[:, tb * 512:(tb + 1) * 512])],
                        [wf2tok, gtoks.get(tb - 2)] + (gbar if tb < 2 else []))
            e1 = sc(lambda e, _ps=ps, _g=gr: e.activation(_g[:, 0:512]
                    if False else _g, _ps[:, 0:256], AF.Abs),
                    waits=[mt, gtoks.get(tb - 2)])
            e1b = sc(lambda e, _ps=ps, _z=zr: e.activation(
                _z, _ps[:, 256:512], AF.Abs), waits=[e1])
            e2 = sc(lambda e, _g=gr: e.activation(_g, _g, AF.Exp, scale=-1.0),
                    waits=[e1])
            e2b = sc(lambda e, _z=zr: e.activation(_z, _z, AF.Exp, scale=-1.0),
                     waits=[e1b])
            e3 = sc(lambda e, _g=gr: e.activation(_g, _g, AF.Ln, bias=ONE_F),
                    waits=[e2])
            e3b = sc(lambda e, _z=zr: e.activation(_z, _z, AF.Ln, bias=ONE_F),
                     waits=[e2b])
            e4 = v(lambda e, _tb=tb, _ps=ps, _g=gr: e.scalar_tensor_tensor(
                projbuf[:, _tb * 512:_tb * 512 + 256], _ps[:, 0:256], 0.0,
                _g, OP.min, OP.subtract), waits=[e3])
            e4b = v(lambda e, _tb=tb, _ps=ps, _z=zr: e.scalar_tensor_tensor(
                projbuf[:, _tb * 512 + 256:(_tb + 1) * 512], _ps[:, 256:512],
                0.0, _z, OP.min, OP.subtract), waits=[e3b])
            gtoks[tb] = e4b
            gevs += [e4, e4b]
        trG = pe_transpose_seq(projbuf, NCC, g_sm, gevs)

        dtoks = []
        cp_prev = zz_prev = se1_prev = se3_prev = se4_prev = None
        for g0, gn in GRPS:
            gw = gn * 128
            mg = None
            for j in range(gn):
                mg = tn(lambda e, _j=j, _c=g0 + j: e.matmul(
                    PSF[0][:, _j * 128:(_j + 1) * 128], UTC,
                    g_sm[:, _c * 128:(_c + 1) * 128], start=True, stop=True),
                    waits=[trG, cp_prev, se1_prev, S.now(SV)] if j == 0 else ())
            mr = None
            for j in range(gn):
                mr = tn(lambda e, _j=j, _c=g0 + j: e.matmul(
                    PSF[1][:, _j * 128:(_j + 1) * 128], HALFM,
                    g_sm[:, _c * 128:(_c + 1) * 128], start=True, stop=True),
                    waits=[zz_prev, se4_prev] if j == 0 else ())
            cp = v(lambda e, _w=gw: e.tensor_copy(grow[:, 0:_w], PSF[0][:, 0:_w]),
                   waits=[mg, S.now(SS)])
            se1 = sc(lambda e, _w=gw, _o=g0 * 128: e.activation(
                E1[:, _o:_o + _w], PSF[0][:, 0:_w], AF.Exp),
                waits=[mg, S.now(SV)])
            zz = v(lambda e, _w=gw: e.tensor_sub(zrow[:, 0:_w], PSF[1][:, 0:_w],
                                                 grow[:, 0:_w]),
                   waits=[cp, mr, se3_prev])
            zh = v(lambda e, _w=gw: e.tensor_add(grow[:, 0:_w], zrow[:, 0:_w],
                                                 PSF[1][:, 0:_w]),
                   waits=[zz, se4_prev, se1])
            zc = v(lambda e, _w=gw: e.tensor_scalar(zrow[:, 0:_w], zrow[:, 0:_w],
                                                    80.0, -80.0, OP.min, OP.max),
                   waits=[zh])
            se2 = sc(lambda e, _w=gw, _o=g0 * 128: e.activation(
                E2[:, _o:_o + _w], zrow[:, 0:_w], AF.Exp, scale=-1.0), waits=[zc])
            se3 = sc(lambda e, _w=gw, _o=g0 * 128: e.activation(
                ERb[:, _o:_o + _w], zrow[:, 0:_w], AF.Exp))
            se4 = sc(lambda e, _w=gw, _o=g0 * 128: e.activation(
                expRr[:, _o:_o + _w], grow[:, 0:_w], AF.Exp), waits=[zh])
            cp_prev, zz_prev, se1_prev, se3_prev, se4_prev = cp, zc, se1, se3, se4
            dtoks += [se1, se2, se3, se4]
        mgc = None
        for c in range(NCC):
            mgc = tn(lambda e, _c=c: e.matmul(PSF[2][:, _c:_c + 1],
                                              g_sm[:, _c * 128:(_c + 1) * 128],
                                              ONES_COL, start=True, stop=True),
                     waits=[trG, S.now(SV), S.now(SS)] if c == 0 else ())
        cgc = v(lambda e: e.tensor_copy(gcols[:, 0:NCC], PSF[2][:, 0:NCC]),
                waits=[mgc, S.now(SS)])
        sdc = sc(lambda e: e.activation(dcols[:, 0:NCC], gcols[:, 0:NCC], AF.Exp),
                 waits=[cgc, S.now(SV)])
        sq1 = v(lambda e: e.tensor_mul(wt_sm[:], W_tm[:], W_tm[:]),
                waits=[trW, S.now(ST)])
        l2b = v(lambda e: e.reduce_sum(
            sscol[:, 0:NCC], wt_sm.rearrange("p (c m) -> p c m", c=NCC), AX.X),
            waits=[sq1, S.now(SS)])
        l2c = sc(lambda e: e.activation(sscol[:, 20:20 + NCC], sscol[:, 0:NCC],
                                        AF.Ln, bias=EPS_L2), waits=[l2b])
        l2d = sc(lambda e: e.activation(sscol[:, 20:20 + NCC],
                                        sscol[:, 20:20 + NCC], AF.Exp,
                                        scale=-0.5), waits=[l2c])
        brt = v(lambda e, _h=h: e.tensor_mul(brcol[:, 0:NCC],
                                             sscol[:, 20:20 + NCC],
                                             beta3[:, 1:1 + NCC, _h]),
                waits=[l2d, beta_tok, S.now(ST)])
        brst = v(lambda e: e.tensor_scalar(brS[:, 0:NCC], brcol[:, 0:NCC], SCALE,
                                           None, OP.mult), waits=[brt])
        wtk = v(lambda e: e.tensor_mul(wt_sm[:], W_tm[:], ERb[:]),
                waits=[l2b] + dtoks)
        whk = v(lambda e: e.tensor_mul(wh_sm[:], W_tm[:], expRr[:]),
                waits=[wtk])
        br3 = brcol[:, 0:NCC].unsqueeze(2).broadcast_to([128, NCC, 128])
        khk = v(lambda e: e.tensor_tensor(
            K_tm.rearrange("p (c t) -> p c t", c=NCC),
            K_tm.rearrange("p (c t) -> p c t", c=NCC), br3, OP.mult),
            waits=[trK, brt])
        vhk = v(lambda e: e.tensor_tensor(
            Vh_tm.rearrange("p (c t) -> p c t", c=NCC),
            V_tm.rearrange("p (c t) -> p c t", c=NCC), br3, OP.mult),
            waits=[trV, brt])
        if h == 0:
            dbg_tap("kT", kT_s, [S.now(SS)])
            dbg_tap("qT", qT_s, [qtok])
            dbg_tap("W_tm", W_tm, [trW])
            dbg_tap("V_tm", V_tm, [trV])
            dbg_tap("g_sm", g_sm, [trG])
            dbg_tap("E1", E1, dtoks)
            dbg_tap("E2", E2, dtoks)
            dbg_tap("ERb", ERb, dtoks)
            dbg_tap("expRr", expRr, dtoks)
            dbg_tap("gcols", gcols, [cgc])
            dbg_tap("dcols", dcols, [sdc])
            dbg_tap("sscol", sscol, [l2d])
            dbg_tap("brcol", brcol, [brt])
            dbg_tap("wt_sm", wt_sm, [wtk])
            dbg_tap("wh_sm", wh_sm, [whk])
            dbg_tap("Khat", K_tm, [khk])
            dbg_tap("Vhat", Vh_tm, [vhk])
        trWT = dma(W_tm.rearrange("p (c t) -> p c t", c=NCC)[:, 0:NCO, :],
                   wt_sm[:, 3 * 128:NCC * 128],
                   waits=[wtk, khk, vhk, S.now(ST)], transpose=True)
        wtT = W_tm

        z1 = v(lambda e: e.memset(SkT[:], 0.0), waits=[S.now(ST)])
        z2 = v(lambda e: e.memset(Sv[:], 0.0))

        upK = upV = None
        up2 = {0: None, 1: None}
        snapKs = None
        for c in range(NCC):
            par = c % 2
            PK = PSF[par][:, 0:128]
            PV = PSF[2 + par][:, 0:128]
            oc = c - 3
            if oc >= 0:
                snapK = v(lambda e, _o=oc: e.tensor_copy(
                    Sk_all[:, _o * 128:(_o + 1) * 128], SkT[:]),
                    waits=[upK, z1])
                snapV = v(lambda e, _o=oc: e.tensor_copy(
                    Sv_all[:, _o * 128:(_o + 1) * 128], Sv[:]), waits=[upV, z2])
                snapKs = snapV
            mdk = tn(lambda e, _c=c, _p=PK: e.matmul(
                _p, wh_sm[:, _c * 128:(_c + 1) * 128],
                K_tm[:, _c * 128:(_c + 1) * 128], start=True, stop=True),
                waits=[whk, khk, up2[par], S.now(SV) if c == 0 else None])
            mdv = tn(lambda e, _c=c, _p=PV: e.matmul(
                _p, wh_sm[:, _c * 128:(_c + 1) * 128],
                Vh_tm[:, _c * 128:(_c + 1) * 128], start=True, stop=True),
                waits=[vhk, S.now(SV) if c == 0 else None])
            upK = v(lambda e, _c=c, _p=PK: e.scalar_tensor_tensor(
                SkT[:], SkT[:], dcols[:, _c:_c + 1], _p, OP.mult, OP.add),
                waits=[mdk, sdc, z1, snapKs, upK])
            upV = v(lambda e, _c=c, _p=PV: e.scalar_tensor_tensor(
                Sv[:], Sv[:], dcols[:, _c:_c + 1], _p, OP.mult, OP.add),
                waits=[mdv, z2, upV])
            up2[par] = upV

        prevp = {0: {}, 1: {}}

        def chunkB(oc, pdict):
            c = oc + 3
            gc = c + 1
            par = oc % 2
            tcol = slice(gc * 128, (gc + 1) * 128)
            ccol = slice(c * 128, (c + 1) * 128)
            Pqk = PSF[3 * par + 0][:, 0:128]
            Psc = PSF[3 * par + 1][:, 0:128]
            Ppt = PSF[3 * par + 2][:, 0:128]
            PB = PB6 if par == 0 else PB7
            trS, trA, trB = PB[:, 0:128], PB[:, 128:256], PB[:, 256:384]
            SkEd_, SkE_, QKm_ = SkEdP[par], SkEP[par], QKmP[par]
            stil_, a_e_, aTt_, aHt_, PTm_ = (stilP[par], a_eP[par], aTtP[par],
                                             aHtP[par], PTmP[par])
            nc0 = ncol[:, 4 * par + 0:4 * par + 1]
            nc1 = ncol[:, 4 * par + 1:4 * par + 2]
            nc2 = ncol[:, 4 * par + 2:4 * par + 3]
            ttr = tn(lambda e, _o=oc: e.transpose(
                trS, Sk_all[:, _o * 128:(_o + 1) * 128], IDENT),
                waits=[snapKs, upK, pdict.get("evB"), pdict.get("evSk")])
            yield
            eve = v(lambda e: e.tensor_copy(SkEd_[:], trS),
                    waits=[ttr, pdict.get("mt1")])
            pdict["evSk"] = eve
            yield
            mqk = tn(lambda e, _t=tcol: e.matmul(Pqk, kT_s[:, _t], qT_s[:, _t],
                                                 start=True, stop=True),
                     waits=[eve, qtok, pdict.get("evT1")])
            yield
            evQK = v(lambda e, _c=c: e.scalar_tensor_tensor(
                QKm_[:], Pqk, brS[:, _c:_c + 1], UTC, OP.mult, OP.mult),
                waits=[mqk, brst])
            yield
            ms1 = tn(lambda e, _c=c: e.matmul(
                Psc, QKm_[:], wt_sm[:, _c * 128:(_c + 1) * 128],
                start=True, stop=True), waits=[evQK, wtk, pdict.get("evS1")])
            yield
            mt1 = tn(lambda e, _t=tcol: e.matmul(Pqk, qT_s[:, _t], SkEd_[:],
                                                 start=True, stop=True),
                     waits=[eve, evQK])
            pdict["mt1"] = mt1
            yield
            evT1 = v(lambda e, _c=c: e.scalar_tensor_tensor(
                SkE_[:], Pqk, SCALE, E1[:, _c * 128:(_c + 1) * 128],
                OP.mult, OP.mult), waits=[mt1])
            pdict["evT1"] = evT1
            yield
            evS1 = v(lambda e, _c=c: e.tensor_mul(
                stil_[:], Psc, E2[:, _c * 128:(_c + 1) * 128]), waits=[ms1])
            pdict["evS1"] = evS1
            yield
            evSC = v(lambda e: e.tensor_add(stil_[:], stil_[:], SkE_[:]),
                     waits=[evS1, evT1])
            yield
            nm = v(lambda e: e.reduce_max(nc0, stil_[:], AX.X, negate=True),
                   waits=[evSC, pdict.get("ae")])
            yield
            ae = sc(lambda e: e.activation(a_e_[:], stil_[:], AF.Exp, bias=nc0),
                    waits=[nm])
            pdict["ae"] = ae
            yield
            sm = v(lambda e: e.reduce_sum(nc1, a_e_[:], AX.X), waits=[ae])
            yield
            ri = v(lambda e: e.reciprocal(nc2, nc1), waits=[sm])
            yield
            at = v(lambda e, _c=c: e.tensor_mul(
                aTt_[:], a_e_[:], E2[:, _c * 128:(_c + 1) * 128]),
                waits=[ae, pdict.get("mpA")])
            yield
            ah = v(lambda e, _c=c: e.tensor_mul(
                aHt_[:], a_e_[:], E1[:, _c * 128:(_c + 1) * 128]),
                waits=[pdict.get("mo2")])
            yield
            ttb = tn(lambda e: e.transpose(trA, aTt_[:], IDENT),
                     waits=[at, eve, pdict.get("evB")])
            yield
            evA = v(lambda e: e.tensor_copy(aTt_[:], trA), waits=[ttb])
            yield
            ttc = tn(lambda e: e.transpose(trB, aHt_[:], IDENT),
                     waits=[ah, evA])
            yield
            evB = v(lambda e: e.tensor_copy(aHt_[:], trB), waits=[ttc])
            pdict["evB"] = evB
            yield
            mp1 = tn(lambda e, _o=oc: e.matmul(
                Ppt[0:64, 0:64], wtT[:, _o * 128:_o * 128 + 64], aTt_[:, 0:64],
                start=True, stop=True), waits=[evA, trWT, pdict.get("evPT")])
            mp2 = tn(lambda e, _o=oc: e.matmul(
                Ppt[64:128, 64:128], wtT[:, _o * 128 + 64:_o * 128 + 128],
                aTt_[:, 64:128], start=True, stop=True))
            mp3 = tn(lambda e, _o=oc: e.matmul(
                Ppt[0:64, 64:128], wtT[:, _o * 128:_o * 128 + 64],
                aTt_[:, 64:128], start=True, stop=True))
            pdict["mpA"] = mp3
            yield
            evPT = v(lambda e, _c=c: e.scalar_tensor_tensor(
                PTm_[:], Ppt, brcol[:, _c:_c + 1], UTC, OP.mult, OP.mult),
                waits=[mp3])
            pdict["evPT"] = evPT
            yield
            mo1 = tn(lambda e, _c=c: e.matmul(
                Ppt, PTm_[:], V_tm[:, _c * 128:(_c + 1) * 128],
                start=True, stop=False), waits=[evPT])
            mo2 = tn(lambda e, _o=oc: e.matmul(
                Ppt, aHt_[:], Sv_all[:, _o * 128:(_o + 1) * 128],
                start=False, stop=True), waits=[evB, snapKs])
            pdict["mo2"] = mo2
            yield
            evO = v(lambda e, _o=oc: e.tensor_scalar(
                obuf[:, _o * 128:(_o + 1) * 128], Ppt, nc2, None, OP.mult),
                waits=[mo2, ri, spill_prev])
            pdict["evO"] = evO
            yield

        for co in range(0, NCO, 2):
            gens = [chunkB(co, prevp[0]), chunkB(co + 1, prevp[1])]
            live = [True, True]
            while any(live):
                for gi in (0, 1):
                    if live[gi]:
                        try:
                            next(gens[gi])
                        except StopIteration:
                            live[gi] = False
        evO = prevp[1].get("evO")
        evO2 = prevp[0].get("evO")
        spill_prev = dma(oSp[h * 128:(h + 1) * 128, :], obuf[:],
                         waits=[evO, evO2])
        if h == 0:
            dbg_tap("SkT_f", SkT, [upK])
            dbg_tap("Sv_f", Sv, [upV])
    ATTN_LD = [S.now(n) for n in LDS]

    srcs = [(W1T, 16, FFH), (W2T, 16, FFH), (W3T, 44, HID)]
    slabs = [(mi, i) for mi, (_, n, _w) in enumerate(srcs) for i in range(n)]
    gb = barrier_toks()
    zg = v(lambda e: e.memset(gsum[:, 0:4], 0.0), waits=gb)
    fslabD = [carve(AR2, 0, FFH, F32), carve(AR2, 45056, FFH, F32)]
    fslaD = [carve(AR2, 22528, FFH, BF16), carve(AR2, 67584, FFH, BF16)]
    fslbD = [carve(AR2, 33792, FFH, BF16), carve(AR2, 78848, FFH, BF16)]
    cons = {}
    gprev = zg
    for si, (mi, i) in enumerate(slabs):
        WSRC, nslab, wid = srcs[mi]
        bi = si % 2
        tl = dma(fslabD[bi][:, 0:wid], WSRC[i * 128:(i + 1) * 128, :],
                 waits=[cons.get(si - 2)] + (gb if si == 0 else []))
        ga = sc(lambda e, _w=wid, _b=bi: e.activation(
            fslaD[_b][:, 0:_w], fslabD[_b][:, 0:_w], AF.Abs,
            accum_out=gsum[:, 4:5]), waits=[tl, gprev])
        gprev = v(lambda e, _m=mi: e.tensor_add(
            gsum[:, _m:_m + 1], gsum[:, _m:_m + 1], gsum[:, 4:5]), waits=[ga])
        cons[si] = ga
    thr = {}
    tprev = gprev
    for mi in range(3):
        mt = tn(lambda e, _m=mi: e.matmul(
            PSF[2][0:1, _m:_m + 1], gsum[:, _m:_m + 1], ONE_F,
            start=True, stop=True), waits=[gprev, S.now(SV), S.now(SS)])
        cv_ = v(lambda e, _m=mi: e.tensor_copy(gtmp[0:1, _m:_m + 1],
                                               PSF[2][0:1, _m:_m + 1]),
                waits=[mt])
        mr = tn(lambda e, _m=mi: e.matmul(
            PSF[3][:, _m:_m + 1], crow_s[:], gtmp[0:1, _m:_m + 1],
            start=True, stop=True), waits=[cv_])
        tp = v(lambda e, _m=mi: e.tensor_scalar(
            gcolq[:, 2 * _m:2 * _m + 1], PSF[3][:, _m:_m + 1], 0.5 / NELEM_W,
            0.5e-8, OP.mult, OP.add), waits=[mr, tprev])
        tng = v(lambda e, _m=mi: e.tensor_scalar(
            gcolq[:, 2 * _m + 1:2 * _m + 2], gcolq[:, 2 * _m:2 * _m + 1],
            -1.0, None, OP.mult), waits=[tp])
        thr[mi] = tng
        tprev = tng
    cons = {}
    stok = {}
    for si, (mi, i) in enumerate(slabs):
        WSRC, nslab, wid = srcs[mi]
        bi = si % 2
        tl = dma(fslabD[bi][:, 0:wid], WSRC[i * 128:(i + 1) * 128, :],
                 waits=[cons.get(si - 2), thr[mi] if i == 0 else None])
        s1 = sc(lambda e, _w=wid, _b=bi, _m=mi: e.activation(
            fslaD[_b][:, 0:_w], fslabD[_b][:, 0:_w], AF.Sign,
            bias=gcolq[:, 2 * _m + 1:2 * _m + 2]),
            waits=[tl, thr[mi], stok.get(si - 2)])
        s2 = sc(lambda e, _w=wid, _b=bi, _m=mi: e.activation(
            fslbD[_b][:, 0:_w], fslabD[_b][:, 0:_w], AF.Sign,
            bias=gcolq[:, 2 * _m:2 * _m + 1]))
        qa = v(lambda e, _w=wid, _b=bi: e.tensor_add(
            fslbD[_b][:, 0:_w], fslbD[_b][:, 0:_w], fslaD[_b][:, 0:_w]),
            waits=[s1, s2, stok.get(si - 2)])
        cons[si] = qa
        stok[si] = dma(q2w[mi][i * 128:(i + 1) * 128, :], fslbD[bi][:, 0:wid],
                       waits=[qa])
    wo_cons = None
    wo_st = {}
    for i in range(16):
        bi = i % 2
        tl = dma(fslabD[bi][:, 0:HID], WOp[i * 128:(i + 1) * 128, :],
                 waits=[wo_cons, cons.get(len(slabs) - 2 + bi)])
        wo_cons = v(lambda e, _b=bi: e.tensor_scalar(
            fslaD[_b][:, 0:HID], fslabD[_b][:, 0:HID], nw_s[:, 0:1], None,
            OP.mult), waits=[tl, wo_st.get(i - 2)])
        wo_st[i] = dma(woB[i * 128:(i + 1) * 128, :], fslaD[bi][:, 0:HID],
                       waits=[wo_cons])
    QDONE = [S.now(n) for n in LDS]

    q2w1r = q2w[0].rearrange("(k p) c -> p k c", p=128)
    q2w2r = q2w[1].rearrange("(k p) c -> p k c", p=128)
    fb = barrier_toks()
    hprev = None
    hst = None
    mmD = {}
    for cs in range(44):
        bi = cs % 2
        tla = dma(q2aD[bi].rearrange("p (k c) -> p k c", k=16),
                  q2w1r[:, :, cs * 128:(cs + 1) * 128],
                  waits=QDONE + [mmD.get(cs - 2)] + (fb if cs < 2 else []))
        tlb = dma(q2bD[bi].rearrange("p (k c) -> p k c", k=16),
                  q2w2r[:, :, cs * 128:(cs + 1) * 128])
        for tb in range(4):
            off = 512 + tb * 512
            p0 = PSF[0][:, 0:512]
            p1 = PSF[1][:, 0:512]
            pairs0 = [(q2aD[bi][:, k * 128:(k + 1) * 128],
                       XNT[:, k * T_IN + off:k * T_IN + off + 512])
                      for k in range(16)]
            pairs1 = [(q2bD[bi][:, k * 128:(k + 1) * 128],
                       XNT[:, k * T_IN + off:k * T_IN + off + 512])
                      for k in range(16)]
            m0 = mm_acc(p0, pairs0, [tla, hprev] + (fb if cs == 0 else []))
            m1 = mm_acc(p1, pairs1, [tlb])
            sg = sc(lambda e, _tb=tb, _p=p0: e.activation(
                sigx[:, _tb * 512:(_tb + 1) * 512], _p, AF.Sigmoid, scale=0.5),
                waits=[m0, hprev])
            hprev = v(lambda e, _tb=tb, _p=p1: e.scalar_tensor_tensor(
                hslab[:, _tb * 512:(_tb + 1) * 512], _p, 0.25,
                sigx[:, _tb * 512:(_tb + 1) * 512], OP.mult, OP.mult),
                waits=[m1, sg, hst])
        mmD[cs] = m1
        hst = dma(hidB[cs * 128:(cs + 1) * 128, :], hslab[:], waits=[hprev])
    F1_LD = [S.now(n) for n in LDS]

    oSpr = oSp.rearrange("(h p) (c d) -> p c h d", p=128, d=128)
    nb = barrier_toks() + F1_LD + ATTN_LD
    ntok = None
    gf_prev = None
    on2 = None
    for oc in range(NCO):
        tl = dma(onrow.rearrange("p (g d) -> p g d", g=H), oSpr[:, oc],
                 waits=[ntok] + (nb if oc == 0 else []))
        gts = []
        for cb in range(4):
            ps = PSF[cb][:, 0:512]
            mg = tn(lambda e, _o=oc, _cb=cb, _p=ps: e.matmul(
                _p, out2T[:, _o * 128:(_o + 1) * 128],
                wg2b[:, _cb * 512:(_cb + 1) * 512], start=True, stop=True),
                waits=[gf_prev, o2tok] + (nb if oc == 0 else []))
            gf = v(lambda e, _cb=cb, _p=ps: e.tensor_add(
                gate_f[:, _cb * 512:(_cb + 1) * 512], _p,
                bg2_s[:, _cb * 512:(_cb + 1) * 512]),
                waits=[mg] + ([on2] if on2 else []))
            gts.append(sc(lambda e, _cb=cb: e.activation(
                gatesig[:, _cb * 512:(_cb + 1) * 512],
                gate_f[:, _cb * 512:(_cb + 1) * 512], AF.Sigmoid), waits=[gf]))
        gf_prev = gts[-1]
        sq = v(lambda e: e.tensor_mul(sqn[:], onrow[:], onrow[:]), waits=[tl])
        rs2 = v(lambda e: e.reduce_sum(
            rstd16[:, 0:16], sqn.rearrange("p (g d) -> p g d", g=H), AX.X),
            waits=[sq, S.now(SS)])
        rs3 = v(lambda e: e.tensor_scalar(rstd16[:, 0:16], rstd16[:, 0:16],
                                          1.0 / 128.0, 1e-5, OP.mult, OP.add),
                waits=[rs2])
        rs4 = sc(lambda e: e.activation(rstd16[:, 0:16], rstd16[:, 0:16],
                                        AF.Ln), waits=[rs3])
        rs5 = sc(lambda e: e.activation(rstd16[:, 0:16], rstd16[:, 0:16],
                                        AF.Exp, scale=-0.5), waits=[rs4])
        on1 = v(lambda e: e.tensor_mul(onrow[:], onrow[:], gatesig[:]),
                waits=gts + [rs2, sq])
        rb3 = rstd16[:, 0:16].unsqueeze(2).broadcast_to([128, 16, 128])
        on2 = v(lambda e: e.tensor_tensor(
            onrow.rearrange("p (g d) -> p g d", g=H),
            onrow.rearrange("p (g d) -> p g d", g=H), rb3, OP.mult),
            waits=[rs5, on1])
        ntok = dma(onT.rearrange("p (j t) -> p j t", j=16)
                   [:, :, oc * 128:(oc + 1) * 128],
                   onrow[:], waits=[on2], transpose=True)
        if oc == 0:
            dbg_tap("gatesig", gatesig, [gts[-1]])
            dbg_tap("rstd16", rstd16, [rs5])
            dbg_tap("onrow0", onrow, [on2])
    NORM_LD = [S.now(n) for n in LDS]
    dbg_tap("onT", onT, NORM_LD)

    q2w3r = q2w[2].rearrange("(k p) c -> p k c", p=128)
    woBr = woB.rearrange("(k p) c -> p k c", p=128)
    w3cbD = [carve(AR2, 45056, 44 * 256, BF16), carve(AR2, 67584, 44 * 256, BF16)]
    wocbD = [carve(AR1, 65536, 16 * 256, BF16), carve(AR1, 73728, 16 * 256, BF16)]
    xstg = [carve(AR1, 81920 + 1024 * i, 256, F32) for i in range(4)]
    outr2 = [carve(AR2, 90112 + 1024 * i, 256, F32) for i in range(4)]
    f2b = barrier_toks() + NORM_LD
    NCB = 8
    fprev = {}
    yst = {}
    ld3 = {}
    ldo = {}
    ldx = {}
    ldh = None
    for tq in range(4):
        ldh = dma(hidq.rearrange("p (k c) -> p k c", k=44),
                  hidB.rearrange("(k p) c -> p k c", p=128)
                  [:, :, tq * 512:(tq + 1) * 512],
                  waits=[fprev.get((tq - 1, NCB - 1, 3))]
                  + (f2b if tq == 0 else []))
        for cbk in range(NCB):
            bi = cbk % 2
            ld3[cbk] = dma(w3cbD[bi].rearrange("p (k c) -> p k c", k=44),
                           q2w3r[:, :, cbk * 256:(cbk + 1) * 256],
                           waits=[fprev.get((tq, cbk - 2, 3)),
                                  fprev.get((tq - 1, NCB - 1, 3))]
                           + (f2b if tq == 0 and cbk < 2 else []))
            ldo[cbk] = dma(wocbD[bi].rearrange("p (k c) -> p k c", k=16),
                           woBr[:, :, cbk * 256:(cbk + 1) * 256],
                           waits=[])
            for tc in range(4):
                co = tq * 4 + tc
                ps = PSF[tc % 2 + 2 * bi][:, 0:256]
                ldx[(cbk, tc)] = dma(
                    xstg[tc][:], x_in[HALO + co * 128:HALO + (co + 1) * 128,
                                      cbk * 256:(cbk + 1) * 256],
                    waits=[fprev.get((tq, cbk - 1, tc)),
                           fprev.get((tq - 1, NCB - 1, tc) if cbk == 0
                                     else (tq, cbk - 1, tc))])
                pairs = [(hidq[:, k * 512 + tc * 128:k * 512 + tc * 128 + 128],
                          w3cbD[bi][:, k * 256:(k + 1) * 256])
                         for k in range(44)]
                pairs += [(onT[:, j * T_OWN + co * 128:j * T_OWN + co * 128
                               + 128], wocbD[bi][:, j * 256:(j + 1) * 256])
                          for j in range(16)]
                mt = mm_acc(ps, pairs,
                            [ldh, ld3[cbk], ldo[cbk],
                             fprev.get((tq, cbk - 2, tc)),
                             fprev.get((tq, cbk, tc - 2))])
                fprev[(tq, cbk, tc)] = v(
                    lambda e, _p=ps, _t=tc: e.tensor_add(
                        outr2[_t][:], _p, xstg[_t][:]),
                    waits=[mt, ldx[(cbk, tc)],
                           yst.get(((cbk - 1) % NCB, tc))])
                yst[(cbk, tc)] = dma(
                    y[co * 128:(co + 1) * 128, cbk * 256:(cbk + 1) * 256],
                    outr2[tc][:, 0:256], waits=[fprev[(tq, cbk, tc)]])
    S.op("sync", lambda e: None, waits=[S.now(n) for n in LDS])
    S.op("gpsimd", lambda e: None, waits=[S.now(n) for n in GDS])

    from contextlib import ExitStack
    es = ExitStack()
    sems = {n: es.enter_context(nc.semaphore(n))
            for n in (["SV", "SS", "ST"] + ["LD%d" % i for i in range(8)]
                      + ["GD%d" % i for i in range(4)])}
    with nc.Block() as blk:
        @blk.sync
        def _(eng):
            S.emit("sync", eng, sems)

        @blk.vector
        def _(eng):
            S.emit("vector", eng, sems)

        @blk.scalar
        def _(eng):
            S.emit("scalar", eng, sems)

        @blk.tensor
        def _(eng):
            S.emit("tensor", eng, sems)

        @blk.gpsimd
        def _(eng):
            S.emit("gpsimd", eng, sems)
    es.close()
    return nc


_NC_CACHE = None


def kernel(L0_attn_hidden_states, L0_attn_Wq, L0_attn_Wk, L0_attn_Wv, L0_attn_Ww,
           L0_attn_cq, L0_attn_ck, L0_attn_cv, L0_attn_Wf1, L0_attn_Wf2,
           L0_attn_Wb, L0_attn_Wg1, L0_attn_Wg2, L0_attn_bg2, L0_attn_norm_w,
           L0_attn_Wo, L0_ffn_w1, L0_ffn_w2, L0_ffn_w3):
    global _NC_CACHE
    x = np.asarray(L0_attn_hidden_states, dtype=np.float32)
    B, T, _ = x.shape
    assert (B, T) == (2, 8192)

    shared = {
        "Wq": np.ascontiguousarray(np.asarray(L0_attn_Wq, np.float32)),
        "Wk": np.ascontiguousarray(np.asarray(L0_attn_Wk, np.float32)),
        "Wv": np.ascontiguousarray(np.asarray(L0_attn_Wv, np.float32)),
        "Ww": np.ascontiguousarray(np.asarray(L0_attn_Ww, np.float32)),
        "Wo": np.ascontiguousarray(np.asarray(L0_attn_Wo, np.float32)),
        "Wf1": np.ascontiguousarray(np.asarray(L0_attn_Wf1, np.float32)),
        "Wf2": np.ascontiguousarray(np.asarray(L0_attn_Wf2, np.float32)),
        "Wb": np.ascontiguousarray(np.asarray(L0_attn_Wb, np.float32)),
        "Wg1": np.ascontiguousarray(np.asarray(L0_attn_Wg1, np.float32)),
        "Wg2": np.ascontiguousarray(np.asarray(L0_attn_Wg2, np.float32)),
        "cq": np.ascontiguousarray(np.asarray(L0_attn_cq, np.float32)),
        "ck": np.ascontiguousarray(np.asarray(L0_attn_ck, np.float32)),
        "cv": np.ascontiguousarray(np.asarray(L0_attn_cv, np.float32)),
        "w1T": np.ascontiguousarray(np.asarray(L0_ffn_w1, np.float32).T),
        "w2T": np.ascontiguousarray(np.asarray(L0_ffn_w2, np.float32).T),
        "w3T": np.ascontiguousarray(np.asarray(L0_ffn_w3, np.float32).T),
        "bg2r": np.ascontiguousarray(
            np.broadcast_to(np.asarray(L0_attn_bg2, np.float32), (128, HID))),
        "nw_col": np.ascontiguousarray(
            np.asarray(L0_attn_norm_w, np.float32).reshape(128, 1)),
        "cb": np.ascontiguousarray(np.concatenate(
            [np.eye(128, dtype=BFD),
             np.triu(np.ones((128, 128), dtype=BFD)),
             np.full((128, 128), 0.5, dtype=BFD)], axis=1)),
        "cf": np.ascontiguousarray(np.tile(
            np.array([1e-5, 1e-6, 1.0, 0.0, 0.0, 0.0, 0.0, 0.0],
                     np.float32), (128, 1))),
        "crow": np.ones((1, 128), np.float32),
    }

    in_maps = []
    for c in range(8):
        b, qi = c // 4, c % 4
        lo = qi * T_OWN
        if qi == 0:
            xs = np.concatenate([np.zeros((HALO, HID), np.float32),
                                 x[b, 0:T_OWN]], axis=0)
        else:
            xs = x[b, lo - HALO: lo + T_OWN]
        m = dict(shared)
        m["x_in"] = np.ascontiguousarray(xs)
        in_maps.append(m)

    if _NC_CACHE is None:
        _NC_CACHE = build_program(debug=bool(globals().get("_DBG_MODE")))
    res = run_bass_kernel_spmd(_NC_CACHE, in_maps, core_ids=list(range(8)))
    if globals().get("_DBG_MODE"):
        globals()["_DBG_RES"] = res

    out = np.empty((2, 8192, HID), np.float32)
    for c in range(8):
        b, qi = c // 4, c % 4
        out[b, qi * T_OWN:(qi + 1) * T_OWN] = res.results[c]["y"]
    return out


if __name__ == "__main__":
    nc = build_program()
    print("build ok, instructions:", nc.n_instructions())


# version 19
# speedup vs baseline: 1.2397x; 1.0150x over previous
import numpy as np
import ml_dtypes
import concourse.bass as bass
import concourse.mybir as mybir
from concourse.bass_utils import run_bass_kernel_spmd
from concourse.alu_op_type import AluOpType as OP

F32 = mybir.dt.float32
BF16 = mybir.dt.bfloat16
AF = mybir.ActivationFunctionType
AX = mybir.AxisListType
BFD = ml_dtypes.bfloat16

HID = 2048
H = 16
T_OWN = 2048
HALO = 512
T_IN = T_OWN + HALO
NCI = T_IN // 128
NCC = 19
NCO = 16
FFH = 5632
SCALE = 128.0 ** -0.5
NELEM_W = float(HID * FFH)


class Sch:
    ENGS = ("sync", "vector", "scalar", "tensor", "gpsimd")

    def __init__(self):
        self.ops = {e: [] for e in self.ENGS}
        self.cnt = {}
        self.seen = {e: {} for e in self.ENGS}

    def sem(self, name):
        self.cnt.setdefault(name, 0)
        return name

    DMA_SEMS = tuple("LD%d" % i for i in range(8))

    def op(self, eng, fn, waits=(), inc=None, amt=1):
        w = []
        for t in waits:
            if t is None:
                continue
            s, vv = t
            if s is None or vv is None or vv <= 0:
                continue
            if self.seen[eng].get(s, -1) >= vv:
                continue
            self.seen[eng][s] = vv
            w.append((s, vv))
        tgt = None
        if inc is not None:
            self.cnt[inc] += amt
            tgt = self.cnt[inc]
        self.ops[eng].append((w, fn, inc, amt))
        return (inc, tgt) if inc is not None else None

    def now(self, sem):
        return (sem, self.cnt.get(sem, 0))

    def emit(self, eng_name, eng, sems):
        for w, fn, inc, amt in self.ops[eng_name]:
            for s, vv in w:
                eng.wait_ge(sems[s], vv)
            ins = fn(eng)
            if inc is not None:
                if ins is None:
                    ins = eng.nop()
                ins.then_inc(sems[inc], amt)


def build_program(debug=False):
    nc = bass.Bass("TRN2", target_bir_lowering=False)

    def din(name, shape, dt=F32):
        return nc.dram_tensor(name, list(shape), dt, kind="ExternalInput").ap()

    x_in = din("x_in", (T_IN, HID))
    WQ = din("Wq", (HID, HID)); WK = din("Wk", (HID, HID))
    WV = din("Wv", (HID, HID)); WW = din("Ww", (HID, HID))
    WOp = din("Wo", (HID, HID))
    WF1 = din("Wf1", (HID, 128)); WF2 = din("Wf2", (128, HID))
    WB = din("Wb", (HID, H))
    WG1 = din("Wg1", (HID, 128)); WG2 = din("Wg2", (128, HID))
    CQ = din("cq", (HID, 4)); CK = din("ck", (HID, 4)); CV = din("cv", (HID, 4))
    W1T = din("w1T", (HID, FFH)); W2T = din("w2T", (HID, FFH))
    W3T = din("w3T", (FFH, HID))
    BG2R = din("bg2r", (128, HID))
    NWC = din("nw_col", (128, 1))
    CB = din("cb", (128, 3 * 128), BF16)
    CF = din("cf", (128, 8))
    CROW = din("crow", (1, 128))

    y = nc.dram_tensor("y", [T_OWN, HID], F32, kind="ExternalOutput").ap()

    def dint(name, shape, dt):
        return nc.dram_tensor(name, list(shape), dt, kind="Internal").ap()

    def dmaybe(name, shape, dt):
        kind = "ExternalOutput" if debug else "Internal"
        return nc.dram_tensor(name, list(shape), dt, kind=kind).ap()

    q2w = [dmaybe("q2w1", (HID, FFH), BF16), dmaybe("q2w2", (HID, FFH), BF16),
           dmaybe("q2w3", (FFH, HID), BF16)]
    hidB = dmaybe("hidB", (FFH, T_OWN), BF16)
    woB = dint("woB", (HID, HID), BF16)
    oSp = dmaybe("oSp", (H * 128, NCO * 128), BF16)

    DBG = {}

    def dbg_tap(name, ap, waits):
        if not debug:
            return
        if not isinstance(ap, bass.AP):
            ap = ap[:]
        t = nc.dram_tensor("dbg_" + name, [ap.shape[0], ap.shape[1]],
                           ap.dtype, kind="ExternalOutput").ap()
        DBG[name] = dma(t, ap, waits=waits)

    sbt = nc.alloc_sbuf_tensor

    cb_s = sbt("cb_s", [128, 3 * 128], BF16)
    IDENT = cb_s[:, 0:128]
    UTC = cb_s[:, 128:256]
    HALFM = cb_s[:, 256:384]
    cf_s = sbt("cf_s", [128, 8], F32)
    EPS_LN = cf_s[:, 0:1]; EPS_L2 = cf_s[:, 1:2]; ONE_F = cf_s[:, 2:3]
    crow_s = sbt("crow_s", [1, 128], F32)
    ones_bf = sbt("ones_bf", [128, 16], BF16)
    ONES_COL = ones_bf[:, 0:1]
    nw_s = sbt("nw_s", [128, 1], F32)
    bg2_s = sbt("bg2_s", [128, HID], BF16)
    wf2b = sbt("wf2b", [128, HID], BF16)
    wg2b = sbt("wg2b", [128, HID], BF16)
    cqall = sbt("cqall", [128, 3 * 64], F32)
    beta_s = sbt("beta_s", [128, NCI * H], F32)
    out1T = sbt("out1T", [128, T_IN], BF16)
    out2T = sbt("out2T", [128, T_OWN], BF16)
    lnst = sbt("lnst", [128, 8], F32)
    gcols = sbt("gcols", [128, 32], F32)
    dcols = sbt("dcols", [128, 32], F32)
    ercols = sbt("ercols", [128, 32], F32)
    sscol = sbt("sscol", [128, 48], F32)
    brcol = sbt("brcol", [128, 32], F32)
    brS = sbt("brS", [128, 32], F32)
    ncol = sbt("ncol", [128, 8], F32)
    rstd16 = sbt("rstd16", [128, 16], F32)
    gsum = sbt("gsum", [128, 8], F32)
    gcolq = sbt("gcolq", [128, 8], F32)
    gtmp = sbt("gtmp", [1, 8], F32)
    SkT = sbt("SkT", [128, 128], BF16)
    Sv = sbt("Sv", [128, 128], BF16)
    SkEP = [sbt("SkE%d" % i, [128, 128], BF16) for i in range(2)]
    SkEdP = [sbt("SkEd%d" % i, [128, 128], BF16) for i in range(2)]
    QKmP = [sbt("QKm%d" % i, [128, 128], BF16) for i in range(2)]
    stilP = [sbt("stil%d" % i, [128, 128], BF16) for i in range(2)]
    a_eP = [sbt("a_e%d" % i, [128, 128], BF16) for i in range(2)]
    aTtP = [sbt("aTt%d" % i, [128, 128], BF16) for i in range(2)]
    aHtP = [sbt("aHt%d" % i, [128, 128], BF16) for i in range(2)]
    PTmP = [sbt("PTm%d" % i, [128, 128], BF16) for i in range(2)]

    AR1 = sbt("AR1", [128, 43008], BF16)
    AR2 = sbt("AR2", [128, 47232], BF16)

    def carve(ar, off_bytes, cols, dt):
        assert off_bytes % 4 == 0
        c2 = off_bytes // 2
        if dt == BF16:
            assert (off_bytes + 2 * cols) <= 2 * ar.shape[1]
            return ar[:, c2:c2 + cols]
        assert (off_bytes + 4 * cols) <= 2 * ar.shape[1]
        return ar[:, c2:c2 + 2 * cols].bitcast(F32)

    XNT = carve(AR1, 0, 16 * T_IN, BF16)
    onT = carve(AR1, 0, 16 * T_OWN, BF16)
    wocb = carve(AR1, 65536, 16 * 512, BF16)
    wstage = carve(AR1, 81920, 512, F32)
    outrow = carve(AR1, 83968, 512, F32)

    xrow = carve(AR2, 0, HID, F32)
    sqrow = carve(AR2, 8192, HID, F32)
    xnrow = carve(AR2, 16384, HID, BF16)
    _off = [0]

    def hc(cols, dt):
        ap = carve(AR2, _off[0], cols, dt)
        _off[0] += cols * (2 if dt == BF16 else 4)
        _off[0] = (_off[0] + 63) & ~63
        return ap

    wcb = [hc(HID, BF16), hc(HID, BF16)]
    preconv = hc(T_IN + 4 + 540, BF16)
    projbuf = hc(T_IN, BF16)
    E1 = carve(AR2, 0, NCC * 128, BF16)
    E2 = carve(AR2, 4864, NCC * 128, BF16)
    ERb = carve(AR2, 9728, NCC * 128, BF16)
    expRr = carve(AR2, 14592, NCC * 128, BF16)
    kT_s = hc(T_IN, BF16)
    qT_s = hc(T_IN, BF16)
    K_tm = hc(NCC * 128, BF16)
    V_tm = hc(NCC * 128, BF16)
    Vh_tm = hc(NCC * 128, BF16)
    W_tm = hc(NCC * 128, BF16)
    g_sm = hc(NCC * 128, BF16)
    wt_sm = hc(NCC * 128, BF16)
    wh_sm = hc(NCC * 128, BF16)
    grow = hc(512, F32)
    zrow = hc(512, F32)
    growB = hc(512, F32)
    zrowB = hc(512, F32)
    vbuf = hc(T_IN, BF16)
    obuf = carve(AR1, 81920, NCO * 128, BF16)
    Sk_all = hc(NCO * 128, BF16)
    Sv_all = hc(NCO * 128, BF16)
    assert _off[0] <= 94464, _off[0]
    fslab = carve(AR2, 0, FFH, F32)
    fsl_a = carve(AR2, 22528, FFH, BF16)
    fsl_b = carve(AR2, 33792, FFH, BF16)
    q2aD = [carve(AR2, 0, HID, BF16), carve(AR2, 16384, HID, BF16)]
    q2bD = [carve(AR2, 4096, HID, BF16), carve(AR2, 20480, HID, BF16)]
    sigx = carve(AR2, 8192, T_OWN, BF16)
    hslab = carve(AR2, 12288, T_OWN, BF16)
    onrow = carve(AR2, 0, HID, BF16)
    gate_f = carve(AR2, 4096, HID, F32)
    gatesig = carve(AR2, 12288, HID, BF16)
    sqn = carve(AR2, 16384, HID, BF16)
    hidq = carve(AR2, 0, 44 * 512, BF16)
    w3cb = carve(AR2, 45056, 44 * 512, BF16)
    xstage = carve(AR2, 90112, 512, F32)

    PSF = [nc.alloc_psum_tensor("PS%d" % i, [128, 512], F32) for i in range(6)]
    PB6 = nc.alloc_psum_tensor("PB6", [128, 1024], BF16)
    PB7 = nc.alloc_psum_tensor("PB7", [128, 1024], BF16)
    TRA = PB6[:, 0:128]
    TRB = PB7[:, 0:128]
    TRC = PB7[:, 512:640]

    S = Sch()
    LDS = [S.sem("LD%d" % i) for i in range(8)]
    GDS = [S.sem("GD%d" % i) for i in range(4)]
    SV = S.sem("SV"); SS = S.sem("SS"); ST = S.sem("ST")
    _rr = [0]
    _gr = [0]

    def dma(out_, in_, waits=(), transpose=False):
        if transpose:
            waits = list(waits) + [S.now(n) for n in LDS]
        sem = LDS[_rr[0] % 8]
        _rr[0] += 1
        if S.cnt[sem] > 0:
            S.op("sync", lambda e: None, waits=[S.now(sem)])

        def f(e, _o=out_, _i=in_, _t=transpose):
            return e.dma_start(out=_o, in_=_i, transpose=_t)
        tok = S.op("sync", f, waits=waits, inc=sem, amt=16)
        if transpose:
            for n in LDS:
                S.seen["sync"].pop(n, None)
            S.op("sync", lambda e: None, waits=[S.now(sem)])
        return tok

    def gdma(out_, in_, waits=()):
        sem = GDS[_gr[0] % 4]
        _gr[0] += 1
        if S.cnt[sem] > 0:
            S.op("gpsimd", lambda e: None, waits=[S.now(sem)])

        def f(e, _o=out_, _i=in_):
            return e.dma_start(out=_o, in_=_i)
        return S.op("gpsimd", f, waits=waits, inc=sem, amt=16)

    def v(fn, waits=()):
        return S.op("vector", fn, waits=waits, inc=SV)

    def sc(fn, waits=()):
        return S.op("scalar", fn, waits=waits, inc=SS)

    def tn(fn, waits=()):
        return S.op("tensor", fn, waits=waits, inc=ST)

    def barrier_toks():
        return ([S.now(SV), S.now(SS), S.now(ST)]
                + [S.now(n) for n in LDS] + [S.now(n) for n in GDS])

    def mm_acc(ps_ap, pairs, waits_first):
        tok = None
        n = len(pairs)
        for k, (lt, rh) in enumerate(pairs):
            tok = tn(lambda e, _l=lt, _r=rh, _s=(k == 0), _e=(k == n - 1),
                     _p=ps_ap: e.matmul(_p, _l, _r, start=_s, stop=_e),
                     waits=waits_first if k == 0 else ())
        return tok

    dma(cb_s[:], CB)
    dma(cf_s[:], CF)
    dma(crow_s[:], CROW)
    dma(nw_s[:], NWC)
    v(lambda e: e.memset(ones_bf[:], 1.0))
    v(lambda e: e.memset(PSF[5][64:128, 0:64], 0.0))
    v(lambda e: e.memset(PSF[2][64:128, 0:64], 0.0))
    bgtok = gdma(bg2_s[:], BG2R)
    dma(cqall.rearrange("p (s k w) -> p s k w", s=3, w=4)[:, 0],
        CQ.rearrange("(k p) w -> p k w", p=128))
    dma(cqall.rearrange("p (s k w) -> p s k w", s=3, w=4)[:, 1],
        CK.rearrange("(k p) w -> p k w", p=128))
    dma(cqall.rearrange("p (s k w) -> p s k w", s=3, w=4)[:, 2],
        CV.rearrange("(k p) w -> p k w", p=128))
    wf2tok = gdma(wf2b[:], WF2)
    wg2tok = gdma(wg2b[:], WG2)

    consts_rdy = [S.now(n) for n in LDS] + [wg2tok, wf2tok, bgtok]
    prev_used = wg2tok
    ttok = None
    for i in range(NCI):
        tl = dma(xrow[:], x_in[i * 128:(i + 1) * 128, :],
                 waits=[prev_used] + (consts_rdy if i == 0 else []))
        t1 = v(lambda e: e.reduce_sum(lnst[:, 0:1], xrow[:], AX.X), waits=[tl])
        t2 = sc(lambda e: e.activation(sqrow[:], xrow[:], AF.Square,
                                       accum_out=lnst[:, 1:2]), waits=[tl])
        t3 = v(lambda e: e.tensor_scalar(lnst[:, 2:3], lnst[:, 0:1], 1.0 / HID,
                                         None, OP.mult), waits=[t1])
        t4 = v(lambda e: e.tensor_scalar(lnst[:, 3:4], lnst[:, 1:2], 1.0 / HID,
                                         None, OP.mult), waits=[t2])
        t5 = v(lambda e: e.tensor_mul(lnst[:, 4:5], lnst[:, 2:3], lnst[:, 2:3]),
               waits=[t3])
        t6 = v(lambda e: e.tensor_sub(lnst[:, 5:6], lnst[:, 3:4], lnst[:, 4:5]),
               waits=[t4, t5])
        t7 = sc(lambda e: e.activation(lnst[:, 6:7], lnst[:, 5:6], AF.Ln,
                                       bias=EPS_LN), waits=[t6])
        t8 = sc(lambda e: e.activation(lnst[:, 7:8], lnst[:, 6:7], AF.Exp,
                                       scale=-0.5), waits=[t7])
        t9 = v(lambda e: e.tensor_scalar(xnrow[:], xrow[:], lnst[:, 2:3],
                                         lnst[:, 7:8], OP.subtract, OP.mult),
               waits=[t8, t3, ttok])
        ttok = dma(XNT.rearrange("p (k t) -> p k t", k=16)
                   [:, :, i * 128:(i + 1) * 128],
                   xnrow[:], waits=[t9], transpose=True)
        prev_used = t9
    XN_RDY = ttok
    dbg_tap("XNT", XNT, [XN_RDY])

    cwf = gdma(wcb[0].rearrange("p (k c) -> p k c", k=16),
               WF1.rearrange("(k p) c -> p k c", p=128), waits=[XN_RDY])
    cwg = gdma(wcb[1].rearrange("p (k c) -> p k c", k=16),
               WG1.rearrange("(k p) c -> p k c", p=128), waits=[XN_RDY])
    evtoks = {}
    for tb in range(5):
        ps = PSF[tb % 2][:, 0:512]
        pairs = [(wcb[0][:, k * 128:(k + 1) * 128],
                  XNT[:, k * T_IN + tb * 512:k * T_IN + tb * 512 + 512])
                 for k in range(16)]
        mt = mm_acc(ps, pairs, [cwf, XN_RDY, evtoks.get(tb - 2)])
        evtoks[tb] = v(lambda e, _tb=tb, _ps=ps: e.tensor_copy(
            out1T[:, _tb * 512:(_tb + 1) * 512], _ps), waits=[mt])
    o1tok = S.now(SV)
    f1mm = mt
    evtoks = {}
    for tb in range(4):
        ps = PSF[2 + tb % 2][:, 0:512]
        off = 512 + tb * 512
        pairs = [(wcb[1][:, k * 128:(k + 1) * 128],
                  XNT[:, k * T_IN + off:k * T_IN + off + 512]) for k in range(16)]
        mt = mm_acc(ps, pairs, [cwg, evtoks.get(tb - 2)])
        evtoks[tb] = v(lambda e, _tb=tb, _ps=ps: e.tensor_copy(
            out2T[:, _tb * 512:(_tb + 1) * 512], _ps), waits=[mt])
    o2tok = S.now(SV)
    cwb = gdma(wcb[0].rearrange("p (k c) -> p k c", c=16)[:, 0:16, :],
               WB.rearrange("(k p) c -> p k c", p=128), waits=[f1mm])
    bprev = None
    for i in range(NCI):
        ps = PSF[4][:, 0:16]
        pairs = [(XNT[:, k * T_IN + i * 128:k * T_IN + i * 128 + 128],
                  wcb[0][:, k * 16:(k + 1) * 16]) for k in range(16)]
        mt = mm_acc(ps, pairs, [cwb, bprev])
        bprev = sc(lambda e, _i=i, _ps=ps: e.activation(
            beta_s[:, _i * 16:(_i + 1) * 16], _ps, AF.Sigmoid), waits=[mt])
    beta_tok = bprev
    dbg_tap("out1T", out1T, [o1tok])
    dbg_tap("out2T", out2T, [o2tok])
    dbg_tap("beta", beta_s, [beta_tok])

    beta3 = beta_s.rearrange("p (c h) -> p c h", h=H)
    zpad = v(lambda e: e.memset(preconv[:, 0:4], 0.0), waits=[beta_tok])

    TRREG = [(PB6, 0), (PB7, 0), (PB6, 128), (PB7, 128)]
    trst = {"rot": 0, "ev": {}}

    def pe_transpose_seq(src, n, dst, waits_first):
        ev = None
        for i in range(n):
            bank, off = TRREG[trst["rot"] % 4]
            reg = bank[:, off:off + 128]
            w = [trst["ev"].get(trst["rot"] - 2)]
            if i == 0:
                w += list(waits_first)
            tt = tn(lambda e, _s=src[:, 128 + i * 128:128 + (i + 1) * 128],
                    _r=reg: e.transpose(_r, _s, IDENT), waits=w)
            ev = v(lambda e, _i=i, _r=reg, _d=dst: e.tensor_copy(
                _d[:, _i * 128:(_i + 1) * 128], _r), waits=[tt])
            trst["ev"][trst["rot"]] = ev
            trst["rot"] += 1
        return ev

    WSEQ = [(WW, 2), (WV, 2), (WK, 1), (WQ, 0)]
    PDST = [projbuf, vbuf, kT_s, qT_s]
    PTM = [W_tm, V_tm, K_tm, None]

    GRPS = [(0, 4), (4, 4), (8, 4), (12, 4), (16, 3)]
    spill_prev = None
    prev_evs = {}
    for h in range(H):
        hb = barrier_toks()
        ldw = [None] * 4
        for j in (0, 1):
            ldw[j] = gdma(wcb[j].rearrange("p (k c) -> p k c", k=16),
                          WSEQ[j][0].rearrange("(k p) c -> p k c", p=128)
                          [:, :, h * 128:(h + 1) * 128], waits=hb)
        trtok = [None] * 4
        silu_t = [None] * 4
        prev_tap3 = None
        for j in range(4):
            buf = wcb[j % 2]
            evs = {}
            mt = None
            for tb in range(5):
                ps = PSF[4 + tb % 2][:, 0:512]
                pairs = [(buf[:, k * 128:(k + 1) * 128],
                          XNT[:, k * T_IN + tb * 512:k * T_IN + tb * 512 + 512])
                         for k in range(16)]
                bank_prev = (evs.get(tb - 2)
                             or prev_evs.get(4 if tb % 2 == 0 else 3))
                mt = mm_acc(ps, pairs,
                            [ldw[j], bank_prev]
                            + (hb if (h == 0 and j == 0 and tb == 0) else []))
                evs[tb] = v(lambda e, _tb=tb, _ps=ps: e.tensor_copy(
                    preconv[:, 4 + _tb * 512:4 + (_tb + 1) * 512], _ps),
                    waits=[mt] + ([prev_tap3] if tb == 0 else []))
            if j + 2 < 4:
                ldw[j + 2] = gdma(wcb[j % 2].rearrange("p (k c) -> p k c", k=16),
                                  WSEQ[j + 2][0]
                                  .rearrange("(k p) c -> p k c", p=128)
                                  [:, :, h * 128:(h + 1) * 128], waits=[mt])
            base = WSEQ[j][1] * 64 + h * 4
            cc = [cqall[:, base + i:base + i + 1] for i in range(4)]
            dest = PDST[j]
            t = v(lambda e, _d=dest: e.tensor_scalar(
                _d[:], preconv[:, 1:1 + T_IN], cc[0], None, OP.mult),
                waits=[evs[4], zpad, S.now(ST)])
            for i in (1, 2, 3):
                t = v(lambda e, _i=i, _d=dest: e.scalar_tensor_tensor(
                    _d[:], preconv[:, 1 + _i:1 + _i + T_IN], cc[_i], _d[:],
                    OP.mult, OP.add), waits=[t])
            prev_tap3 = t
            silu_t[j] = sc(lambda e, _d=dest: e.activation(_d[:], _d[:],
                                                           AF.Silu), waits=[t])
            if j > 0 and PTM[j - 1] is not None:
                trtok[j - 1] = pe_transpose_seq(
                    PDST[j - 1], NCC, PTM[j - 1],
                    [silu_t[j - 1]] + ([S.now(SV)] if j == 1 else []))
            prev_evs = evs
        trtok[2] = pe_transpose_seq(PDST[2], NCC, PTM[2], [silu_t[2]])
        qtok = silu_t[3]
        trW, trV, trK = trtok[0], trtok[1], trtok[2]
        gtoks = {}
        gevs = []
        gbar = [S.now(SV), S.now(SS), S.now(ST)]
        for tb in range(5):
            ps = PSF[2 + tb % 2][:, 0:512]
            half = (tb % 2) * 256
            zr = zrow[:, half:half + 256]
            zr2 = zrow[:, half + 128:half + 128 + 128]
            gr = grow[:, half:half + 256]
            mt = mm_acc(ps, [(wf2b[:, h * 128:(h + 1) * 128],
                              out1T[:, tb * 512:(tb + 1) * 512])],
                        [wf2tok, gtoks.get(tb - 2)] + (gbar if tb < 2 else []))
            e1 = sc(lambda e, _ps=ps, _g=gr: e.activation(_g[:, 0:512]
                    if False else _g, _ps[:, 0:256], AF.Abs),
                    waits=[mt, gtoks.get(tb - 2)])
            e1b = sc(lambda e, _ps=ps, _z=zr: e.activation(
                _z, _ps[:, 256:512], AF.Abs), waits=[e1])
            e2 = sc(lambda e, _g=gr: e.activation(_g, _g, AF.Exp, scale=-1.0),
                    waits=[e1])
            e2b = sc(lambda e, _z=zr: e.activation(_z, _z, AF.Exp, scale=-1.0),
                     waits=[e1b])
            e3 = sc(lambda e, _g=gr: e.activation(_g, _g, AF.Ln, bias=ONE_F),
                    waits=[e2])
            e3b = sc(lambda e, _z=zr: e.activation(_z, _z, AF.Ln, bias=ONE_F),
                     waits=[e2b])
            e4 = v(lambda e, _tb=tb, _ps=ps, _g=gr: e.scalar_tensor_tensor(
                projbuf[:, _tb * 512:_tb * 512 + 256], _ps[:, 0:256], 0.0,
                _g, OP.min, OP.subtract), waits=[e3])
            e4b = v(lambda e, _tb=tb, _ps=ps, _z=zr: e.scalar_tensor_tensor(
                projbuf[:, _tb * 512 + 256:(_tb + 1) * 512], _ps[:, 256:512],
                0.0, _z, OP.min, OP.subtract), waits=[e3b])
            gtoks[tb] = e4b
            gevs += [e4, e4b]
        trG = pe_transpose_seq(projbuf, NCC, g_sm, gevs)

        dtoks = []
        cp_prev = zz_prev = se1_prev = se3_prev = se4_prev = None
        for g0, gn in GRPS:
            gw = gn * 128
            mg = None
            for j in range(gn):
                mg = tn(lambda e, _j=j, _c=g0 + j: e.matmul(
                    PSF[0][:, _j * 128:(_j + 1) * 128], UTC,
                    g_sm[:, _c * 128:(_c + 1) * 128], start=True, stop=True),
                    waits=[trG, cp_prev, se1_prev, S.now(SV)] if j == 0 else ())
            mr = None
            for j in range(gn):
                mr = tn(lambda e, _j=j, _c=g0 + j: e.matmul(
                    PSF[1][:, _j * 128:(_j + 1) * 128], HALFM,
                    g_sm[:, _c * 128:(_c + 1) * 128], start=True, stop=True),
                    waits=[zz_prev, se4_prev] if j == 0 else ())
            cp = v(lambda e, _w=gw: e.tensor_copy(grow[:, 0:_w], PSF[0][:, 0:_w]),
                   waits=[mg, S.now(SS)])
            se1 = sc(lambda e, _w=gw, _o=g0 * 128: e.activation(
                E1[:, _o:_o + _w], PSF[0][:, 0:_w], AF.Exp),
                waits=[mg, S.now(SV)])
            zz = v(lambda e, _w=gw: e.tensor_sub(zrow[:, 0:_w], PSF[1][:, 0:_w],
                                                 grow[:, 0:_w]),
                   waits=[cp, mr, se3_prev])
            zh = v(lambda e, _w=gw: e.tensor_add(grow[:, 0:_w], zrow[:, 0:_w],
                                                 PSF[1][:, 0:_w]),
                   waits=[zz, se4_prev, se1])
            zc = v(lambda e, _w=gw: e.tensor_scalar(zrow[:, 0:_w], zrow[:, 0:_w],
                                                    80.0, -80.0, OP.min, OP.max),
                   waits=[zh])
            se2 = sc(lambda e, _w=gw, _o=g0 * 128: e.activation(
                E2[:, _o:_o + _w], zrow[:, 0:_w], AF.Exp, scale=-1.0), waits=[zc])
            se3 = sc(lambda e, _w=gw, _o=g0 * 128: e.activation(
                ERb[:, _o:_o + _w], zrow[:, 0:_w], AF.Exp))
            se4 = sc(lambda e, _w=gw, _o=g0 * 128: e.activation(
                expRr[:, _o:_o + _w], grow[:, 0:_w], AF.Exp), waits=[zh])
            cp_prev, zz_prev, se1_prev, se3_prev, se4_prev = cp, zc, se1, se3, se4
            dtoks += [se1, se2, se3, se4]
        mgc = None
        for c in range(NCC):
            mgc = tn(lambda e, _c=c: e.matmul(PSF[2][:, _c:_c + 1],
                                              g_sm[:, _c * 128:(_c + 1) * 128],
                                              ONES_COL, start=True, stop=True),
                     waits=[trG, S.now(SV), S.now(SS)] if c == 0 else ())
        cgc = v(lambda e: e.tensor_copy(gcols[:, 0:NCC], PSF[2][:, 0:NCC]),
                waits=[mgc, S.now(SS)])
        sdc = sc(lambda e: e.activation(dcols[:, 0:NCC], gcols[:, 0:NCC], AF.Exp),
                 waits=[cgc, S.now(SV)])
        sq1 = v(lambda e: e.tensor_mul(wt_sm[:], W_tm[:], W_tm[:]),
                waits=[trW, S.now(ST)])
        l2b = v(lambda e: e.reduce_sum(
            sscol[:, 0:NCC], wt_sm.rearrange("p (c m) -> p c m", c=NCC), AX.X),
            waits=[sq1, S.now(SS)])
        l2c = sc(lambda e: e.activation(sscol[:, 20:20 + NCC], sscol[:, 0:NCC],
                                        AF.Ln, bias=EPS_L2), waits=[l2b])
        l2d = sc(lambda e: e.activation(sscol[:, 20:20 + NCC],
                                        sscol[:, 20:20 + NCC], AF.Exp,
                                        scale=-0.5), waits=[l2c])
        brt = v(lambda e, _h=h: e.tensor_mul(brcol[:, 0:NCC],
                                             sscol[:, 20:20 + NCC],
                                             beta3[:, 1:1 + NCC, _h]),
                waits=[l2d, beta_tok, S.now(ST)])
        brst = v(lambda e: e.tensor_scalar(brS[:, 0:NCC], brcol[:, 0:NCC], SCALE,
                                           None, OP.mult), waits=[brt])
        wtk = v(lambda e: e.tensor_mul(wt_sm[:], W_tm[:], ERb[:]),
                waits=[l2b] + dtoks)
        whk = v(lambda e: e.tensor_mul(wh_sm[:], W_tm[:], expRr[:]),
                waits=[wtk])
        br3 = brcol[:, 0:NCC].unsqueeze(2).broadcast_to([128, NCC, 128])
        khk = v(lambda e: e.tensor_tensor(
            K_tm.rearrange("p (c t) -> p c t", c=NCC),
            K_tm.rearrange("p (c t) -> p c t", c=NCC), br3, OP.mult),
            waits=[trK, brt])
        vhk = v(lambda e: e.tensor_tensor(
            Vh_tm.rearrange("p (c t) -> p c t", c=NCC),
            V_tm.rearrange("p (c t) -> p c t", c=NCC), br3, OP.mult),
            waits=[trV, brt])
        if h == 0:
            dbg_tap("kT", kT_s, [S.now(SS)])
            dbg_tap("qT", qT_s, [qtok])
            dbg_tap("W_tm", W_tm, [trW])
            dbg_tap("V_tm", V_tm, [trV])
            dbg_tap("g_sm", g_sm, [trG])
            dbg_tap("E1", E1, dtoks)
            dbg_tap("E2", E2, dtoks)
            dbg_tap("ERb", ERb, dtoks)
            dbg_tap("expRr", expRr, dtoks)
            dbg_tap("gcols", gcols, [cgc])
            dbg_tap("dcols", dcols, [sdc])
            dbg_tap("sscol", sscol, [l2d])
            dbg_tap("brcol", brcol, [brt])
            dbg_tap("wt_sm", wt_sm, [wtk])
            dbg_tap("wh_sm", wh_sm, [whk])
            dbg_tap("Khat", K_tm, [khk])
            dbg_tap("Vhat", Vh_tm, [vhk])
        trWT = dma(W_tm.rearrange("p (c t) -> p c t", c=NCC)[:, 0:NCO, :],
                   wt_sm[:, 3 * 128:NCC * 128],
                   waits=[wtk, khk, vhk, S.now(ST)], transpose=True)
        wtT = W_tm

        z1 = v(lambda e: e.memset(SkT[:], 0.0), waits=[S.now(ST)])
        z2 = v(lambda e: e.memset(Sv[:], 0.0))

        upK = upV = None
        up2 = {0: None, 1: None}
        snapKs = None
        for c in range(NCC):
            par = c % 2
            PK = PSF[par][:, 0:128]
            PV = PSF[2 + par][:, 0:128]
            oc = c - 3
            if oc >= 0:
                snapK = v(lambda e, _o=oc: e.tensor_copy(
                    Sk_all[:, _o * 128:(_o + 1) * 128], SkT[:]),
                    waits=[upK, z1])
                snapV = v(lambda e, _o=oc: e.tensor_copy(
                    Sv_all[:, _o * 128:(_o + 1) * 128], Sv[:]), waits=[upV, z2])
                snapKs = snapV
            mdk = tn(lambda e, _c=c, _p=PK: e.matmul(
                _p, wh_sm[:, _c * 128:(_c + 1) * 128],
                K_tm[:, _c * 128:(_c + 1) * 128], start=True, stop=True),
                waits=[whk, khk, up2[par], S.now(SV) if c == 0 else None])
            mdv = tn(lambda e, _c=c, _p=PV: e.matmul(
                _p, wh_sm[:, _c * 128:(_c + 1) * 128],
                Vh_tm[:, _c * 128:(_c + 1) * 128], start=True, stop=True),
                waits=[vhk, S.now(SV) if c == 0 else None])
            upK = v(lambda e, _c=c, _p=PK: e.scalar_tensor_tensor(
                SkT[:], SkT[:], dcols[:, _c:_c + 1], _p, OP.mult, OP.add),
                waits=[mdk, sdc, z1, snapKs, upK])
            upV = v(lambda e, _c=c, _p=PV: e.scalar_tensor_tensor(
                Sv[:], Sv[:], dcols[:, _c:_c + 1], _p, OP.mult, OP.add),
                waits=[mdv, z2, upV])
            up2[par] = upV

        prevp = {0: {}, 1: {}}

        def chunkB(oc, pdict):
            c = oc + 3
            gc = c + 1
            par = oc % 2
            tcol = slice(gc * 128, (gc + 1) * 128)
            ccol = slice(c * 128, (c + 1) * 128)
            Pqk = PSF[3 * par + 0][:, 0:128]
            Psc = PSF[3 * par + 1][:, 0:128]
            Ppt = PSF[3 * par + 2][:, 0:128]
            PB = PB6 if par == 0 else PB7
            trS, trA, trB = PB[:, 0:128], PB[:, 128:256], PB[:, 256:384]
            SkEd_, SkE_, QKm_ = SkEdP[par], SkEP[par], QKmP[par]
            stil_, a_e_, aTt_, aHt_, PTm_ = (stilP[par], a_eP[par], aTtP[par],
                                             aHtP[par], PTmP[par])
            nc0 = ncol[:, 4 * par + 0:4 * par + 1]
            nc1 = ncol[:, 4 * par + 1:4 * par + 2]
            nc2 = ncol[:, 4 * par + 2:4 * par + 3]
            ttr = tn(lambda e, _o=oc: e.transpose(
                trS, Sk_all[:, _o * 128:(_o + 1) * 128], IDENT),
                waits=[snapKs, upK, pdict.get("evB"), pdict.get("evSk")])
            yield
            eve = v(lambda e: e.tensor_copy(SkEd_[:], trS),
                    waits=[ttr, pdict.get("mt1")])
            pdict["evSk"] = eve
            yield
            mqk = tn(lambda e, _t=tcol: e.matmul(Pqk, kT_s[:, _t], qT_s[:, _t],
                                                 start=True, stop=True),
                     waits=[eve, qtok, pdict.get("evT1")])
            yield
            evQK = v(lambda e, _c=c: e.scalar_tensor_tensor(
                QKm_[:], Pqk, brS[:, _c:_c + 1], UTC, OP.mult, OP.mult),
                waits=[mqk, brst])
            yield
            ms1 = tn(lambda e, _c=c: e.matmul(
                Psc, QKm_[:], wt_sm[:, _c * 128:(_c + 1) * 128],
                start=True, stop=True), waits=[evQK, wtk, pdict.get("evS1")])
            yield
            mt1 = tn(lambda e, _t=tcol: e.matmul(Pqk, qT_s[:, _t], SkEd_[:],
                                                 start=True, stop=True),
                     waits=[eve, evQK])
            pdict["mt1"] = mt1
            yield
            evT1 = v(lambda e, _c=c: e.scalar_tensor_tensor(
                SkE_[:], Pqk, SCALE, E1[:, _c * 128:(_c + 1) * 128],
                OP.mult, OP.mult), waits=[mt1])
            pdict["evT1"] = evT1
            yield
            evS1 = v(lambda e, _c=c: e.tensor_mul(
                stil_[:], Psc, E2[:, _c * 128:(_c + 1) * 128]), waits=[ms1])
            pdict["evS1"] = evS1
            yield
            evSC = v(lambda e: e.tensor_add(stil_[:], stil_[:], SkE_[:]),
                     waits=[evS1, evT1])
            yield
            nm = v(lambda e: e.reduce_max(nc0, stil_[:], AX.X, negate=True),
                   waits=[evSC, pdict.get("ae")])
            yield
            ae = sc(lambda e: e.activation(a_e_[:], stil_[:], AF.Exp, bias=nc0),
                    waits=[nm])
            pdict["ae"] = ae
            yield
            sm = v(lambda e: e.reduce_sum(nc1, a_e_[:], AX.X), waits=[ae])
            yield
            ri = v(lambda e: e.reciprocal(nc2, nc1), waits=[sm])
            yield
            at = v(lambda e, _c=c: e.tensor_mul(
                aTt_[:], a_e_[:], E2[:, _c * 128:(_c + 1) * 128]),
                waits=[ae, pdict.get("mpA")])
            yield
            ah = v(lambda e, _c=c: e.tensor_mul(
                aHt_[:], a_e_[:], E1[:, _c * 128:(_c + 1) * 128]),
                waits=[pdict.get("mo2")])
            yield
            ttb = tn(lambda e: e.transpose(trA, aTt_[:], IDENT),
                     waits=[at, eve, pdict.get("evB")])
            yield
            evA = v(lambda e: e.tensor_copy(aTt_[:], trA), waits=[ttb])
            yield
            ttc = tn(lambda e: e.transpose(trB, aHt_[:], IDENT),
                     waits=[ah, evA])
            yield
            evB = v(lambda e: e.tensor_copy(aHt_[:], trB), waits=[ttc])
            pdict["evB"] = evB
            yield
            mp1 = tn(lambda e, _o=oc: e.matmul(
                Ppt[0:64, 0:64], wtT[:, _o * 128:_o * 128 + 64], aTt_[:, 0:64],
                start=True, stop=True), waits=[evA, trWT, pdict.get("evPT")])
            mp2 = tn(lambda e, _o=oc: e.matmul(
                Ppt[64:128, 64:128], wtT[:, _o * 128 + 64:_o * 128 + 128],
                aTt_[:, 64:128], start=True, stop=True))
            mp3 = tn(lambda e, _o=oc: e.matmul(
                Ppt[0:64, 64:128], wtT[:, _o * 128:_o * 128 + 64],
                aTt_[:, 64:128], start=True, stop=True))
            pdict["mpA"] = mp3
            yield
            evPT = v(lambda e, _c=c: e.scalar_tensor_tensor(
                PTm_[:], Ppt, brcol[:, _c:_c + 1], UTC, OP.mult, OP.mult),
                waits=[mp3])
            pdict["evPT"] = evPT
            yield
            mo1 = tn(lambda e, _c=c: e.matmul(
                Ppt, PTm_[:], V_tm[:, _c * 128:(_c + 1) * 128],
                start=True, stop=False), waits=[evPT])
            mo2 = tn(lambda e, _o=oc: e.matmul(
                Ppt, aHt_[:], Sv_all[:, _o * 128:(_o + 1) * 128],
                start=False, stop=True), waits=[evB, snapKs])
            pdict["mo2"] = mo2
            yield
            evO = v(lambda e, _o=oc: e.tensor_scalar(
                obuf[:, _o * 128:(_o + 1) * 128], Ppt, nc2, None, OP.mult),
                waits=[mo2, ri, spill_prev])
            pdict["evO"] = evO
            yield

        for co in range(0, NCO, 2):
            gens = [chunkB(co, prevp[0]), chunkB(co + 1, prevp[1])]
            live = [True, True]
            while any(live):
                for gi in (0, 1):
                    if live[gi]:
                        try:
                            next(gens[gi])
                        except StopIteration:
                            live[gi] = False
        evO = prevp[1].get("evO")
        evO2 = prevp[0].get("evO")
        spill_prev = dma(oSp[h * 128:(h + 1) * 128, :], obuf[:],
                         waits=[evO, evO2])
        if h == 0:
            dbg_tap("SkT_f", SkT, [upK])
            dbg_tap("Sv_f", Sv, [upV])
    ATTN_LD = [S.now(n) for n in LDS]

    srcs = [(W1T, 16, FFH), (W2T, 16, FFH), (W3T, 44, HID)]
    slabs12 = [(mi, i) for mi in (0, 1) for i in range(srcs[mi][1])]
    gb = barrier_toks()
    zg = v(lambda e: e.memset(gsum[:, 0:4], 0.0), waits=gb)
    fslabD = [carve(AR2, 0, FFH, F32), carve(AR2, 45056, FFH, F32)]
    fslaD = [carve(AR2, 22528, FFH, BF16), carve(AR2, 67584, FFH, BF16)]
    fslbD = [carve(AR2, 33792, FFH, BF16), carve(AR2, 78848, FFH, BF16)]
    cons = {}
    gprev = zg
    for si, (mi, i) in enumerate(slabs12):
        WSRC, nslab, wid = srcs[mi]
        bi = si % 2
        tl = dma(fslabD[bi][:, 0:wid], WSRC[i * 128:(i + 1) * 128, :],
                 waits=[cons.get(si - 2)] + (gb if si == 0 else []))
        ga = sc(lambda e, _w=wid, _b=bi: e.activation(
            fslaD[_b][:, 0:_w], fslabD[_b][:, 0:_w], AF.Abs,
            accum_out=gsum[:, 4:5]), waits=[tl, gprev])
        gprev = v(lambda e, _m=mi: e.tensor_add(
            gsum[:, _m:_m + 1], gsum[:, _m:_m + 1], gsum[:, 4:5]), waits=[ga])
        cons[si] = ga
    gprev12 = gprev

    def thr_seq(mi, gdone):
        mt = tn(lambda e, _m=mi: e.matmul(
            PSF[2][0:1, _m:_m + 1], gsum[:, _m:_m + 1], ONE_F,
            start=True, stop=True), waits=[gdone, S.now(SV), S.now(SS)])
        cv_ = v(lambda e, _m=mi: e.tensor_copy(gtmp[0:1, _m:_m + 1],
                                               PSF[2][0:1, _m:_m + 1]),
                waits=[mt])
        mr = tn(lambda e, _m=mi: e.matmul(
            PSF[3][:, _m:_m + 1], crow_s[:], gtmp[0:1, _m:_m + 1],
            start=True, stop=True), waits=[cv_])
        tp = v(lambda e, _m=mi: e.tensor_scalar(
            gcolq[:, 2 * _m:2 * _m + 1], PSF[3][:, _m:_m + 1], 0.5 / NELEM_W,
            0.5e-8, OP.mult, OP.add), waits=[mr, cv_])
        return v(lambda e, _m=mi: e.tensor_scalar(
            gcolq[:, 2 * _m + 1:2 * _m + 2], gcolq[:, 2 * _m:2 * _m + 1],
            -1.0, None, OP.mult), waits=[tp])

    thr = {}
    for mi in range(2):
        thr[mi] = thr_seq(mi, gprev)
        gprev = thr[mi]
    cons = {}
    stok = {}
    for si, (mi, i) in enumerate(slabs12):
        WSRC, nslab, wid = srcs[mi]
        bi = si % 2
        tl = dma(fslabD[bi][:, 0:wid], WSRC[i * 128:(i + 1) * 128, :],
                 waits=[cons.get(si - 2), thr[mi] if i == 0 else None])
        s1 = sc(lambda e, _w=wid, _b=bi, _m=mi: e.activation(
            fslaD[_b][:, 0:_w], fslabD[_b][:, 0:_w], AF.Sign,
            bias=gcolq[:, 2 * _m + 1:2 * _m + 2]),
            waits=[tl, thr[mi], stok.get(si - 2)])
        s2 = sc(lambda e, _w=wid, _b=bi, _m=mi: e.activation(
            fslbD[_b][:, 0:_w], fslabD[_b][:, 0:_w], AF.Sign,
            bias=gcolq[:, 2 * _m:2 * _m + 1]))
        qa = v(lambda e, _w=wid, _b=bi: e.tensor_add(
            fslbD[_b][:, 0:_w], fslbD[_b][:, 0:_w], fslaD[_b][:, 0:_w]),
            waits=[s1, s2, stok.get(si - 2)])
        cons[si] = qa
        stok[si] = dma(q2w[mi][i * 128:(i + 1) * 128, :], fslbD[bi][:, 0:wid],
                       waits=[qa])
    QDONE = [S.now(n) for n in LDS]

    gstg = [carve(AR2, 24576, HID, F32), carve(AR2, 32768, HID, F32)]
    gouA = [carve(AR2, 40960, HID, BF16), carve(AR2, 45056, HID, BF16)]
    gouB = [carve(AR2, 49152, HID, BF16), carve(AR2, 53248, HID, BF16)]

    def w3pipe():
        cns = {}
        gpv = gprev12
        for si in range(44):
            bi = si % 2
            tl = dma(gstg[bi], W3T[si * 128:(si + 1) * 128, :],
                     waits=[cns.get(si - 2)] + (QDONE if si < 2 else []))
            ga = sc(lambda e, _b=bi: e.activation(
                gouA[_b], gstg[_b], AF.Abs, accum_out=gsum[:, 4:5]),
                waits=[tl, gpv])
            gpv = v(lambda e: e.tensor_add(
                gsum[:, 2:3], gsum[:, 2:3], gsum[:, 4:5]), waits=[ga])
            cns[si] = ga
            yield
        thr2 = thr_seq(2, gpv)
        yield
        cns = {}
        stk = {}
        for si in range(44):
            bi = si % 2
            tl = dma(gstg[bi], W3T[si * 128:(si + 1) * 128, :],
                     waits=[cns.get(si - 2), thr2 if si < 2 else None])
            s1 = sc(lambda e, _b=bi: e.activation(
                gouA[_b], gstg[_b], AF.Sign, bias=gcolq[:, 5:6]),
                waits=[tl, thr2, stk.get(si - 2)])
            s2 = sc(lambda e, _b=bi: e.activation(
                gouB[_b], gstg[_b], AF.Sign, bias=gcolq[:, 4:5]))
            qa = v(lambda e, _b=bi: e.tensor_add(
                gouB[_b], gouB[_b], gouA[_b]), waits=[s1, s2, stk.get(si - 2)])
            cns[si] = qa
            stk[si] = dma(q2w[2][si * 128:(si + 1) * 128, :], gouB[bi],
                          waits=[qa])
            yield
        wc = None
        wst = {}
        for i in range(16):
            bi = i % 2
            tl = dma(gstg[bi], WOp[i * 128:(i + 1) * 128, :],
                     waits=[cns.get(42 + bi), wc])
            wc = v(lambda e, _b=bi: e.tensor_scalar(
                gouA[_b], gstg[_b], nw_s[:, 0:1], None, OP.mult),
                waits=[tl, wst.get(i - 2), stk.get(43 - bi)])
            wst[i] = dma(woB[i * 128:(i + 1) * 128, :], gouA[bi], waits=[wc])
            yield

    q2w1r = q2w[0].rearrange("(k p) c -> p k c", p=128)
    q2w2r = q2w[1].rearrange("(k p) c -> p k c", p=128)
    fb = barrier_toks()
    hprev = None
    hst = None
    mmD = {}
    w3g = w3pipe()
    w3n = 0
    for cs in range(44):
        bi = cs % 2
        tla = dma(q2aD[bi].rearrange("p (k c) -> p k c", k=16),
                  q2w1r[:, :, cs * 128:(cs + 1) * 128],
                  waits=QDONE + [mmD.get(cs - 2)] + (fb if cs < 2 else []))
        tlb = dma(q2bD[bi].rearrange("p (k c) -> p k c", k=16),
                  q2w2r[:, :, cs * 128:(cs + 1) * 128])
        for tb in range(4):
            off = 512 + tb * 512
            p0 = PSF[0][:, 0:512]
            p1 = PSF[1][:, 0:512]
            pairs0 = [(q2aD[bi][:, k * 128:(k + 1) * 128],
                       XNT[:, k * T_IN + off:k * T_IN + off + 512])
                      for k in range(16)]
            pairs1 = [(q2bD[bi][:, k * 128:(k + 1) * 128],
                       XNT[:, k * T_IN + off:k * T_IN + off + 512])
                      for k in range(16)]
            m0 = mm_acc(p0, pairs0, [tla, hprev] + (fb if cs == 0 else []))
            m1 = mm_acc(p1, pairs1, [tlb])
            sg = sc(lambda e, _tb=tb, _p=p0: e.activation(
                sigx[:, _tb * 512:(_tb + 1) * 512], _p, AF.Sigmoid, scale=0.5),
                waits=[m0, hprev])
            hprev = v(lambda e, _tb=tb, _p=p1: e.scalar_tensor_tensor(
                hslab[:, _tb * 512:(_tb + 1) * 512], _p, 0.25,
                sigx[:, _tb * 512:(_tb + 1) * 512], OP.mult, OP.mult),
                waits=[m1, sg, hst])
        mmD[cs] = m1
        hst = dma(hidB[cs * 128:(cs + 1) * 128, :], hslab[:], waits=[hprev])
        tgt = (105 * (cs + 1)) // 44
        while w3n < tgt and next(w3g, "END") != "END":
            w3n += 1
    for _ in w3g:
        pass
    F1_LD = [S.now(n) for n in LDS]

    oSpr = oSp.rearrange("(h p) (c d) -> p c h d", p=128, d=128)
    nb = barrier_toks() + F1_LD + ATTN_LD
    ntok = None
    gf_prev = None
    on2 = None
    for oc in range(NCO):
        tl = dma(onrow.rearrange("p (g d) -> p g d", g=H), oSpr[:, oc],
                 waits=[ntok] + (nb if oc == 0 else []))
        gts = []
        for cb in range(4):
            ps = PSF[cb][:, 0:512]
            mg = tn(lambda e, _o=oc, _cb=cb, _p=ps: e.matmul(
                _p, out2T[:, _o * 128:(_o + 1) * 128],
                wg2b[:, _cb * 512:(_cb + 1) * 512], start=True, stop=True),
                waits=[gf_prev, o2tok] + (nb if oc == 0 else []))
            gf = v(lambda e, _cb=cb, _p=ps: e.tensor_add(
                gate_f[:, _cb * 512:(_cb + 1) * 512], _p,
                bg2_s[:, _cb * 512:(_cb + 1) * 512]),
                waits=[mg] + ([on2] if on2 else []))
            gts.append(sc(lambda e, _cb=cb: e.activation(
                gatesig[:, _cb * 512:(_cb + 1) * 512],
                gate_f[:, _cb * 512:(_cb + 1) * 512], AF.Sigmoid), waits=[gf]))
        gf_prev = gts[-1]
        sq = v(lambda e: e.tensor_mul(sqn[:], onrow[:], onrow[:]), waits=[tl])
        rs2 = v(lambda e: e.reduce_sum(
            rstd16[:, 0:16], sqn.rearrange("p (g d) -> p g d", g=H), AX.X),
            waits=[sq, S.now(SS)])
        rs3 = v(lambda e: e.tensor_scalar(rstd16[:, 0:16], rstd16[:, 0:16],
                                          1.0 / 128.0, 1e-5, OP.mult, OP.add),
                waits=[rs2])
        rs4 = sc(lambda e: e.activation(rstd16[:, 0:16], rstd16[:, 0:16],
                                        AF.Ln), waits=[rs3])
        rs5 = sc(lambda e: e.activation(rstd16[:, 0:16], rstd16[:, 0:16],
                                        AF.Exp, scale=-0.5), waits=[rs4])
        on1 = v(lambda e: e.tensor_mul(onrow[:], onrow[:], gatesig[:]),
                waits=gts + [rs2, sq])
        rb3 = rstd16[:, 0:16].unsqueeze(2).broadcast_to([128, 16, 128])
        on2 = v(lambda e: e.tensor_tensor(
            onrow.rearrange("p (g d) -> p g d", g=H),
            onrow.rearrange("p (g d) -> p g d", g=H), rb3, OP.mult),
            waits=[rs5, on1])
        ntok = dma(onT.rearrange("p (j t) -> p j t", j=16)
                   [:, :, oc * 128:(oc + 1) * 128],
                   onrow[:], waits=[on2], transpose=True)
        if oc == 0:
            dbg_tap("gatesig", gatesig, [gts[-1]])
            dbg_tap("rstd16", rstd16, [rs5])
            dbg_tap("onrow0", onrow, [on2])
    NORM_LD = [S.now(n) for n in LDS]
    dbg_tap("onT", onT, NORM_LD)

    q2w3r = q2w[2].rearrange("(k p) c -> p k c", p=128)
    woBr = woB.rearrange("(k p) c -> p k c", p=128)
    w3cbD = [carve(AR2, 45056, 44 * 256, BF16), carve(AR2, 67584, 44 * 256, BF16)]
    wocbD = [carve(AR1, 65536, 16 * 256, BF16), carve(AR1, 73728, 16 * 256, BF16)]
    xstg = [carve(AR1, 81920 + 1024 * i, 256, F32) for i in range(4)]
    outr2 = [carve(AR2, 90112 + 1024 * i, 256, F32) for i in range(4)]
    f2b = barrier_toks() + NORM_LD
    NCB = 8
    fprev = {}
    yst = {}
    ld3 = {}
    ldo = {}
    ldx = {}
    ldh = None
    for tq in range(4):
        ldh = dma(hidq.rearrange("p (k c) -> p k c", k=44),
                  hidB.rearrange("(k p) c -> p k c", p=128)
                  [:, :, tq * 512:(tq + 1) * 512],
                  waits=[fprev.get((tq - 1, NCB - 1, 3))]
                  + (f2b if tq == 0 else []))
        for cbk in range(NCB):
            bi = cbk % 2
            ld3[cbk] = dma(w3cbD[bi].rearrange("p (k c) -> p k c", k=44),
                           q2w3r[:, :, cbk * 256:(cbk + 1) * 256],
                           waits=[fprev.get((tq, cbk - 2, 3)),
                                  fprev.get((tq - 1, NCB - 1, 3))]
                           + (f2b if tq == 0 and cbk < 2 else []))
            ldo[cbk] = dma(wocbD[bi].rearrange("p (k c) -> p k c", k=16),
                           woBr[:, :, cbk * 256:(cbk + 1) * 256],
                           waits=[])
            for tc in range(4):
                co = tq * 4 + tc
                ps = PSF[tc % 2 + 2 * bi][:, 0:256]
                ldx[(cbk, tc)] = dma(
                    xstg[tc][:], x_in[HALO + co * 128:HALO + (co + 1) * 128,
                                      cbk * 256:(cbk + 1) * 256],
                    waits=[fprev.get((tq, cbk - 1, tc)),
                           fprev.get((tq - 1, NCB - 1, tc) if cbk == 0
                                     else (tq, cbk - 1, tc))])
                pairs = [(hidq[:, k * 512 + tc * 128:k * 512 + tc * 128 + 128],
                          w3cbD[bi][:, k * 256:(k + 1) * 256])
                         for k in range(44)]
                pairs += [(onT[:, j * T_OWN + co * 128:j * T_OWN + co * 128
                               + 128], wocbD[bi][:, j * 256:(j + 1) * 256])
                          for j in range(16)]
                mt = mm_acc(ps, pairs,
                            [ldh, ld3[cbk], ldo[cbk],
                             fprev.get((tq, cbk - 2, tc)),
                             fprev.get((tq, cbk, tc - 2))])
                fprev[(tq, cbk, tc)] = v(
                    lambda e, _p=ps, _t=tc: e.tensor_add(
                        outr2[_t][:], _p, xstg[_t][:]),
                    waits=[mt, ldx[(cbk, tc)],
                           yst.get(((cbk - 1) % NCB, tc))])
                yst[(cbk, tc)] = dma(
                    y[co * 128:(co + 1) * 128, cbk * 256:(cbk + 1) * 256],
                    outr2[tc][:, 0:256], waits=[fprev[(tq, cbk, tc)]])
    S.op("sync", lambda e: None, waits=[S.now(n) for n in LDS])
    S.op("gpsimd", lambda e: None, waits=[S.now(n) for n in GDS])

    from contextlib import ExitStack
    es = ExitStack()
    sems = {n: es.enter_context(nc.semaphore(n))
            for n in (["SV", "SS", "ST"] + ["LD%d" % i for i in range(8)]
                      + ["GD%d" % i for i in range(4)])}
    with nc.Block() as blk:
        @blk.sync
        def _(eng):
            S.emit("sync", eng, sems)

        @blk.vector
        def _(eng):
            S.emit("vector", eng, sems)

        @blk.scalar
        def _(eng):
            S.emit("scalar", eng, sems)

        @blk.tensor
        def _(eng):
            S.emit("tensor", eng, sems)

        @blk.gpsimd
        def _(eng):
            S.emit("gpsimd", eng, sems)
    es.close()
    return nc


_NC_CACHE = None


def kernel(L0_attn_hidden_states, L0_attn_Wq, L0_attn_Wk, L0_attn_Wv, L0_attn_Ww,
           L0_attn_cq, L0_attn_ck, L0_attn_cv, L0_attn_Wf1, L0_attn_Wf2,
           L0_attn_Wb, L0_attn_Wg1, L0_attn_Wg2, L0_attn_bg2, L0_attn_norm_w,
           L0_attn_Wo, L0_ffn_w1, L0_ffn_w2, L0_ffn_w3):
    global _NC_CACHE
    x = np.asarray(L0_attn_hidden_states, dtype=np.float32)
    B, T, _ = x.shape
    assert (B, T) == (2, 8192)

    shared = {
        "Wq": np.ascontiguousarray(np.asarray(L0_attn_Wq, np.float32)),
        "Wk": np.ascontiguousarray(np.asarray(L0_attn_Wk, np.float32)),
        "Wv": np.ascontiguousarray(np.asarray(L0_attn_Wv, np.float32)),
        "Ww": np.ascontiguousarray(np.asarray(L0_attn_Ww, np.float32)),
        "Wo": np.ascontiguousarray(np.asarray(L0_attn_Wo, np.float32)),
        "Wf1": np.ascontiguousarray(np.asarray(L0_attn_Wf1, np.float32)),
        "Wf2": np.ascontiguousarray(np.asarray(L0_attn_Wf2, np.float32)),
        "Wb": np.ascontiguousarray(np.asarray(L0_attn_Wb, np.float32)),
        "Wg1": np.ascontiguousarray(np.asarray(L0_attn_Wg1, np.float32)),
        "Wg2": np.ascontiguousarray(np.asarray(L0_attn_Wg2, np.float32)),
        "cq": np.ascontiguousarray(np.asarray(L0_attn_cq, np.float32)),
        "ck": np.ascontiguousarray(np.asarray(L0_attn_ck, np.float32)),
        "cv": np.ascontiguousarray(np.asarray(L0_attn_cv, np.float32)),
        "w1T": np.ascontiguousarray(np.asarray(L0_ffn_w1, np.float32).T),
        "w2T": np.ascontiguousarray(np.asarray(L0_ffn_w2, np.float32).T),
        "w3T": np.ascontiguousarray(np.asarray(L0_ffn_w3, np.float32).T),
        "bg2r": np.ascontiguousarray(
            np.broadcast_to(np.asarray(L0_attn_bg2, np.float32), (128, HID))),
        "nw_col": np.ascontiguousarray(
            np.asarray(L0_attn_norm_w, np.float32).reshape(128, 1)),
        "cb": np.ascontiguousarray(np.concatenate(
            [np.eye(128, dtype=BFD),
             np.triu(np.ones((128, 128), dtype=BFD)),
             np.full((128, 128), 0.5, dtype=BFD)], axis=1)),
        "cf": np.ascontiguousarray(np.tile(
            np.array([1e-5, 1e-6, 1.0, 0.0, 0.0, 0.0, 0.0, 0.0],
                     np.float32), (128, 1))),
        "crow": np.ones((1, 128), np.float32),
    }

    in_maps = []
    for c in range(8):
        b, qi = c // 4, c % 4
        lo = qi * T_OWN
        if qi == 0:
            xs = np.concatenate([np.zeros((HALO, HID), np.float32),
                                 x[b, 0:T_OWN]], axis=0)
        else:
            xs = x[b, lo - HALO: lo + T_OWN]
        m = dict(shared)
        m["x_in"] = np.ascontiguousarray(xs)
        in_maps.append(m)

    if _NC_CACHE is None:
        _NC_CACHE = build_program(debug=bool(globals().get("_DBG_MODE")))
    res = run_bass_kernel_spmd(_NC_CACHE, in_maps, core_ids=list(range(8)))
    if globals().get("_DBG_MODE"):
        globals()["_DBG_RES"] = res

    out = np.empty((2, 8192, HID), np.float32)
    for c in range(8):
        b, qi = c // 4, c % 4
        out[b, qi * T_OWN:(qi + 1) * T_OWN] = res.results[c]["y"]
    return out


if __name__ == "__main__":
    nc = build_program()
    print("build ok, instructions:", nc.n_instructions())
